# version 01
# baseline (speedup 1.0000x reference)
import numpy as np
import ml_dtypes
from collections import defaultdict

import concourse.bass as bass
import concourse.mybir as mybir
from concourse.bass_utils import run_bass_kernel_spmd

F32 = mybir.dt.float32
BF16 = mybir.dt.bfloat16
I32 = mybir.dt.int32
U32 = mybir.dt.uint32
AF = mybir.ActivationFunctionType
ALU = mybir.AluOpType

ENGINES = ("sync", "vector", "scalar", "tensor", "gpsimd")
PSUM_KEYS = frozenset(["pA", "pB", "pC", "pD", "pO", "pR", "pBB", "pBk"])

T = 2048
D = 1024
NCHUNK = 8
TG = 4
NEG = -100000.0


class KB:
    def __init__(self, nc):
        self.nc = nc
        self.ops = {e: [] for e in ENGINES}
        self.count = defaultdict(int)
        self.is_dma_sem = {}
        self.known = {e: defaultdict(int) for e in ENGINES}
        self.key_w = {}
        self.key_r = defaultdict(list)
        self.final_waits = []

    def emit(self, eng, fn, reads=(), writes=(), dma=None, extra_waits=(), final=False):
        if dma is not None:
            assert eng in ("sync", "scalar", "gpsimd"), eng
            pfx = "sw" if eng == "gpsimd" else "hw"
            sem_name = f"D_{pfx}_{dma}"
            self.is_dma_sem[sem_name] = True
            inc = 16
        else:
            assert eng != "sync", "sync engine is DMA-only"
            sem_name = "E_" + eng
            self.is_dma_sem[sem_name] = False
            inc = 1
        deps = list(extra_waits)
        pr = [k for k in reads if k in PSUM_KEYS]
        if pr:
            reads = [k for k in reads if k not in PSUM_KEYS]
            writes = list(writes) + pr
        for k in reads:
            t = self.key_w.get(k)
            if t is not None:
                deps.append(t)
        for k in writes:
            t = self.key_w.get(k)
            if t is not None:
                deps.append(t)
            deps.extend(self.key_r.get(k, ()))
        waits = {}
        kn = self.known[eng]
        for (s, v) in deps:
            if self.is_dma_sem.get(s):
                v = self.count[s]
            elif s == sem_name and eng == "tensor":
                continue
            if v <= kn[s]:
                continue
            if waits.get(s, 0) < v:
                waits[s] = v
        for s, v in waits.items():
            kn[s] = v
        self.count[sem_name] += inc
        tok = (sem_name, self.count[sem_name])
        self.ops[eng].append((sorted(waits.items()), fn, sem_name, inc))
        for k in writes:
            self.key_w[k] = tok
            self.key_r[k] = []
        for k in reads:
            self.key_r[k].append(tok)
        if final:
            self.final_waits.append(tok)
        return tok

    def build(self):
        nc = self.nc
        import contextlib
        with contextlib.ExitStack() as ctx:
            sems = {s: ctx.enter_context(nc.semaphore(s)) for s in self.count}
            block = ctx.enter_context(nc.Block())

            def mk_section(eng_name):
                def section(engine):
                    for waits, fn, sem_name, inc in self.ops[eng_name]:
                        for s, v in waits:
                            engine.wait_ge(sems[s], v)
                        inst = fn(engine)
                        inst.then_inc(sems[sem_name], inc)
                    if eng_name == "scalar":
                        fw = {}
                        for s, v in self.final_waits:
                            v = self.count[s] if self.is_dma_sem.get(s) else v
                            fw[s] = max(fw.get(s, 0), v)
                        for s, v in sorted(fw.items()):
                            engine.wait_ge(sems[s], v)
                return section

            for e in ENGINES:
                if self.ops[e] or e == "scalar":
                    getattr(block, e)(mk_section(e))
        return self


def host_constants():
    c = {}
    pos = np.arange(T, dtype=np.float64)
    ang = pos[:, None] / (10000.0 ** (np.arange(0, 64, 2, dtype=np.float64) / 64.0))
    rope = np.stack([np.cos(ang), np.sin(ang)], axis=-1).reshape(T, 64).astype(np.float32)
    r1 = np.tile(rope.T, (2, 1))
    r2 = np.tile((1.0 - rope).T, (2, 1)).astype(np.float32)
    c["rope12"] = np.concatenate([r1, r2], axis=1).astype(np.float32)

    P = np.zeros((64, 64), np.float32)
    for i in range(32):
        P[2 * i, 2 * i + 1] = -1.0
        P[2 * i + 1, 2 * i] = 1.0
    PT = P.T
    c["protL0"] = np.block([[PT, np.zeros_like(PT)], [np.zeros_like(PT), PT]]).astype(np.float32)

    c["ones_f"] = np.ones((128, 128), np.float32)
    c["ones_b"] = np.ones((128, 128), np.float32).astype(ml_dtypes.bfloat16)
    c["eye_f"] = np.eye(128, dtype=np.float32)
    c["eye_b"] = np.eye(128, dtype=np.float32).astype(ml_dtypes.bfloat16)

    i_ = np.arange(128)[:, None]
    j_ = np.arange(512)[None, :]
    m = [(i_ <= j_ - d).astype(np.float32) for d in (0, 128, 256, 384)]
    c["cmask"] = np.concatenate(m, axis=1).astype(ml_dtypes.bfloat16)
    scl = np.zeros((1, 16), np.float32)
    scl[0, 0] = 1e-5
    scl[0, 1] = 1e-24
    c["scl"] = scl
    c["iota32"] = np.tile(np.arange(32, dtype=np.float32)[None, :], (128, 1))
    c["tokcol"] = np.arange(128, dtype=np.float32).reshape(128, 1)
    ii = np.arange(128)
    c["lstrict"] = (ii[:, None] < ii[None, :]).astype(np.float32)

    DH = 64
    half = np.arange(0, DH, 2, dtype=np.float64)
    inv_freq = 1.0 / 10000.0 ** (half / DH)
    tt = np.arange(T, dtype=np.float64)
    fr = tt[:, None] * inv_freq[None, :]
    freqs = np.concatenate([fr, fr], axis=1)
    base = (half + 0.4 * DH) / (1.4 * DH)
    power = (tt - T // 2) / 256.0
    scale = base[None, :] ** power[:, None]
    scale = np.concatenate([scale, scale], axis=1)
    cosf = np.cos(freqs); sinf = np.sin(freqs)
    g1q = (cosf * scale).T.astype(np.float32)
    g2q = (sinf * scale).T.astype(np.float32)
    g1k = (cosf / scale).T.astype(np.float32)
    g2k = (sinf / scale).T.astype(np.float32)
    c["xposq"] = np.concatenate([np.tile(g1q, (2, 1)), np.tile(g2q, (2, 1))], axis=1)
    c["xposk"] = np.concatenate([np.tile(g1k, (2, 1)), np.tile(g2k, (2, 1))], axis=1)

    c["blockmap"] = np.kron(np.eye(2, dtype=np.float32),
                            np.ones((64, 64), np.float32))

    Ph = np.zeros((64, 64), np.float32)
    for i in range(32):
        Ph[i, 32 + i] = -1.0
        Ph[32 + i, i] = 1.0
    PhT = Ph.T
    c["prothalf"] = np.block([[PhT, np.zeros_like(PhT)], [np.zeros_like(PhT), PhT]]).astype(np.float32)

    im = np.arange(128)[:, None]
    jl = np.arange(512)[None, :]
    madd = []
    mmul = []
    for a in range(8):
        diff = jl - (im + 128 * a - 512)
        keep = (diff >= 0) & (diff <= 512)
        madd.append(np.where(keep, 0.0, -100000.0).astype(np.float32))
        mmul.append(keep.astype(np.float32))
    c["fmaskadd"] = np.concatenate(madd, axis=1).astype(np.float32)
    c["fmaskmul"] = np.concatenate(mmul, axis=1).astype(ml_dtypes.bfloat16)
    return c


def make_const_inputs(nc, consts):
    aps = {}
    for k, v in consts.items():
        dt = {np.dtype(np.float32): F32, np.dtype(ml_dtypes.bfloat16): BF16,
              np.dtype(np.int32): I32}[v.dtype]
        aps[k] = nc.dram_tensor("c_" + k, list(v.shape), dt, kind="ExternalInput").ap()
    return aps


def prep_weights(inp):
    w = {}
    Wqkv = np.asarray(inp["L0_attn_Wqkv"])
    idx = np.zeros(3072, np.int64)
    for h in range(16):
        for p3 in range(3):
            for d in range(64):
                idx[p3 * 1024 + h * 64 + d] = h * 192 + p3 * 64 + d
    w["l0_wqkvT"] = np.ascontiguousarray(Wqkv[idx].T)
    w["l0_bqkv"] = np.ascontiguousarray(
        np.asarray(inp["L0_attn_bqkv"])[idx].reshape(3, 8, 128).transpose(2, 1, 0).reshape(128, 24))
    w["l0_wprojT"] = np.ascontiguousarray(np.asarray(inp["L0_attn_Wproj"]).T)
    w["l0_bproj"] = np.ascontiguousarray(
        np.asarray(inp["L0_attn_bproj"]).reshape(8, 128).T)
    Eq = np.asarray(inp["L1_attn_Eq"]); Ek = np.asarray(inp["L1_attn_Ek"])
    Ev = np.asarray(inp["L1_attn_Ev"])
    w["l1_eqT"] = np.ascontiguousarray(Eq.T)
    w["l1_ekT"] = np.ascontiguousarray(Ek.T)
    w["l1_ekTrev"] = np.ascontiguousarray(Ek.T[:, ::-1])
    w["l1_ev0"] = np.ascontiguousarray(Ev[0:1])
    w["l1_ev128"] = np.ascontiguousarray(Ev[128:129])
    jj = np.arange(255)
    idx = np.clip(127 - jj, -64, 64) + 64
    evb2 = Ev[idx]
    w["l1_evb2a"] = np.ascontiguousarray(evb2[0:128])
    w["l1_evb2b"] = np.ascontiguousarray(evb2[128:255])
    wqkv = np.asarray(inp["L1_ffn_wqkv"])
    wo = np.asarray(inp["L1_ffn_wo"])
    w1 = np.asarray(inp["L1_ffn_w1"])
    b1 = np.asarray(inp["L1_ffn_b1"])
    w2 = np.asarray(inp["L1_ffn_w2"])
    b2 = np.asarray(inp["L1_ffn_b2"])
    lng = np.asarray(inp["L1_ffn_ln_g"])
    lnb = np.asarray(inp["L1_ffn_ln_b"])
    qs = np.asarray(inp["L1_ffn_q_scale"])
    ks = np.asarray(inp["L1_ffn_k_scale"])
    HID = 2730
    HPAD = 2816
    for l in range(2):
        w[f"f_wq{l}"] = np.ascontiguousarray(wqkv[l][:, 0:512])
        w[f"f_wk{l}"] = np.ascontiguousarray(wqkv[l][:, 512:1024])
        w[f"f_wv{l}"] = np.ascontiguousarray(wqkv[l][:, 1024:1536])
        w[f"f_wo{l}"] = np.ascontiguousarray(wo[l])
        w1a = np.zeros((1024, HPAD), np.float32); w1a[:, :HID] = w1[l][:, :HID]
        w1g = np.zeros((1024, HPAD), np.float32); w1g[:, :HID] = w1[l][:, HID:]
        w[f"f_w1a{l}"] = w1a
        w[f"f_w1g{l}"] = w1g
        w2p = np.zeros((HPAD, 1024), np.float32); w2p[:HID] = w2[l]
        w[f"f_w2{l}"] = w2p
        b1a = np.zeros(HPAD, np.float32); b1a[:HID] = b1[l][:HID]
        b1g = np.zeros(HPAD, np.float32); b1g[:HID] = b1[l][HID:]
        w[f"f_b1a{l}"] = np.ascontiguousarray(b1a.reshape(22, 128).T)
        w[f"f_b1g{l}"] = np.ascontiguousarray(b1g.reshape(22, 128).T)
        w[f"f_b2{l}"] = np.ascontiguousarray(b2[l].reshape(8, 128).T)
        w[f"f_lng{l}"] = np.ascontiguousarray(lng[l].reshape(8, 128).T)
        w[f"f_lnb{l}"] = np.ascontiguousarray(lnb[l].reshape(8, 128).T)
        w[f"f_qs{l}"] = np.ascontiguousarray(np.tile(qs[l], 2).reshape(128, 1))
        w[f"f_ks{l}"] = np.ascontiguousarray(np.tile(ks[l], 2).reshape(128, 1))
        w[f"f_qsr{l}"] = np.ascontiguousarray(qs[l].reshape(1, 64))
        w[f"f_ksr{l}"] = np.ascontiguousarray(ks[l].reshape(1, 64))
    return w


def csl(arena, c, lo, hi):
    return arena[:, c * T + lo: c * T + hi]


def emit_ln(kb, C, src_key, dst_key, X, XNB):
    E = kb.emit
    ones = C["ones_f32"]
    SQ = C["UTIL"]
    PA, PBk, PC, PD = C["PA"], C["PBk"], C["PC"], C["PD"]
    stat = C["STAT"]
    for g in range(TG):
        for c in range(NCHUNK):
            E("tensor", lambda t, c=c, g=g: t.matmul(
                PA[0:1, :], ones[:, 0:1], csl(X, c, g * 512, g * 512 + 512),
                start=(c == 0), stop=(c == NCHUNK - 1)),
              reads=[src_key, "consts"], writes=["pA"])
        for c in range(NCHUNK):
            sq = SQ[:, (c % 2) * 512:(c % 2) * 512 + 512]
            kq = f"ut{c % 2}"
            E("scalar", lambda s, c=c, g=g, sq=sq: s.activation(
                out=sq, in_=csl(X, c, g * 512, g * 512 + 512), func=AF.Square),
              reads=[src_key], writes=[kq])
            E("tensor", lambda t, sq=sq, c=c: t.matmul(
                PBk[0:1, :], ones[:, 0:1], sq,
                start=(c == 0), stop=(c == NCHUNK - 1)),
              reads=[kq, "consts"], writes=["pB"])
        mu = stat[0:1, 0:512]
        ms = stat[0:1, 512:1024]
        va = stat[0:1, 1024:1536]
        rs = stat[0:1, 1536:2048]
        E("scalar", lambda s: s.mul(mu, PA[0:1, :], 1.0 / D), reads=["pA"], writes=["st_mu"])
        E("scalar", lambda s: s.mul(ms, PBk[0:1, :], 1.0 / D), reads=["pB"], writes=["st_ms"])
        E("scalar", lambda s: s.activation(out=va, in_=mu, func=AF.Square),
          reads=["st_mu"], writes=["st_va"])
        E("vector", lambda v: v.tensor_sub(va, ms, va), reads=["st_ms", "st_va"], writes=["st_va"])
        E("scalar", lambda s: s.activation(out=rs, in_=va, func=AF.Ln,
                                           bias=C["scl_s"][0:1, 0:1]),
          reads=["st_va", "consts"], writes=["st_rs"])
        E("scalar", lambda s: s.activation(out=rs, in_=rs, func=AF.Exp, scale=-0.5),
          reads=["st_rs"], writes=["st_rs"])
        E("tensor", lambda t: t.matmul(PC[:, 0:512], ones[0:1, :], mu, start=True, stop=True),
          reads=["st_mu", "consts"], writes=["pC"])
        E("tensor", lambda t: t.matmul(PD[:, 0:512], ones[0:1, :], rs, start=True, stop=True),
          reads=["st_rs", "consts"], writes=["pD"])
        for c in range(NCHUNK):
            tmp = SQ[:, (c % 2) * 512:(c % 2) * 512 + 512]
            kq = f"ut{c % 2}"
            E("vector", lambda v, c=c, g=g, tmp=tmp: v.tensor_sub(
                tmp, csl(X, c, g * 512, g * 512 + 512), PC[:, 0:512]),
              reads=[src_key, "pC"], writes=[kq])
            E("vector", lambda v, c=c, g=g, tmp=tmp: v.tensor_mul(
                csl(XNB, c, g * 512, g * 512 + 512), tmp, PD[:, 0:512]),
              reads=[kq, "pD"], writes=[dst_key])


def emit_l0_attn(kb, C, XNB, X):
    E = kb.emit
    QK, QKB, PT, VT = C["QK"], C["QKB"], C["PT"], C["VT"]
    ONB, WQ = C["ONORM"], C["WQKV"]
    PC, PD, PO, PR, PBB = C["PC"], C["PD"], C["PO"], C["PR"], C["PBB"]
    bcol = C["l0_b_cols"]
    wq_in = C["in_l0_wqkvT"]
    gbank = [("pC", PC), ("pD", PD)]

    for hp in range(8):
        ws = hp % 2
        for i3 in range(3):
            E("gpsimd", lambda g, ws=ws, i3=i3, hp=hp: g.dma_start(
                out=WQ[:, ws * 3072 + i3 * 1024: ws * 3072 + i3 * 1024 + 1024]
                    .rearrange("p (a o) -> p a o", a=8),
                in_=wq_in.rearrange("(a p) o -> p a o", p=128)
                    [:, :, i3 * 1024 + hp * 128: i3 * 1024 + hp * 128 + 128]),
              writes=[f"w{ws}{i3}"], dma=f"wl{ws}")
        for i3 in range(3):
            for g in range(TG):
                bk, pb = gbank[g % 2]
                for c in range(NCHUNK):
                    E("tensor", lambda t, i3=i3, g=g, c=c, ws=ws, pb=pb: t.matmul(
                        pb[:, 0:512],
                        WQ[:, ws * 3072 + i3 * 1024 + c * 128: ws * 3072 + i3 * 1024 + c * 128 + 128],
                        csl(XNB, c, g * 512, g * 512 + 512),
                        start=(c == 0), stop=(c == NCHUNK - 1)),
                      reads=[f"w{ws}{i3}", "xnb"], writes=[bk])
                if i3 == 2:
                    E("scalar", lambda s, g=g, hp=hp, pb=pb: s.activation(
                        out=QKB[:, 2 * T + g * 512: 2 * T + g * 512 + 512],
                        in_=pb[:, 0:512], func=AF.Identity,
                        bias=bcol[:, 3 * hp + 2: 3 * hp + 3]),
                      reads=[bk, "l0bc"], writes=["qkb2"])
                    continue
                qk = QK[:, (g % 2) * 512: (g % 2) * 512 + 512]
                kqf = f"qkf{g % 2}"
                E("scalar", lambda s, i3=i3, g=g, hp=hp, pb=pb, qk=qk: s.activation(
                    out=qk, in_=pb[:, 0:512], func=AF.Identity,
                    bias=bcol[:, 3 * hp + i3: 3 * hp + i3 + 1]),
                  reads=[bk, "l0bc"], writes=[kqf])
                bk2, pb2 = gbank[(g + 1) % 2]
                E("tensor", lambda t, pb2=pb2, qk=qk: t.matmul(
                    pb2[:, 0:512], C["protL0_s"][:, :], qk,
                    start=True, stop=True),
                  reads=[kqf, "consts"], writes=[bk2])
                tmp = C["UTIL"][:, (g % 2) * 512:(g % 2) * 512 + 512]
                kq = f"ut{g % 2}"
                E("vector", lambda v, g=g, pb2=pb2, tmp=tmp: v.tensor_mul(
                    tmp, pb2[:, 0:512],
                    C["rope_s"][:, 2048 + g * 512: 2048 + g * 512 + 512]),
                  reads=[bk2, "consts"], writes=[kq])
                E("vector", lambda v, g=g, qk=qk: v.tensor_mul(
                    qk, qk, C["rope_s"][:, g * 512: g * 512 + 512]),
                  reads=[kqf, "consts"], writes=[kqf])
                E("vector", lambda v, i3=i3, g=g, tmp=tmp, qk=qk: v.tensor_add(
                    QKB[:, i3 * T + g * 512: i3 * T + g * 512 + 512], qk, tmp),
                  reads=[kqf, kq], writes=[f"qkb{i3}"])
        E("vector", lambda v: v.memset(VT[:], 1.0), writes=["vt"])
        for hh in range(2):
            vb = 64 * hh
            for mc in range(16):
                E("tensor", lambda t, vb=vb, mc=mc: t.transpose(
                    PBB[:, 0:64],
                    QKB[vb:vb + 64, 2 * T + mc * 128: 2 * T + mc * 128 + 128],
                    C["eye_b_s"][vb:vb + 64, vb:vb + 64]),
                  reads=["qkb2", "consts_b"], writes=["pBB"])
                E("vector", lambda v, hh=hh, mc=mc: v.tensor_copy(
                    VT[:, (hh * 16 + mc) * 65: (hh * 16 + mc) * 65 + 64],
                    PBB[:, 0:64]),
                  reads=["pBB"], writes=["vt"])
        for hh in range(2):
            qb = 64 * hh
            for g in range(TG):
                nmc = 4 * g + 4
                for mc in range(nmc):
                    bk, pb = gbank[mc % 2]
                    sl = mc % 4
                    pts = PT[:, sl * 512: sl * 512 + 512]
                    E("tensor", lambda t, qb=qb, g=g, mc=mc, pb=pb: t.matmul(
                        pb[:, 0:512],
                        QKB[qb:qb + 64, T + mc * 128: T + mc * 128 + 128],
                        QKB[qb:qb + 64, g * 512: g * 512 + 512],
                        start=True, stop=True),
                      reads=["qkb0", "qkb1"], writes=[bk])
                    E("scalar", lambda s, pb=pb, pts=pts: s.activation(
                        out=pts, in_=pb[:, 0:512],
                        func=AF.Exp, scale=0.125),
                      reads=[bk], writes=[f"pt{sl}"])
                    if mc >= 4 * g:
                        dlt = mc - 4 * g
                        E("vector", lambda v, dlt=dlt, pts=pts: v.tensor_mul(
                            pts, pts,
                            C["cmask_s"][:, dlt * 512: dlt * 512 + 512]),
                          reads=[f"pt{sl}", "consts_b"], writes=[f"pt{sl}"])
                    E("tensor", lambda t, hh=hh, mc=mc, nmc=nmc, pts=pts: t.matmul(
                        PO[0:65, 0:512],
                        VT[:, (hh * 16 + mc) * 65: (hh * 16 + mc) * 65 + 65],
                        pts,
                        start=(mc == 0), stop=(mc == nmc - 1)),
                      reads=["vt", f"pt{sl}"], writes=["pO"])
                E("vector", lambda v: v.reciprocal(C["U72"][64:65, 0:512], PO[64:65, 0:512]),
                  reads=["pO"], writes=["u72"])
                E("tensor", lambda t: t.matmul(
                    PR[0:64, 0:512], C["ones_f32"][64:65, 0:64], C["U72"][64:65, 0:512],
                    start=True, stop=True),
                  reads=["u72", "consts"], writes=["pR"])
                E("vector", lambda v: v.tensor_copy(C["UTIL"][0:64, 0:512], PR[0:64, 0:512]),
                  reads=["pR"], writes=["ut0"])
                E("vector", lambda v, hp=hp, hh=hh, g=g: v.tensor_mul(
                    csl(ONB, hp, g * 512, g * 512 + 512)[64 * hh: 64 * hh + 64, :],
                    PO[0:64, 0:512], C["UTIL"][0:64, 0:512]),
                  reads=["pO", "ut0"], writes=["onorm"])
    wp_in = C["in_l0_wprojT"]
    for og in range(NCHUNK):
        ws = og % 2
        wkey = f"w{ws}0"
        E("gpsimd", lambda g, og=og, ws=ws: g.dma_start(
            out=WQ[:, ws * 3072: ws * 3072 + 1024].rearrange("p (a o) -> p a o", a=8),
            in_=wp_in.rearrange("(a p) o -> p a o", p=128)
                [:, :, og * 128: og * 128 + 128]),
          writes=[wkey], dma=f"wl{ws}")
        for g in range(TG):
            bk, pb = gbank[g % 2]
            for c in range(NCHUNK):
                E("tensor", lambda t, ws=ws, g=g, c=c, pb=pb: t.matmul(
                    pb[:, 0:512],
                    WQ[:, ws * 3072 + c * 128: ws * 3072 + c * 128 + 128],
                    csl(ONB, c, g * 512, g * 512 + 512),
                    start=(c == 0), stop=(c == NCHUNK - 1)),
                  reads=[wkey, "onorm"], writes=[bk])
            E("vector", lambda v, og=og, g=g, pb=pb: v.scalar_tensor_tensor(
                out=csl(X, og, g * 512, g * 512 + 512),
                in0=pb[:, 0:512], scalar=C["l0_bp_col"][:, og:og + 1],
                in1=csl(X, og, g * 512, g * 512 + 512),
                op0=ALU.add, op1=ALU.add),
              reads=[bk, "l0bc", "x"], writes=["x"])


def build_launch1(consts):
    nc = bass.Bass("TRN2", target_bir_lowering=False)
    xT = nc.dram_tensor("xT", [D, T], F32, kind="ExternalInput").ap()
    wqkvT = nc.dram_tensor("l0_wqkvT", [D, 3072], F32, kind="ExternalInput").ap()
    bqkv = nc.dram_tensor("l0_bqkv", [128, 24], F32, kind="ExternalInput").ap()
    wprojT = nc.dram_tensor("l0_wprojT", [D, D], F32, kind="ExternalInput").ap()
    bproj = nc.dram_tensor("l0_bproj", [128, 8], F32, kind="ExternalInput").ap()
    caps = make_const_inputs(nc, consts)

    x1p = nc.dram_tensor("x1p", [D, T], F32, kind="ExternalOutput").ap()
    xnb_out = nc.dram_tensor("xnb_out", [D, T], BF16, kind="ExternalOutput").ap()
    onorm_out = nc.dram_tensor("onorm_out", [D, T], BF16, kind="ExternalOutput").ap()

    import contextlib
    with contextlib.ExitStack() as ec:
        e = ec.enter_context
        X = e(nc.sbuf_tensor("X", [128, NCHUNK * T], F32))
        XNB = e(nc.sbuf_tensor("XNB", [128, NCHUNK * T], BF16))
        ONORM = e(nc.sbuf_tensor("ONORM", [128, NCHUNK * T], BF16))
        QK = e(nc.sbuf_tensor("QK", [128, 1024], F32))
        QKB = e(nc.sbuf_tensor("QKB", [128, 3 * T], BF16))
        PT = e(nc.sbuf_tensor("PTS", [128, 4 * 512], BF16))
        VT = e(nc.sbuf_tensor("VT", [128, 2080], BF16))
        WQKV = e(nc.sbuf_tensor("WQKV", [128, 6144], BF16))
        UTIL = e(nc.sbuf_tensor("UTIL", [128, 1024], F32))
        U72 = e(nc.sbuf_tensor("U72", [72, 512], F32))
        STAT = e(nc.sbuf_tensor("STAT", [1, 2048], F32))
        ones_f32 = e(nc.sbuf_tensor("ones_f32", [128, 128], F32))
        ones_b_s = e(nc.sbuf_tensor("ones_b_s", [128, 128], BF16))
        eye_f_s = e(nc.sbuf_tensor("eye_f_s", [128, 128], F32))
        eye_b_s = e(nc.sbuf_tensor("eye_b_s", [128, 128], BF16))
        protL0_s = e(nc.sbuf_tensor("protL0_s", [128, 128], F32))
        rope_s = e(nc.sbuf_tensor("rope_s", [128, 4096], F32))
        cmask_s = e(nc.sbuf_tensor("cmask_s", [128, 2048], BF16))
        scl_s = e(nc.sbuf_tensor("scl_s", [1, 16], F32))
        bcol = e(nc.sbuf_tensor("l0_b_cols", [128, 24], F32))
        bpcol = e(nc.sbuf_tensor("l0_bp_col", [128, 8], F32))
        PA = e(nc.psum_tensor("PA", [128, 512], F32))
        PBk = e(nc.psum_tensor("PBk", [128, 512], F32))
        PC = e(nc.psum_tensor("PC", [128, 512], F32))
        PD = e(nc.psum_tensor("PD", [128, 512], F32))
        PO = e(nc.psum_tensor("PO", [128, 512], F32))
        PR = e(nc.psum_tensor("PR", [128, 512], F32))
        PBB = e(nc.psum_tensor("PBB", [128, 1024], BF16))

        kb = KB(nc)
        E = kb.emit
        C = dict(
            QK=QK, QKB=QKB, PT=PT, VT=VT, ONORM=ONORM, WQKV=WQKV,
            UTIL=UTIL, U72=U72, STAT=STAT,
            ones_f32=ones_f32, ones_b_s=ones_b_s, eye_f_s=eye_f_s,
            eye_b_s=eye_b_s, protL0_s=protL0_s,
            rope_s=rope_s, cmask_s=cmask_s, scl_s=scl_s,
            l0_b_cols=bcol, l0_bp_col=bpcol,
            in_l0_wqkvT=wqkvT, in_l0_wprojT=wprojT,
            PA=PA, PBk=PBk, PC=PC, PD=PD, PO=PO, PR=PR, PBB=PBB,
        )

        for c in range(NCHUNK):
            E("sync", lambda g, c=c: g.dma_start(
                out=X[:, c * T:(c + 1) * T], in_=xT[c * 128:(c + 1) * 128, :]),
              writes=["x"], dma="x")
        E("sync", lambda g: g.dma_start(out=ones_f32[:], in_=caps["ones_f"]),
          writes=["consts"], dma="c")
        E("gpsimd", lambda g: g.dma_start(out=ones_b_s[:], in_=caps["ones_b"]),
          writes=["consts_b"], dma="cb")
        E("sync", lambda g: g.dma_start(out=eye_f_s[:], in_=caps["eye_f"]),
          writes=["consts"], dma="c")
        E("sync", lambda g: g.dma_start(out=eye_b_s[:], in_=caps["eye_b"]),
          writes=["consts_b"], dma="c")
        E("sync", lambda g: g.dma_start(out=protL0_s[:], in_=caps["protL0"]),
          writes=["consts"], dma="c")
        E("sync", lambda g: g.dma_start(out=rope_s[:], in_=caps["rope12"]),
          writes=["consts"], dma="c")
        E("sync", lambda g: g.dma_start(out=cmask_s[:], in_=caps["cmask"]),
          writes=["consts_b"], dma="c")
        E("sync", lambda g: g.dma_start(out=scl_s[:], in_=caps["scl"]),
          writes=["consts"], dma="c")
        E("sync", lambda g: g.dma_start(out=bcol[:, :], in_=bqkv),
          writes=["l0bc"], dma="c")
        E("sync", lambda g: g.dma_start(out=bpcol[:, :], in_=bproj),
          writes=["l0bc"], dma="c")

        emit_ln(kb, C, "x", "xnb", X, XNB)
        emit_l0_attn(kb, C, XNB, X)

        for c in range(NCHUNK):
            E("sync", lambda g, c=c: g.dma_start(
                out=x1p[c * 128:(c + 1) * 128, :], in_=X[:, c * T:(c + 1) * T]),
              reads=["x"], dma="st", final=True)
            E("sync", lambda g, c=c: g.dma_start(
                out=onorm_out[c * 128:(c + 1) * 128, :], in_=ONORM[:, c * T:(c + 1) * T]),
              reads=["onorm"], dma="st", final=True)
            E("sync", lambda g, c=c: g.dma_start(
                out=xnb_out[c * 128:(c + 1) * 128, :], in_=XNB[:, c * T:(c + 1) * T]),
              reads=["xnb"], dma="st", final=True)
        kb.build()
    return nc




CLOC = 512


def emit_moe(kb, C, X, XNB):
    import os
    MOE_STAGES = frozenset(os.environ.get("MOE_STAGES", "abcde").split(",")[0])
    E = kb.emit
    PA, PBk, PC, PD, PR, PBB = C["PA"], C["PBk"], C["PC"], C["PD"], C["PR"], C["PBB"]
    UT = C["UTIL"]
    SCO, G4, IDXF, RS, AT = C["SCO"], C["G4"], C["IDXF"], C["RS"], C["AT"]
    GIDXI, GATE = C["GIDXI"], C["GATE"]
    XTM, XG, UU = C["XTM"], C["XG"], C["UU"]
    WE1, WE3, WE2, WG = C["WE1"], C["WE3"], C["WE2"], C["WG"]
    stat = C["STAT"]
    eye_b = C["eye_b_s"]
    ones = C["ones_f32"]
    xn_tm, lists, xout = C["d_xn_tm"], C["d_lists"], C["d_xout"]

    for i in range(16 if "a" in MOE_STAGES else 0):
        for c in range(NCHUNK):
            E("tensor", lambda t, i=i, c=c: t.transpose(
                PBB[:, 0:128], csl(XNB, c, i * 128, i * 128 + 128), eye_b[:, :]),
              reads=["xnb", "consts_b"], writes=["pBB"])
            E("vector", lambda v, c=c: v.tensor_copy(XTM[:, c * 128: c * 128 + 128], PBB[:, 0:128]),
              reads=["pBB"], writes=["xtm"])
        E("sync", lambda g, i=i: g.dma_start(
            out=xn_tm[i * 128:(i + 1) * 128, :], in_=XTM[:, :]),
          reads=["xtm"], writes=["d_xntm"], dma="xtm", final=False)

    E("vector", lambda v: v.memset(UT[:, :], 0.0), writes=["ut0", "ut1"])
    E("sync", lambda g: g.dma_start(
        out=lists.rearrange("(p a) c -> p a c", p=128), in_=UT[:, 0:258].rearrange("p (a c) -> p a c", c=2)),
      reads=["ut0"], writes=["d_lists"], dma="lst")
    E("sync", lambda g: g.dma_start(out=xout[16384:16385, :], in_=UT[0:1, 0:1024]),
      reads=["ut0", "ut1"], writes=["d_xout_z"], dma="lst")

    E("gpsimd", lambda g: g.dma_start(
        out=WG[:, :].rearrange("p (a o) -> p a o", a=8),
        in_=C["in_wg"].rearrange("(a p) o -> p a o", p=128)),
      writes=["wg"], dma="wg")
    for tg in range(TG):
        for c in range(NCHUNK):
            E("tensor", lambda t, tg=tg, c=c: t.matmul(
                PC[0:32, 0:512], WG[:, c * 32: c * 32 + 32],
                csl(XNB, c, tg * 512, tg * 512 + 512),
                start=(c == 0), stop=(c == NCHUNK - 1)),
              reads=["wg", "xnb"], writes=["pC"])
        E("vector", lambda v, tg=tg: v.tensor_copy(UT[0:32, 0:512], PC[0:32, 0:512]),
          reads=["pC"], writes=["ut0"])
        for q in range(4):
            E("tensor", lambda t, q=q: t.transpose(
                PD[:, 0:32], UT[0:32, q * 128: q * 128 + 128], C["eye_f_s"][0:32, 0:32]),
              reads=["ut0", "consts"], writes=["pD"])
            E("vector", lambda v, tg=tg, q=q: v.tensor_copy(
                SCO[:, (tg * 4 + q) * 32: (tg * 4 + q) * 32 + 32], PD[:, 0:32]),
              reads=["pD"], writes=["sco"])

    BP = frozenset(os.environ.get("MOE_BP", "123456"))
    E("vector", lambda v: v.memset(stat[0:1, 0:32], 0.0), writes=["prefrow"])
    for i in range(16 if "b" in MOE_STAGES else 0):
        sco = SCO[:, i * 32: i * 32 + 32]
        u0 = UT[:, 0:1]
        u1 = UT[:, 1:2]
        u2 = UT[:, 2:3]
        u3 = UT[:, 3:4]
        E("vector", lambda v, sco=sco: v.reduce_max(u0, sco, axis=mybir.AxisListType.X),
          reads=["sco"], writes=["ut0"])
        E("vector", lambda v: v.tensor_scalar(out=u1, in0=u0, scalar1=-1.0, scalar2=None, op0=ALU.mult),
          reads=["ut0"], writes=["ut0"])
        E("scalar", lambda s, sco=sco: s.activation(out=sco, in_=sco, func=AF.Exp,
                                                    bias=u1, accum_out=u2),
          reads=["sco", "ut0"], writes=["sco", "ut0"])
        E("vector", lambda v: v.reciprocal(u3, u2), reads=["ut0"], writes=["ut0"])
        E("vector", lambda v, sco=sco: v.tensor_scalar(out=sco, in0=sco, scalar1=u3,
                                                       scalar2=None, op0=ALU.mult),
          reads=["sco", "ut0"], writes=["sco"])
        if "2" not in BP:
            continue
        mx8 = UT[:, 8:16]
        E("vector", lambda v, sco=sco, mx8=mx8: v.max(out=mx8, in_=sco), reads=["sco"], writes=["ut0"])
        E("vector", lambda v, i=i, mx8=mx8: v.tensor_copy(G4[:, i * 4: i * 4 + 4], mx8[:, 0:4]),
          reads=["ut0"], writes=["g4"])
        idxu = C["IDXU"]
        E("vector", lambda v, sco=sco, mx8=mx8, idxu=idxu: v.max_index(
            out=idxu[:, 0:8], in_max=mx8[:, 0:8], in_values=sco),
          reads=["sco", "ut0"], writes=["idxu"])
        E("vector", lambda v, i=i, idxu=idxu: v.tensor_copy(IDXF[:, i * 4: i * 4 + 4], idxu[:, 0:4]),
          reads=["idxu"], writes=["idxf"])
        if "3" not in BP:
            continue
        at = AT[:, i * 32: i * 32 + 32]
        E("vector", lambda v, at=at: v.memset(at, 0.0), writes=["at"])
        for k in range(4):
            msk = UT[:, 32:64]
            E("vector", lambda v, i=i, k=k, msk=msk: v.tensor_scalar(
                out=msk, in0=C["iota32_s"][:, :], scalar1=IDXF[:, i * 4 + k: i * 4 + k + 1],
                scalar2=None, op0=ALU.is_equal),
              reads=["idxf", "consts"], writes=["ut0"])
            E("vector", lambda v, at=at, msk=msk: v.tensor_add(at, at, msk),
              reads=["ut0", "at"], writes=["at"])
        if "4" not in BP:
            continue
        E("tensor", lambda t, at=at: t.matmul(PD[:, 0:32], C["lstrict_s"][:, :], at,
                                              start=True, stop=False),
          reads=["at", "consts"], writes=["pD"])
        E("tensor", lambda t: t.matmul(PD[:, 0:32], ones[0:1, :], stat[0:1, 0:32],
                                       start=False, stop=True),
          reads=["prefrow", "consts"], writes=["pD"])
        E("vector", lambda v, i=i: v.tensor_copy(RS[:, i * 32: i * 32 + 32], PD[:, 0:32]),
          reads=["pD"], writes=["rs"])
        E("tensor", lambda t, at=at: t.matmul(PR[0:1, 0:32], ones[:, 0:1], at,
                                              start=True, stop=True),
          reads=["at", "consts"], writes=["pR"])
        E("vector", lambda v: v.tensor_add(stat[0:1, 0:32], stat[0:1, 0:32], PR[0:1, 0:32]),
          reads=["pR", "prefrow"], writes=["prefrow"])
        if "5" not in BP:
            continue
        tokf = UT[:, 64:65]
        E("vector", lambda v, i=i, tokf=tokf: v.tensor_scalar(
            out=tokf, in0=C["tokcol_s"][:, :], scalar1=float(i * 128), scalar2=None, op0=ALU.add),
          reads=["consts"], writes=["ut0"])
        for k in range(4):
            msk = UT[:, 32:64]
            rk = UT[:, 65:66]
            vk = UT[:, 66:67]
            gk = UT[:, 67:68]
            dk = UT[:, 68:69]
            sc32 = UT[:, 96:128]
            E("vector", lambda v, i=i, k=k, msk=msk: v.tensor_scalar(
                out=msk, in0=C["iota32_s"][:, :], scalar1=IDXF[:, i * 4 + k: i * 4 + k + 1],
                scalar2=None, op0=ALU.is_equal),
              reads=["idxf", "consts"], writes=["ut0"])
            E("vector", lambda v, i=i, msk=msk, sc32=sc32: v.tensor_mul(
                sc32, RS[:, i * 32: i * 32 + 32], msk),
              reads=["rs", "ut0"], writes=["ut0"])
            E("vector", lambda v, rk=rk, sc32=sc32: v.reduce_sum(
                rk, sc32, axis=mybir.AxisListType.X),
              reads=["ut0"], writes=["ut0"])
            E("vector", lambda v, rk=rk, vk=vk: v.tensor_scalar(
                out=vk, in0=rk, scalar1=float(CLOC), scalar2=None, op0=ALU.is_lt),
              reads=["ut0"], writes=["ut0"])
            E("vector", lambda v, i=i, k=k, vk=vk, gk=gk: v.tensor_mul(
                gk, G4[:, i * 4 + k: i * 4 + k + 1], vk),
              reads=["g4", "ut0"], writes=["ut0"])
            E("vector", lambda v, i=i, k=k, gk=gk: v.tensor_copy(
                GATE[:, i * 4 + k: i * 4 + k + 1], gk),
              reads=["ut0"], writes=["gate"])
            E("vector", lambda v, i=i, k=k, dk=dk: v.tensor_scalar(
                out=dk, in0=IDXF[:, i * 4 + k: i * 4 + k + 1], scalar1=float(CLOC),
                scalar2=None, op0=ALU.mult),
              reads=["idxf"], writes=["ut0"])
            E("vector", lambda v, rk=rk, dk=dk: v.tensor_add(dk, dk, rk),
              reads=["ut0", "ut0"], writes=["ut0"])
            E("vector", lambda v, dk=dk: v.tensor_scalar(
                out=dk, in0=dk, scalar1=-16384.0, scalar2=None, op0=ALU.add),
              reads=["ut0"], writes=["ut0"])
            E("vector", lambda v, dk=dk, vk=vk: v.tensor_mul(dk, dk, vk),
              reads=["ut0", "ut0"], writes=["ut0"])
            E("vector", lambda v, dk=dk: v.tensor_scalar(
                out=dk, in0=dk, scalar1=16384.0, scalar2=None, op0=ALU.add),
              reads=["ut0"], writes=["ut0"])
            E("vector", lambda v, i=i, k=k, dk=dk: v.tensor_copy(
                GIDXI[:, i * 4 + k: i * 4 + k + 1], dk),
              reads=["ut0"], writes=["gidxi"])
            pair = UT[:, 70:72]
            E("vector", lambda v, pair=pair, tokf=tokf: v.tensor_copy(pair[:, 0:1], tokf),
              reads=["ut0"], writes=["ut0"])
            E("vector", lambda v, pair=pair, gk=gk: v.tensor_copy(pair[:, 1:2], gk),
              reads=["ut0"], writes=["ut0"])
            di = C["DI1"]
            E("vector", lambda v, dk=dk, di=di: v.tensor_copy(di[:, 0:1], dk),
              reads=["ut0"], writes=["di1"])
            if "6" in BP:
                E("gpsimd", lambda g, pair=pair, di=di: g.indirect_dma_start(
                    out=lists, out_offset=bass.IndirectOffsetOnAxis(ap=di[:, 0:1], axis=0),
                    in_=pair, in_offset=None),
                  reads=["ut0", "ut0", "di1", "d_lists"], writes=["d_lists"], dma="isc")

    for nm, w_in, wt in (("we1", C["in_ws1"], WE1), ("we3", C["in_ws3"], WE3), ("we2", C["in_ws2"], WE2)):
        E("gpsimd", lambda g, w_in=w_in, wt=wt: g.dma_start(
            out=wt[:, :].rearrange("p (a o) -> p a o", a=8),
            in_=w_in.rearrange("(a p) o -> p a o", p=128)),
          writes=[nm + "0", nm + "1"], dma="wl0")
    for tg in range(TG if "c" in MOE_STAGES else 0):
        for hc in range(8):
            for c in range(NCHUNK):
                E("tensor", lambda t, tg=tg, hc=hc, c=c: t.matmul(
                    PA[:, 0:512], WE1[:, c * 1024 + hc * 128: c * 1024 + hc * 128 + 128],
                    csl(XNB, c, tg * 512, tg * 512 + 512),
                    start=(c == 0), stop=(c == NCHUNK - 1)),
                  reads=["we10", "we11", "xnb"], writes=["pA"])
            for c in range(NCHUNK):
                E("tensor", lambda t, tg=tg, hc=hc, c=c: t.matmul(
                    PBk[:, 0:512], WE3[:, c * 1024 + hc * 128: c * 1024 + hc * 128 + 128],
                    csl(XNB, c, tg * 512, tg * 512 + 512),
                    start=(c == 0), stop=(c == NCHUNK - 1)),
                  reads=["we30", "we31", "xnb"], writes=["pB"])
            sl = UT[:, 0:512]
            E("scalar", lambda s, sl=sl: s.activation(out=sl, in_=PA[:, 0:512], func=AF.Silu),
              reads=["pA"], writes=["ut0"])
            E("vector", lambda v, hc=hc, sl=sl: v.tensor_mul(
                UU[:, hc * 512: hc * 512 + 512], PBk[:, 0:512], sl),
              reads=["pB", "ut0"], writes=[f"uu{hc}"])
        for dc in range(NCHUNK):
            bk, pb = (("pC", PC) if dc % 2 == 0 else ("pD", PD))
            for hc in range(8):
                E("tensor", lambda t, dc=dc, hc=hc, pb=pb: t.matmul(
                    pb[:, 0:512],
                    WE2[:, hc * 1024 + dc * 128: hc * 1024 + dc * 128 + 128],
                    UU[:, hc * 512: hc * 512 + 512],
                    start=(hc == 0), stop=(hc == 7)),
                  reads=["we20", "we21", f"uu{hc}"], writes=[bk])
            E("vector", lambda v, tg=tg, dc=dc, pb=pb, bk=bk: v.tensor_add(
                csl(X, dc, tg * 512, tg * 512 + 512), pb[:, 0:512],
                csl(X, dc, tg * 512, tg * 512 + 512)),
              reads=[bk, "x"], writes=["x"])

    LG, LI, GC, DI1 = C["LG"], C["LI"], C["GC"], C["DI1"]
    OFD, OT4, GR, ACC = C["OFD"], C["OT4"], C["GR"], C["ACC"]
    for e in range(32 if "d" in MOE_STAGES else 0):
        s = e % 2
        E("gpsimd", lambda g, e=e, s=s: g.dma_start(
            out=WE1[:, s * 4096: s * 4096 + 4096].rearrange("p (a o) -> p a o", a=8),
            in_=C["in_w1"][e].rearrange("(a p) o -> p a o", p=128)),
          writes=[f"we1{s}"], dma=f"wl{s}")
        E("gpsimd", lambda g, e=e, s=s: g.dma_start(
            out=WE3[:, s * 4096: s * 4096 + 4096].rearrange("p (a o) -> p a o", a=8),
            in_=C["in_w3"][e].rearrange("(a p) o -> p a o", p=128)),
          writes=[f"we3{s}"], dma=f"wl{s}")
        E("gpsimd", lambda g, e=e, s=s: g.dma_start(
            out=WE2[:, s * 4096: s * 4096 + 4096].rearrange("p (a o) -> p a o", a=4),
            in_=C["in_w2"][e].rearrange("(a p) o -> p a o", p=128)),
          writes=[f"we2{s}"], dma=f"wl{s}")
        E("sync", lambda g, e=e: g.dma_start(
            out=LG[:, :].rearrange("p (a c) -> p a c", c=2),
            in_=bass.AP(C["d_lists"].tensor, e * CLOC * 2, [[2, 128], [256, 4], [1, 2]])),
          reads=["d_lists"], writes=["lg"], dma="lst")
        E("vector", lambda v: v.tensor_copy(LI[:, 0:4], LG[:, :].rearrange("p (a c) -> p a c", c=2)[:, :, 0]),
          reads=["lg"], writes=["li"])
        E("vector", lambda v: v.tensor_copy(GC[:, 0:4], LG[:, :].rearrange("p (a c) -> p a c", c=2)[:, :, 1]),
          reads=["lg"], writes=["gc"])
        for j in range(4):
            E("gpsimd", lambda g, j=j: g.indirect_dma_start(
                out=XTM[:, :], out_offset=None, in_=C["d_xn_tm"],
                in_offset=bass.IndirectOffsetOnAxis(ap=LI[:, j:j + 1], axis=0)),
              reads=["li", "d_xntm"], writes=["xtm"], dma="gx")
            for c in range(NCHUNK):
                E("tensor", lambda t, c=c: t.transpose(
                    PBB[:, 0:128], XTM[:, c * 128: c * 128 + 128], C["eye_b_s"][:, :]),
                  reads=["xtm", "consts_b"], writes=["pBB"])
                E("vector", lambda v, c=c, j=j: v.tensor_copy(
                    XG[:, c * 512 + j * 128: c * 512 + j * 128 + 128], PBB[:, 0:128]),
                  reads=["pBB"], writes=["xg"])
        for hc in range(4):
            for c in range(NCHUNK):
                E("tensor", lambda t, hc=hc, c=c, s=s: t.matmul(
                    PA[:, 0:512], WE1[:, s * 4096 + c * 512 + hc * 128: s * 4096 + c * 512 + hc * 128 + 128],
                    XG[:, c * 512: c * 512 + 512],
                    start=(c == 0), stop=(c == NCHUNK - 1)),
                  reads=[f"we1{s}", "xg"], writes=["pA"])
            for c in range(NCHUNK):
                E("tensor", lambda t, hc=hc, c=c, s=s: t.matmul(
                    PBk[:, 0:512], WE3[:, s * 4096 + c * 512 + hc * 128: s * 4096 + c * 512 + hc * 128 + 128],
                    XG[:, c * 512: c * 512 + 512],
                    start=(c == 0), stop=(c == NCHUNK - 1)),
                  reads=[f"we3{s}", "xg"], writes=["pB"])
            sl = UT[:, 0:512]
            E("scalar", lambda s2, sl=sl: s2.activation(out=sl, in_=PA[:, 0:512], func=AF.Silu),
              reads=["pA"], writes=["ut0"])
            E("vector", lambda v, hc=hc, sl=sl: v.tensor_mul(
                UU[:, hc * 512: hc * 512 + 512], PBk[:, 0:512], sl),
              reads=["pB", "ut0"], writes=[f"uu{hc}"])
        for dc in range(NCHUNK):
            bk, pb = (("pC", PC) if dc % 2 == 0 else ("pD", PD))
            for hc in range(4):
                E("tensor", lambda t, dc=dc, hc=hc, s=s, pb=pb: t.matmul(
                    pb[:, 0:512],
                    WE2[:, s * 4096 + hc * 1024 + dc * 128: s * 4096 + hc * 1024 + dc * 128 + 128],
                    UU[:, hc * 512: hc * 512 + 512],
                    start=(hc == 0), stop=(hc == 3)),
                  reads=[f"we2{s}", f"uu{hc}"], writes=[bk])
            E("vector", lambda v, pb=pb, bk=bk: v.tensor_copy(OFD[:, 0:512], pb[:, 0:512]),
              reads=[bk], writes=["ofd"])
            for j in range(4):
                E("tensor", lambda t, j=j: t.transpose(
                    PR[:, 0:128], OFD[:, j * 128: j * 128 + 128], C["eye_f_s"][:, :]),
                  reads=["ofd", "consts"], writes=["pR"])
                E("scalar", lambda s2, j=j, dc=dc: s2.activation(
                    out=OT4[:, j * 1024 + dc * 128: j * 1024 + dc * 128 + 128],
                    in_=PR[:, 0:128], func=AF.Identity, scale=GC[:, j:j + 1]),
                  reads=["pR", "gc"], writes=[f"ot{j}"])
        for j in range(4):
            E("sync", lambda g, e=e, j=j: g.dma_start(
                out=C["d_xout"][e * CLOC + j * 128: e * CLOC + j * 128 + 128, :],
                in_=OT4[:, j * 1024: j * 1024 + 1024]),
              reads=[f"ot{j}"], writes=["d_xout"], dma="xo")

    for i in range(16 if "e" in MOE_STAGES else 0):
        for k in range(4):
            E("vector", lambda v, i=i, k=k: v.tensor_copy(DI1[:, 0:1], GIDXI[:, i * 4 + k: i * 4 + k + 1]),
              reads=["gidxi"], writes=["di1"])
            E("gpsimd", lambda g: g.indirect_dma_start(
                out=GR[:, :], out_offset=None, in_=C["d_xout"],
                in_offset=bass.IndirectOffsetOnAxis(ap=DI1[:, 0:1], axis=0)),
              reads=["di1", "d_xout", "d_xout_z"], writes=["gr"], dma="gy")
            if k == 0:
                E("vector", lambda v: v.tensor_copy(ACC[:, :], GR[:, :]),
                  reads=["gr"], writes=["acc"])
            else:
                E("vector", lambda v: v.tensor_add(ACC[:, :], ACC[:, :], GR[:, :]),
                  reads=["gr", "acc"], writes=["acc"])
        for c in range(NCHUNK):
            E("tensor", lambda t, c=c: t.transpose(
                PR[:, 0:128], ACC[:, c * 128: c * 128 + 128], C["eye_f_s"][:, :]),
              reads=["acc", "consts"], writes=["pR"])
            E("vector", lambda v, i=i, c=c: v.tensor_add(
                csl(X, c, i * 128, i * 128 + 128), PR[:, 0:128],
                csl(X, c, i * 128, i * 128 + 128)),
              reads=["pR", "x"], writes=["x"])


def build_launch2(consts):
    nc = bass.Bass("TRN2", target_bir_lowering=False)
    x1p = nc.dram_tensor("x1p", [D, T], F32, kind="ExternalInput").ap()
    xnb_in = nc.dram_tensor("xnb_in", [D, T], BF16, kind="ExternalInput").ap()
    wg = nc.dram_tensor("moe_wg", [D, 32], F32, kind="ExternalInput").ap()
    w1 = nc.dram_tensor("moe_w1", [32, D, 512], F32, kind="ExternalInput").ap()
    w2 = nc.dram_tensor("moe_w2", [32, 512, D], F32, kind="ExternalInput").ap()
    w3 = nc.dram_tensor("moe_w3", [32, D, 512], F32, kind="ExternalInput").ap()
    ws1 = nc.dram_tensor("moe_ws1", [D, D], F32, kind="ExternalInput").ap()
    ws2 = nc.dram_tensor("moe_ws2", [D, D], F32, kind="ExternalInput").ap()
    ws3 = nc.dram_tensor("moe_ws3", [D, D], F32, kind="ExternalInput").ap()
    caps = make_const_inputs(nc, consts)

    x1_out = nc.dram_tensor("x1_out", [D, T], F32, kind="ExternalOutput").ap()

    d_xn_tm = nc.dram_tensor("d_xn_tm", [T, D], BF16, kind="Internal").ap()
    d_lists = nc.dram_tensor("d_lists", [16512, 2], F32, kind="Internal").ap()
    d_xout = nc.dram_tensor("d_xout", [16385, D], F32, kind="Internal").ap()

    import contextlib
    with contextlib.ExitStack() as ec:
        e = ec.enter_context
        X = e(nc.sbuf_tensor("X", [128, NCHUNK * T], F32))
        XNB = e(nc.sbuf_tensor("XNB", [128, NCHUNK * T], BF16))
        WE1 = e(nc.sbuf_tensor("WE1", [128, 8192], BF16))
        WE3 = e(nc.sbuf_tensor("WE3", [128, 8192], BF16))
        WE2 = e(nc.sbuf_tensor("WE2", [128, 8192], BF16))
        WG = e(nc.sbuf_tensor("WG", [128, 256], BF16))
        XG = e(nc.sbuf_tensor("XG", [128, 4096], BF16))
        UU = e(nc.sbuf_tensor("UU", [128, 4096], BF16))
        XTM = e(nc.sbuf_tensor("XTM", [128, 1024], BF16))
        OFD = e(nc.sbuf_tensor("OFD", [128, 512], F32))
        OT4 = e(nc.sbuf_tensor("OT4", [128, 4096], F32))
        GR = e(nc.sbuf_tensor("GR", [128, 1024], F32))
        ACC = e(nc.sbuf_tensor("ACC", [128, 1024], F32))
        SCO = e(nc.sbuf_tensor("SCO", [128, 512], F32))
        G4 = e(nc.sbuf_tensor("G4", [128, 64], F32))
        IDXF = e(nc.sbuf_tensor("IDXF", [128, 64], F32))
        IDXU = e(nc.sbuf_tensor("IDXU", [128, 8], U32))
        RS = e(nc.sbuf_tensor("RS", [128, 512], F32))
        AT = e(nc.sbuf_tensor("AT", [128, 512], F32))
        GATE = e(nc.sbuf_tensor("GATE", [128, 64], F32))
        GIDXI = e(nc.sbuf_tensor("GIDXI", [128, 64], F32))
        DI1 = e(nc.sbuf_tensor("DI1", [128, 1], I32))
        LG = e(nc.sbuf_tensor("LG", [128, 8], F32))
        LI = e(nc.sbuf_tensor("LI", [128, 4], I32))
        GC = e(nc.sbuf_tensor("GC", [128, 4], F32))
        UTIL = e(nc.sbuf_tensor("UTIL", [128, 1024], F32))
        STAT = e(nc.sbuf_tensor("STAT", [1, 64], F32))
        ones_f32 = e(nc.sbuf_tensor("ones_f32", [128, 128], F32))
        eye_f_s = e(nc.sbuf_tensor("eye_f_s", [128, 128], F32))
        eye_b_s = e(nc.sbuf_tensor("eye_b_s", [128, 128], BF16))
        iota32_s = e(nc.sbuf_tensor("iota32_s", [128, 32], F32))
        tokcol_s = e(nc.sbuf_tensor("tokcol_s", [128, 1], F32))
        lstrict_s = e(nc.sbuf_tensor("lstrict_s", [128, 128], F32))
        PA = e(nc.psum_tensor("PA", [128, 512], F32))
        PBk = e(nc.psum_tensor("PBk", [128, 512], F32))
        PC = e(nc.psum_tensor("PC", [128, 512], F32))
        PD = e(nc.psum_tensor("PD", [128, 512], F32))
        PR = e(nc.psum_tensor("PR", [128, 512], F32))
        PBB = e(nc.psum_tensor("PBB", [128, 1024], BF16))

        kb = KB(nc)
        E = kb.emit
        C = dict(
            UTIL=UTIL, STAT=STAT, ones_f32=ones_f32, eye_f_s=eye_f_s, eye_b_s=eye_b_s,
            iota32_s=iota32_s, tokcol_s=tokcol_s, lstrict_s=lstrict_s,
            SCO=SCO, G4=G4, IDXF=IDXF, IDXU=IDXU, RS=RS, AT=AT, GATE=GATE, GIDXI=GIDXI,
            XTM=XTM, XG=XG, UU=UU, OFD=OFD, OT4=OT4, GR=GR, ACC=ACC,
            WE1=WE1, WE3=WE3, WE2=WE2, WG=WG, LG=LG, LI=LI, GC=GC, DI1=DI1,
            PA=PA, PBk=PBk, PC=PC, PD=PD, PR=PR, PBB=PBB,
            d_xn_tm=d_xn_tm, d_lists=d_lists, d_xout=d_xout,
            in_wg=wg, in_w1=w1, in_w2=w2, in_w3=w3,
            in_ws1=ws1, in_ws2=ws2, in_ws3=ws3,
        )

        for c in range(NCHUNK):
            E("sync", lambda g, c=c: g.dma_start(
                out=X[:, c * T:(c + 1) * T], in_=x1p[c * 128:(c + 1) * 128, :]),
              writes=["x"], dma="x")
            E("sync", lambda g, c=c: g.dma_start(
                out=XNB[:, c * T:(c + 1) * T], in_=xnb_in[c * 128:(c + 1) * 128, :]),
              writes=["xnb"], dma="x")
        for nm, cap, sb in (("consts", "ones_f", ones_f32), ("consts", "eye_f", eye_f_s),
                            ("consts", "iota32", iota32_s), ("consts", "tokcol", tokcol_s),
                            ("consts", "lstrict", lstrict_s)):
            E("sync", lambda g, cap=cap, sb=sb: g.dma_start(out=sb[:, :], in_=caps[cap]),
              writes=[nm], dma="c")
        E("gpsimd", lambda g: g.dma_start(out=eye_b_s[:], in_=caps["eye_b"].bitcast(BF16) if False else caps["eye_b"]),
          writes=["consts_b"], dma="cb")

        emit_moe(kb, C, X, XNB)

        for c in range(NCHUNK):
            E("sync", lambda g, c=c: g.dma_start(
                out=x1_out[c * 128:(c + 1) * 128, :], in_=X[:, c * T:(c + 1) * T]),
              reads=["x"], dma="st", final=True)
        kb.build()
    return nc


def emit_l1a(kb, C, XNB, X):
    E = kb.emit
    PA, PBk, PC, PD, PO, PR, PBB = (C["PA"], C["PBk"], C["PC"], C["PD"],
                                    C["PO"], C["PR"], C["PBB"])
    UT = C["UTIL"]
    Q1, K1, V1 = C["Q1"], C["K1"], C["V1"]
    VT1, VTLO, VTHI = C["VT1"], C["VTLO"], C["VTHI"]
    PT = C["PT"]
    WH = C["WH"]
    AUG = C["AUG"]
    ONER = C["ONER"]
    PADB = C["PADB"]
    QEF = C["QEF"]
    EVBC = C["EVBC"]
    TMPB = C["TMPB"]
    U72 = C["U72"]
    eye_b = C["eye_b_s"]
    ones = C["ones_f32"]
    ones_b = C["ones_b_s"]
    qepad, kepad, pbd = C["d_qepad"], C["d_kepad"], C["d_pbd"]

    E("vector", lambda v: v.tensor_mul(QEF[0:64, 0:129], C["eqT_f"][:, :], C["ekT_f"][:, :]),
      reads=["consts"], writes=["qef"])
    E("tensor", lambda t: t.matmul(PA[0:1, 0:129], ones[0:64, 0:1], QEF[0:64, 0:129],
                                   start=True, stop=True),
      reads=["qef", "consts"], writes=["pA"])
    E("vector", lambda v: v.tensor_copy(C["PPROW"][0:1, 0:129], PA[0:1, 0:129]),
      reads=["pA"], writes=["pprow"])
    E("tensor", lambda t: t.matmul(PR[:, 0:64], ones_b[0:1, :], C["ev128_b"][0:1, 0:64],
                                   start=True, stop=True),
      reads=["consts_b"], writes=["pR"])
    E("vector", lambda v: v.tensor_copy(EVBC[:, 0:64], PR[:, 0:64]), reads=["pR"], writes=["evbc"])
    E("tensor", lambda t: t.matmul(PR[:, 0:64], ones_b[0:1, :], C["ev0_b"][0:1, 0:64],
                                   start=True, stop=True),
      reads=["consts_b"], writes=["pR"])
    E("vector", lambda v: v.tensor_copy(EVBC[:, 64:128], PR[:, 0:64]), reads=["pR"], writes=["evbc"])
    E("vector", lambda v: v.memset(ONER[:, :], 1.0), writes=["oner"])

    for h in range(16):
        ws = h % 2
        for pi, wsrc in ((0, C["in_wq1"]), (1, C["in_wk1"]), (2, C["in_wv1"])):
            E("gpsimd", lambda g, h=h, ws=ws, pi=pi, wsrc=wsrc: g.dma_start(
                out=WH[:, ws * 1536 + pi * 512: ws * 1536 + pi * 512 + 512]
                    .rearrange("p (a o) -> p a o", a=8),
                in_=wsrc.rearrange("(a p) o -> p a o", p=128)
                    [:, :, h * 64: h * 64 + 64]),
              writes=[f"wh{ws}{pi}"], dma=f"wl{ws}")
        for pi, dst, dk in ((0, Q1, "q1"), (1, K1, "k1"), (2, V1, "v1")):
            for g2 in range(TG):
                bk, pb = (("pC", PC) if g2 % 2 == 0 else ("pD", PD))
                for c in range(NCHUNK):
                    E("tensor", lambda t, ws=ws, pi=pi, g2=g2, c=c, pb=pb: t.matmul(
                        pb[0:64, 0:512],
                        WH[:, ws * 1536 + pi * 512 + c * 64: ws * 1536 + pi * 512 + c * 64 + 64],
                        csl(XNB, c, g2 * 512, g2 * 512 + 512),
                        start=(c == 0), stop=(c == NCHUNK - 1)),
                      reads=[f"wh{ws}{pi}", "xnb"], writes=[bk])
                E("vector", lambda v, dst=dst, g2=g2, pb=pb: v.tensor_copy(
                    dst[:, g2 * 512: g2 * 512 + 512], pb[0:64, 0:512]),
                  reads=[bk], writes=[dk])
        E("vector", lambda v: v.memset(VT1[:], 1.0), writes=["vt1"])
        for mc in range(16):
            E("tensor", lambda t, mc=mc: t.transpose(
                PBB[:, 0:64], V1[:, mc * 128: mc * 128 + 128], eye_b[0:64, 0:64]),
              reads=["v1", "consts_b"], writes=["pBB"])
            E("vector", lambda v, mc=mc: v.tensor_copy(
                VT1[:, mc * 65: mc * 65 + 64], PBB[:, 0:64]),
              reads=["pBB"], writes=["vt1"])
        E("vector", lambda v: v.tensor_copy(VTLO[:], VT1[:]), reads=["vt1"], writes=["vtlo"])
        E("vector", lambda v: v.tensor_copy(VTHI[:], VT1[:]), reads=["vt1"], writes=["vthi"])
        for mc in range(16):
            E("vector", lambda v, mc=mc: v.tensor_add(
                VTLO[:, mc * 65: mc * 65 + 64], VTLO[:, mc * 65: mc * 65 + 64], EVBC[:, 0:64]),
              reads=["evbc", "vtlo"], writes=["vtlo"])
            E("vector", lambda v, mc=mc: v.tensor_add(
                VTHI[:, mc * 65: mc * 65 + 64], VTHI[:, mc * 65: mc * 65 + 64], EVBC[:, 64:128]),
              reads=["evbc", "vthi"], writes=["vthi"])
        for g2 in range(TG):
            E("tensor", lambda t, g2=g2: t.matmul(
                PA[0:1, 0:512], C["ekT_b"][:, 128:129], Q1[:, g2 * 512: g2 * 512 + 512],
                start=True, stop=True),
              reads=["q1", "consts_b"], writes=["pA"])
            E("vector", lambda v, g2=g2: v.tensor_copy(
                AUG[0:1, 0 * 2048 + g2 * 512: 0 * 2048 + g2 * 512 + 512], PA[0:1, 0:512]),
              reads=["pA"], writes=["aug"])
            E("tensor", lambda t, g2=g2: t.matmul(
                PA[0:1, 0:512], C["ekT_b"][:, 0:1], Q1[:, g2 * 512: g2 * 512 + 512],
                start=True, stop=True),
              reads=["q1", "consts_b"], writes=["pA"])
            E("vector", lambda v, g2=g2: v.tensor_copy(
                AUG[0:1, 1 * 2048 + g2 * 512: 1 * 2048 + g2 * 512 + 512], PA[0:1, 0:512]),
              reads=["pA"], writes=["aug"])
            E("tensor", lambda t, g2=g2: t.matmul(
                PA[0:1, 0:512], C["eqT_b"][:, 128:129], K1[:, g2 * 512: g2 * 512 + 512],
                start=True, stop=True),
              reads=["k1", "consts_b"], writes=["pA"])
            E("scalar", lambda s, g2=g2: s.activation(
                out=AUG[0:1, 2 * 2048 + g2 * 512: 2 * 2048 + g2 * 512 + 512],
                in_=PA[0:1, 0:512], func=AF.Identity, bias=C["PPROW"][0:1, 128:129]),
              reads=["pA", "pprow"], writes=["aug"])
            E("tensor", lambda t, g2=g2: t.matmul(
                PA[0:1, 0:512], C["eqT_b"][:, 0:1], K1[:, g2 * 512: g2 * 512 + 512],
                start=True, stop=True),
              reads=["k1", "consts_b"], writes=["pA"])
            E("scalar", lambda s, g2=g2: s.activation(
                out=AUG[0:1, 3 * 2048 + g2 * 512: 3 * 2048 + g2 * 512 + 512],
                in_=PA[0:1, 0:512], func=AF.Identity, bias=C["PPROW"][0:1, 0:1]),
              reads=["pA", "pprow"], writes=["aug"])
        for lt in range(16):
            E("tensor", lambda t, lt=lt: t.matmul(
                PA[:, 0:129], Q1[:, lt * 128: lt * 128 + 128], C["ekTrev_b"][:, :],
                start=True, stop=True),
              reads=["q1", "consts_b"], writes=["pA"])
            E("vector", lambda v: v.tensor_copy(QEF[:, 0:129], PA[:, 0:129]),
              reads=["pA"], writes=["qef"])
            E("vector", lambda v: v.tensor_copy(
                PADB[:, 0:192], QEF[:, 0:1].to_broadcast([128, 192])),
              reads=["qef"], writes=["padb"])
            E("vector", lambda v: v.tensor_copy(PADB[:, 192:321], QEF[:, 0:129]),
              reads=["qef"], writes=["padb"])
            E("vector", lambda v: v.tensor_copy(
                PADB[:, 321:512], QEF[:, 128:129].to_broadcast([128, 191])),
              reads=["qef"], writes=["padb"])
            E("sync", lambda g, lt=lt: g.dma_start(
                out=qepad[lt * 128:(lt + 1) * 128, :], in_=PADB[:, :]),
              reads=["padb"], writes=["d_qepad"], dma="pads")
            E("tensor", lambda t, lt=lt: t.matmul(
                PA[:, 0:129], K1[:, lt * 128: lt * 128 + 128], C["eqT_b"][:, :],
                start=True, stop=False),
              reads=["k1", "consts_b"], writes=["pA"])
            E("tensor", lambda t: t.matmul(
                PA[:, 0:129], ones[0:1, :], C["PPROW"][0:1, 0:129],
                start=False, stop=True),
              reads=["pprow", "consts"], writes=["pA"])
            E("vector", lambda v: v.tensor_copy(QEF[:, 0:129], PA[:, 0:129]),
              reads=["pA"], writes=["qef"])
            E("vector", lambda v: v.tensor_copy(
                UT[:, 0:192], QEF[:, 0:1].to_broadcast([128, 192])),
              reads=["qef"], writes=["ut0"])
            E("vector", lambda v: v.tensor_copy(UT[:, 192:321], QEF[:, 0:129]),
              reads=["qef"], writes=["ut0"])
            E("vector", lambda v: v.tensor_copy(
                UT[:, 321:512], QEF[:, 128:129].to_broadcast([128, 191])),
              reads=["qef"], writes=["ut0"])
            E("sync", lambda g, lt=lt: g.dma_start(
                out=kepad[lt * 128:(lt + 1) * 128, :], in_=UT[:, 0:512]),
              reads=["ut0"], writes=["d_kepad"], dma="pads")
        for g2 in range(TG):
            for mc in range(16):
                cls_all = []
                for lt in range(4 * g2, 4 * g2 + 4):
                    d = mc - lt
                    cls_all.append("B" if abs(d) <= 1 else ("L" if d < 0 else "H"))
                if all(cc == "L" for cc in cls_all):
                    outer = "L"
                elif all(cc == "H" for cc in cls_all):
                    outer = "H"
                else:
                    outer = None
                bk, pb = (("pC", PC) if mc % 2 == 0 else ("pD", PD))
                if outer is not None:
                    ai = (0, 2) if outer == "L" else (1, 3)
                    E("tensor", lambda t, mc=mc, g2=g2, pb=pb: t.matmul(
                        pb[:, 0:512], K1[:, mc * 128: mc * 128 + 128],
                        Q1[:, g2 * 512: g2 * 512 + 512],
                        start=True, stop=False),
                      reads=["q1", "k1"], writes=[bk])
                    E("tensor", lambda t, mc=mc, g2=g2, pb=pb, ai=ai: t.matmul(
                        pb[:, 0:512], ONER[0:1, mc * 128: mc * 128 + 128],
                        AUG[0:1, ai[0] * 2048 + g2 * 512: ai[0] * 2048 + g2 * 512 + 512],
                        start=False, stop=False),
                      reads=["oner", "aug"], writes=[bk])
                    E("tensor", lambda t, mc=mc, g2=g2, pb=pb, ai=ai: t.matmul(
                        pb[:, 0:512],
                        AUG[0:1, ai[1] * 2048 + mc * 128: ai[1] * 2048 + mc * 128 + 128],
                        ONER[0:1, g2 * 512: g2 * 512 + 512],
                        start=False, stop=True),
                      reads=["oner", "aug"], writes=[bk])
                    E("scalar", lambda s, mc=mc, pb=pb: s.activation(
                        out=PT[:, mc * 512: mc * 512 + 512], in_=pb[:, 0:512],
                        func=AF.Exp, scale=0.125),
                      reads=[bk], writes=[f"pt{mc}"])
                else:
                    for li, lt in enumerate(range(4 * g2, 4 * g2 + 4)):
                        cc = cls_all[li]
                        lo = li * 128
                        E("tensor", lambda t, mc=mc, lt=lt, lo=lo, pb=pb, cc=cc: t.matmul(
                            pb[:, lo: lo + 128], K1[:, mc * 128: mc * 128 + 128],
                            Q1[:, lt * 128: lt * 128 + 128],
                            start=True, stop=(cc == "B")),
                          reads=["q1", "k1"], writes=[bk])
                        if cc == "B":
                            pass
                        else:
                            ai = (0, 2) if cc == "L" else (1, 3)
                            E("tensor", lambda t, mc=mc, lt=lt, lo=lo, pb=pb, ai=ai: t.matmul(
                                pb[:, lo: lo + 128], ONER[0:1, mc * 128: mc * 128 + 128],
                                AUG[0:1, ai[0] * 2048 + lt * 128: ai[0] * 2048 + lt * 128 + 128],
                                start=False, stop=False),
                              reads=["oner", "aug"], writes=[bk])
                            E("tensor", lambda t, mc=mc, lt=lt, lo=lo, pb=pb, ai=ai: t.matmul(
                                pb[:, lo: lo + 128],
                                AUG[0:1, ai[1] * 2048 + mc * 128: ai[1] * 2048 + mc * 128 + 128],
                                ONER[0:1, lt * 128: lt * 128 + 128],
                                start=False, stop=True),
                              reads=["oner", "aug"], writes=[bk])
                    for li, lt in enumerate(range(4 * g2, 4 * g2 + 4)):
                        cc = cls_all[li]
                        lo = li * 128
                        if cc == "B":
                            ap2 = bass.AP(kepad.tensor, (mc * 128) * 511 + lt * 128 + 256,
                                          [[511, 128], [1, 128]])
                            kb2 = f"skb{mc % 2}"
                            E("sync", lambda g, ap2=ap2, mc=mc: g.dma_start(
                                out=C["SKF"][:, (mc % 2) * 128: (mc % 2) * 128 + 128], in_=ap2),
                              reads=["d_kepad"], writes=[kb2], dma=f"skf{mc % 2}")
                            E("vector", lambda v, mc=mc, lo=lo, pb=pb: v.tensor_add(
                                pb[:, lo: lo + 128], pb[:, lo: lo + 128],
                                C["SKF"][:, (mc % 2) * 128: (mc % 2) * 128 + 128]),
                              reads=[bk, kb2], writes=[bk])
                            ap1 = bass.AP(qepad.tensor, (lt * 128) * 511 + mc * 128 + 256,
                                          [[511, 128], [1, 128]])
                            kb1 = f"skt{mc % 2}"
                            E("sync", lambda g, ap1=ap1, mc=mc: g.dma_start(
                                out=C["SKB"][:, (mc % 2) * 128: (mc % 2) * 128 + 128], in_=ap1,
                                transpose=True),
                              reads=["d_qepad"], writes=[kb1], dma=f"skq{mc % 2}")
                            E("vector", lambda v, mc=mc, lo=lo, pb=pb: v.tensor_add(
                                pb[:, lo: lo + 128], pb[:, lo: lo + 128],
                                C["SKB"][:, (mc % 2) * 128: (mc % 2) * 128 + 128]),
                              reads=[bk, kb1], writes=[bk])
                    E("scalar", lambda s, mc=mc, pb=pb: s.activation(
                        out=PT[:, mc * 512: mc * 512 + 512], in_=pb[:, 0:512],
                        func=AF.Exp, scale=0.125),
                      reads=[bk], writes=[f"pt{mc}"])
            def emit_pv(mc_list, stop_last):
                for mc in mc_list:
                    for li, lt in enumerate(range(4 * g2, 4 * g2 + 4)):
                        d = mc - lt
                        cc = "B" if abs(d) <= 1 else ("L" if d < 0 else "H")
                        vt, vk = ((VT1, "vt1") if cc == "B" else
                                  (VTLO, "vtlo") if cc == "L" else (VTHI, "vthi"))
                        E("tensor", lambda t, mc=mc, li=li, vt=vt, st=(mc == 0 and li == 0),
                                    sp=(stop_last and mc == 15 and li == 3): t.matmul(
                            PO[0:65, li * 128: li * 128 + 128], vt[:, mc * 65: mc * 65 + 65],
                            PT[:, mc * 512 + li * 128: mc * 512 + li * 128 + 128],
                            start=st, stop=sp),
                          reads=[vk, f"pt{mc}"], writes=["pO"])
            emit_pv(range(15), False)
            for li, lt in enumerate(range(4 * g2, 4 * g2 + 4)):
                E("vector", lambda v: v.memset(C["PBTM"][:, :], 0.0), writes=["pbtm"])
                for bi, mc in enumerate((lt - 1, lt, lt + 1)):
                    if mc < 0 or mc > 15:
                        continue
                    E("tensor", lambda t, mc=mc, li=li: t.transpose(
                        PBB[:, 0:128], PT[:, mc * 512 + li * 128: mc * 512 + li * 128 + 128],
                        eye_b[:, :]),
                      reads=[f"pt{mc}", "consts_b"], writes=["pBB"])
                    E("vector", lambda v, bi=bi: v.tensor_copy(
                        C["PBTM"][:, bi * 128: bi * 128 + 128], PBB[:, 0:128]),
                      reads=["pBB"], writes=["pbtm"])
                E("sync", lambda g, lt=lt: g.dma_start(
                    out=pbd[lt * 128:(lt + 1) * 128, :], in_=C["PBTM"][:, :]),
                  reads=["pbtm"], writes=["d_pbd"], dma="pads")
                for jh in range(2):
                    apj = bass.AP(pbd.tensor, (lt * 128) * 384 + 1 + jh * 128,
                                  [[385, 128], [1, 128]])
                    E("sync", lambda g, apj=apj, jh=jh: g.dma_start(
                        out=C["SKB"][:, (2 + jh) * 128: (2 + jh) * 128 + 128], in_=apj,
                        transpose=True),
                      reads=["d_pbd"], writes=[f"skj{jh}"], dma=f"skj{jh}")
                E("tensor", lambda t, li=li: t.matmul(
                    PO[0:64, li * 128: li * 128 + 128], C["evb2a_b"][:, :],
                    C["SKB"][:, 2 * 128: 2 * 128 + 128],
                    start=False, stop=False),
                  reads=["skj0", "consts_b"], writes=["pO"])
                E("tensor", lambda t, li=li: t.matmul(
                    PO[0:64, li * 128: li * 128 + 128], C["evb2b_b"][:, :],
                    C["SKB"][:, 3 * 128: 3 * 128 + 128][0:127, :],
                    start=False, stop=False),
                  reads=["skj1", "consts_b"], writes=["pO"])
            emit_pv([15], True)
            E("vector", lambda v: v.reciprocal(U72[64:65, 0:512], PO[64:65, 0:512]),
              reads=["pO"], writes=["u72"])
            E("tensor", lambda t: t.matmul(
                PR[0:64, 0:512], ones[64:65, 0:64], U72[64:65, 0:512],
                start=True, stop=True),
              reads=["u72", "consts"], writes=["pR"])
            E("vector", lambda v: v.tensor_copy(UT[0:64, 0:512], PR[0:64, 0:512]),
              reads=["pR"], writes=["ut0"])
            E("vector", lambda v: v.tensor_mul(TMPB[:, 0:512], PO[0:64, 0:512], UT[0:64, 0:512]),
              reads=["pO", "ut0"], writes=["tmpb"])
            ch = h // 2
            if h % 2 == 0:
                E("vector", lambda v, ch=ch, g2=g2: v.tensor_add(
                    csl(X, ch, g2 * 512, g2 * 512 + 512)[0:64, :],
                    csl(X, ch, g2 * 512, g2 * 512 + 512)[0:64, :], TMPB[:, 0:512]),
                  reads=["tmpb", "x"], writes=["x"])
            else:
                E("tensor", lambda t: t.matmul(
                    PBk[64:128, 0:512], eye_b[0:64, 0:64], TMPB[:, 0:512],
                    start=True, stop=True),
                  reads=["tmpb", "consts_b"], writes=["pBk"])
                E("vector", lambda v, ch=ch, g2=g2: v.tensor_add(
                    csl(X, ch, g2 * 512, g2 * 512 + 512)[64:128, :],
                    csl(X, ch, g2 * 512, g2 * 512 + 512)[64:128, :], PBk[64:128, 0:512]),
                  reads=["pBk", "x"], writes=["x"])


def build_launch3(consts):
    nc = bass.Bass("TRN2", target_bir_lowering=False)
    x1 = nc.dram_tensor("x1", [D, T], F32, kind="ExternalInput").ap()
    wq1 = nc.dram_tensor("l1_wq", [D, D], F32, kind="ExternalInput").ap()
    wk1 = nc.dram_tensor("l1_wk", [D, D], F32, kind="ExternalInput").ap()
    wv1 = nc.dram_tensor("l1_wv", [D, D], F32, kind="ExternalInput").ap()
    eqT = nc.dram_tensor("l1_eqT", [64, 129], F32, kind="ExternalInput").ap()
    ekT = nc.dram_tensor("l1_ekT", [64, 129], F32, kind="ExternalInput").ap()
    ekTrev = nc.dram_tensor("l1_ekTrev", [64, 129], F32, kind="ExternalInput").ap()
    ev0 = nc.dram_tensor("l1_ev0", [1, 64], F32, kind="ExternalInput").ap()
    ev128 = nc.dram_tensor("l1_ev128", [1, 64], F32, kind="ExternalInput").ap()
    evb2a = nc.dram_tensor("l1_evb2a", [128, 64], F32, kind="ExternalInput").ap()
    evb2b = nc.dram_tensor("l1_evb2b", [127, 64], F32, kind="ExternalInput").ap()
    caps = make_const_inputs(nc, consts)

    x2p = nc.dram_tensor("x2p", [D, T], F32, kind="ExternalOutput").ap()
    xnb_out = nc.dram_tensor("xnb1_out", [D, T], BF16, kind="ExternalOutput").ap()

    d_qepad = nc.dram_tensor("d_qepad", [T, 512], BF16, kind="Internal").ap()
    d_kepad = nc.dram_tensor("d_kepad", [T, 512], F32, kind="Internal").ap()
    d_pbd = nc.dram_tensor("d_pbd", [T + 256, 384], BF16, kind="Internal").ap()

    import contextlib
    with contextlib.ExitStack() as ec:
        e = ec.enter_context
        X = e(nc.sbuf_tensor("X", [128, NCHUNK * T], F32))
        XNB = e(nc.sbuf_tensor("XNB", [128, NCHUNK * T], BF16))
        PT = e(nc.sbuf_tensor("PTS", [128, 16 * 512], BF16))
        Q1 = e(nc.sbuf_tensor("Q1", [64, T], BF16))
        K1 = e(nc.sbuf_tensor("K1", [64, T], BF16))
        V1 = e(nc.sbuf_tensor("V1", [64, T], BF16))
        VT1 = e(nc.sbuf_tensor("VT1", [128, 1040], BF16))
        VTLO = e(nc.sbuf_tensor("VTLO", [128, 1040], BF16))
        VTHI = e(nc.sbuf_tensor("VTHI", [128, 1040], BF16))
        WH = e(nc.sbuf_tensor("WH", [128, 3072], BF16))
        AUG = e(nc.sbuf_tensor("AUG", [1, 8192], BF16))
        ONER = e(nc.sbuf_tensor("ONER", [1, T], BF16))
        PADB = e(nc.sbuf_tensor("PADB", [128, 512], BF16))
        QEF = e(nc.sbuf_tensor("QEF", [128, 512], F32))
        EVBC = e(nc.sbuf_tensor("EVBC", [128, 128], BF16))
        TMPB = e(nc.sbuf_tensor("TMPB", [64, 512], BF16))
        SKB = e(nc.sbuf_tensor("SKB", [128, 512], BF16))
        SKF = e(nc.sbuf_tensor("SKF", [128, 256], F32))
        PBTM = e(nc.sbuf_tensor("PBTM", [128, 384], BF16))
        PPROW = e(nc.sbuf_tensor("PPROW", [1, 129], F32))
        UTIL = e(nc.sbuf_tensor("UTIL", [128, 1024], F32))
        U72 = e(nc.sbuf_tensor("U72", [72, 512], F32))
        STAT = e(nc.sbuf_tensor("STAT", [1, 2048], F32))
        ones_f32 = e(nc.sbuf_tensor("ones_f32", [128, 128], F32))
        ones_b_s = e(nc.sbuf_tensor("ones_b_s", [128, 128], BF16))
        eye_b_s = e(nc.sbuf_tensor("eye_b_s", [128, 128], BF16))
        scl_s = e(nc.sbuf_tensor("scl_s", [1, 16], F32))
        eqT_f = e(nc.sbuf_tensor("eqT_f", [64, 129], F32))
        ekT_f = e(nc.sbuf_tensor("ekT_f", [64, 129], F32))
        eqT_b = e(nc.sbuf_tensor("eqT_b", [64, 129], BF16))
        ekT_b = e(nc.sbuf_tensor("ekT_b", [64, 129], BF16))
        ekTrev_b = e(nc.sbuf_tensor("ekTrev_b", [64, 129], BF16))
        ev0_b = e(nc.sbuf_tensor("ev0_b", [1, 64], BF16))
        ev128_b = e(nc.sbuf_tensor("ev128_b", [1, 64], BF16))
        evb2a_b = e(nc.sbuf_tensor("evb2a_b", [128, 64], BF16))
        evb2b_b = e(nc.sbuf_tensor("evb2b_b", [127, 64], BF16))
        PA = e(nc.psum_tensor("PA", [128, 512], F32))
        PBk = e(nc.psum_tensor("PBk", [128, 512], F32))
        PC = e(nc.psum_tensor("PC", [128, 512], F32))
        PD = e(nc.psum_tensor("PD", [128, 512], F32))
        PO = e(nc.psum_tensor("PO", [128, 512], F32))
        PR = e(nc.psum_tensor("PR", [128, 512], F32))
        PBB = e(nc.psum_tensor("PBB", [128, 1024], BF16))

        kb = KB(nc)
        E = kb.emit
        C = dict(
            UTIL=UTIL, U72=U72, STAT=STAT, ones_f32=ones_f32, ones_b_s=ones_b_s,
            eye_b_s=eye_b_s, scl_s=scl_s,
            Q1=Q1, K1=K1, V1=V1, VT1=VT1, VTLO=VTLO, VTHI=VTHI, PT=PT, WH=WH,
            AUG=AUG, ONER=ONER, PADB=PADB, QEF=QEF, EVBC=EVBC, TMPB=TMPB,
            SKB=SKB, SKF=SKF, PBTM=PBTM, PPROW=PPROW,
            eqT_f=eqT_f, ekT_f=ekT_f, eqT_b=eqT_b, ekT_b=ekT_b, ekTrev_b=ekTrev_b,
            ev0_b=ev0_b, ev128_b=ev128_b, evb2a_b=evb2a_b, evb2b_b=evb2b_b,
            PA=PA, PBk=PBk, PC=PC, PD=PD, PO=PO, PR=PR, PBB=PBB,
            d_qepad=d_qepad, d_kepad=d_kepad, d_pbd=d_pbd,
            in_wq1=wq1, in_wk1=wk1, in_wv1=wv1,
        )

        for c in range(NCHUNK):
            E("sync", lambda g, c=c: g.dma_start(
                out=X[:, c * T:(c + 1) * T], in_=x1[c * 128:(c + 1) * 128, :]),
              writes=["x"], dma="x")
        E("sync", lambda g: g.dma_start(out=ones_f32[:], in_=caps["ones_f"]),
          writes=["consts"], dma="c")
        E("sync", lambda g: g.dma_start(out=scl_s[:], in_=caps["scl"]),
          writes=["consts"], dma="c")
        E("sync", lambda g: g.dma_start(out=eqT_f[:], in_=eqT), writes=["consts"], dma="c")
        E("sync", lambda g: g.dma_start(out=ekT_f[:], in_=ekT), writes=["consts"], dma="c")
        for sb, srcd in ((ones_b_s, caps["ones_b"]), (eye_b_s, caps["eye_b"])):
            E("sync", lambda g, sb=sb, srcd=srcd: g.dma_start(out=sb[:], in_=srcd),
              writes=["consts_b"], dma="c")
        for sb, srcd in ((eqT_b, eqT), (ekT_b, ekT), (ekTrev_b, ekTrev),
                         (ev0_b, ev0), (ev128_b, ev128), (evb2a_b, evb2a), (evb2b_b, evb2b)):
            E("gpsimd", lambda g, sb=sb, srcd=srcd: g.dma_start(out=sb[:], in_=srcd),
              writes=["consts_b"], dma="cb")

        emit_ln(kb, C, "x", "xnb", X, XNB)
        emit_l1a(kb, C, XNB, X)

        for c in range(NCHUNK):
            E("sync", lambda g, c=c: g.dma_start(
                out=x2p[c * 128:(c + 1) * 128, :], in_=X[:, c * T:(c + 1) * T]),
              reads=["x"], dma="st", final=True)
            E("sync", lambda g, c=c: g.dma_start(
                out=xnb_out[c * 128:(c + 1) * 128, :], in_=XNB[:, c * T:(c + 1) * T]),
              reads=["xnb"], dma="st", final=True)
        kb.build()
    return nc


def emit_l1f(kb, C, XI):
    import os
    FST = os.environ.get("F_STAGES", "av")
    FSUB = os.environ.get("F_SUB", "qaw")
    NL = int(os.environ.get("F_NL", "2"))
    E = kb.emit
    PA, PBk, PC, PD, PO, PR, PBB = (C["PA"], C["PBk"], C["PC"], C["PD"],
                                    C["PO"], C["PR"], C["PBB"])
    UT = C["UTIL"]
    HBS = C["HBS"]
    H2S = C["H2S"]
    QBc, KBc, VBc = C["QBc"], C["KBc"], C["VBc"]
    VT1 = C["VT1"]
    PT = C["PT"]
    AOC = C["AOC"]
    UU = C["UU"]
    WQ3 = C["WQ3"]
    WOS = C["WOS"]
    W1S = C["W1S"]
    W2F = C["W2F"]
    GRIDS = C["GRIDS"]
    TMPB = C["TMPB"]
    U72 = C["U72"]
    eye_b = C["eye_b_s"]
    ones = C["ones_f32"]
    FMA = C["FMA"]
    FMM = C["FMM"]
    blockmap = C["blockmap_s"]
    prothalf = C["prothalf_s"]

    def shift_tile(dst, c, tg, dkey):
        base = c * T + tg * 512
        if c < 4:
            E("vector", lambda v, dst=dst, base=base: v.tensor_copy(
                dst, XI[:, base: base + 512]), reads=["xi"], writes=[dkey])
        elif tg == 0:
            E("vector", lambda v, dst=dst, base=base: v.tensor_copy(
                dst[:, 1:512], XI[:, base: base + 511]), reads=["xi"], writes=[dkey])
            E("vector", lambda v, dst=dst: v.memset(dst[:, 0:1], 0.0), writes=[dkey])
        else:
            E("vector", lambda v, dst=dst, base=base: v.tensor_copy(
                dst, XI[:, base - 1: base + 511]), reads=["xi"], writes=[dkey])

    for l in range(NL):
        lw = C[f"fw{l}"]
        for nm, sb in (("b1a", C["B1A"]), ("b1g", C["B1G"]), ("b2", C["B2C"]),
                       ("lng", C["LNG"]), ("lnb", C["LNB"]), ("qs", C["QSC"]), ("ks", C["KSC"])):
            E("sync", lambda g, nm=nm, sb=sb, lw=lw: g.dma_start(out=sb[:, :], in_=lw[nm]),
              writes=["lcols"], dma="lc")
        E("sync", lambda g, lw=lw: g.dma_start(out=C["QROW"][0:1, 0:64], in_=lw["qsr"]),
          writes=["qrow"], dma="lc")
        E("sync", lambda g, lw=lw: g.dma_start(out=C["QROW"][0:1, 64:128], in_=lw["ksr"]),
          writes=["qrow"], dma="lc")
        E("vector", lambda v: v.reduce_max(UT[0:1, 0:1], C["QROW"][0:1, 0:64],
                                           axis=mybir.AxisListType.X),
          reads=["qrow"], writes=["ut0"])
        E("vector", lambda v: v.reduce_max(UT[0:1, 1:2], C["QROW"][0:1, 64:128],
                                           axis=mybir.AxisListType.X),
          reads=["qrow"], writes=["ut0"])
        E("vector", lambda v: v.tensor_mul(UT[0:1, 2:3], UT[0:1, 0:1], UT[0:1, 1:2]),
          reads=["ut0"], writes=["ut0"])
        E("vector", lambda v: v.tensor_scalar(out=UT[0:1, 3:4], in0=UT[0:1, 2:3],
                                              scalar1=-8.0, scalar2=None, op0=ALU.mult),
          reads=["ut0"], writes=["ut0"])
        E("tensor", lambda t: t.matmul(PR[:, 0:1], ones[0:1, :], UT[0:1, 3:4],
                                       start=True, stop=True),
          reads=["ut0", "consts"], writes=["pR"])
        E("vector", lambda v: v.tensor_copy(C["NEGC"][:, 0:1], PR[:, 0:1]),
          reads=["pR"], writes=["negcc"])
        E("gpsimd", lambda g, lw=lw: g.dma_start(
            out=W2F[:, :].rearrange("p (a o) -> p a o", a=22),
            in_=lw["w2"].rearrange("(a p) o -> p a o", p=128)),
          writes=["w2f"], dma="wlw")

        for fc in range(4 if "a" in FST else 0):
            ws = fc % 2
            for pi, nm in ((0, "wq"), (1, "wk"), (2, "wv")):
                E("gpsimd", lambda g, pi=pi, nm=nm, fc=fc, ws=ws, lw=lw: g.dma_start(
                    out=WQ3[:, ws * 3072 + pi * 1024: ws * 3072 + pi * 1024 + 1024]
                        .rearrange("p (a o) -> p a o", a=8),
                    in_=lw[nm].rearrange("(a p) o -> p a o", p=128)
                        [:, :, fc * 128: fc * 128 + 128]),
                  writes=[f"wq3{ws}{pi}"], dma=f"wlq{ws}")
            E("gpsimd", lambda g, fc=fc, lw=lw: g.dma_start(
                out=WOS[:, :], in_=lw["wo"][fc * 128: fc * 128 + 128, :]),
              writes=["wos"], dma="wo")
            for tg in range(TG):
                for c in range(NCHUNK):
                    shift_tile(HBS[:, c * 512:(c + 1) * 512], c, tg, f"hbs{c}")
                for qk in range(2):
                    E("sync", lambda g, qk=qk, tg=tg: g.dma_start(
                        out=GRIDS[:, 0:512],
                        in_=(C["c_xposq"] if qk == 0 else C["c_xposk"])[:, tg * 512: tg * 512 + 512]),
                      writes=["grid"], dma="gr")
                    E("sync", lambda g, qk=qk, tg=tg: g.dma_start(
                        out=GRIDS[:, 512:1024],
                        in_=(C["c_xposq"] if qk == 0 else C["c_xposk"])[:, 2048 + tg * 512: 2048 + tg * 512 + 512]),
                      writes=["grid"], dma="gr")
                    SCL = C["QSC"] if qk == 0 else C["KSC"]
                    DSTB = QBc if qk == 0 else KBc
                    dkey = "qbc" if qk == 0 else "kbc"
                    bk, pb = (("pC", PC) if qk == 0 else ("pD", PD))
                    for c in range(NCHUNK):
                        E("tensor", lambda t, qk=qk, c=c, ws=ws, pb=pb: t.matmul(
                            pb[:, 0:512],
                            WQ3[:, ws * 3072 + qk * 1024 + c * 128: ws * 3072 + qk * 1024 + c * 128 + 128],
                            HBS[:, c * 512:(c + 1) * 512],
                            start=(c == 0), stop=(c == NCHUNK - 1)),
                          reads=[f"wq3{ws}{qk}", f"hbs{c}"], writes=[bk])
                    if "m" == os.environ.get("F_CHAIN", ""):
                        E("vector", lambda v, tg=tg, pb=pb, DSTB=DSTB, dkey=dkey: v.tensor_copy(
                            DSTB[:, tg * 512: tg * 512 + 512], pb[:, 0:512]),
                          reads=[bk], writes=[dkey])
                        continue
                    FCH = os.environ.get("F_CHAIN", "")
                    sq = UT[:, 0:512]
                    E("scalar", lambda s, pb=pb, sq=sq: s.activation(
                        out=sq, in_=pb[:, 0:512], func=AF.Square),
                      reads=[bk], writes=["ut0"])
                    qt = UT[:, 512:1024]
                    E("vector", lambda v, pb=pb, qt=qt: v.tensor_copy(qt, pb[:, 0:512]),
                      reads=[bk], writes=["ut1"])
                    if FCH == "1":
                        E("vector", lambda v, tg=tg, qt=qt, DSTB=DSTB, dkey=dkey: v.tensor_copy(
                            DSTB[:, tg * 512: tg * 512 + 512], qt),
                          reads=["ut1", "ut0"], writes=[dkey])
                        continue
                    E("tensor", lambda t, sq=sq: t.matmul(
                        PA[:, 0:512], blockmap[:, :], sq, start=True, stop=True),
                      reads=["ut0", "consts"], writes=["pA"])
                    rn = H2S[:, 0:512]
                    E("vector", lambda v, rn=rn: v.tensor_scalar(
                        out=rn, in0=PA[:, 0:512], scalar1=1e-24, scalar2=None, op0=ALU.max),
                      reads=["pA"], writes=["h2s0"])
                    if FCH == "2":
                        E("vector", lambda v, tg=tg, rn=rn, DSTB=DSTB, dkey=dkey: v.tensor_copy(
                            DSTB[:, tg * 512: tg * 512 + 512], rn),
                          reads=["h2s0", "ut1"], writes=[dkey])
                        continue
                    E("scalar", lambda s, rn=rn: s.activation(out=rn, in_=rn, func=AF.Ln),
                      reads=["h2s0"], writes=["h2s0"])
                    E("scalar", lambda s, rn=rn: s.activation(out=rn, in_=rn, func=AF.Exp,
                                                              scale=-0.5),
                      reads=["h2s0"], writes=["h2s0"])
                    E("vector", lambda v, rn=rn, SCL=SCL: v.tensor_scalar(
                        out=rn, in0=rn, scalar1=SCL[:, 0:1], scalar2=None, op0=ALU.mult),
                      reads=["h2s0", "lcols"], writes=["h2s0"])
                    E("vector", lambda v, qt=qt, rn=rn: v.tensor_mul(qt, qt, rn),
                      reads=["ut1", "h2s0"], writes=["ut1"])
                    if "n" == os.environ.get("F_CHAIN", ""):
                        E("vector", lambda v, tg=tg, qt=qt, DSTB=DSTB, dkey=dkey: v.tensor_copy(
                            DSTB[:, tg * 512: tg * 512 + 512], qt),
                          reads=["ut1"], writes=[dkey])
                        continue
                    E("tensor", lambda t, qt=qt: t.matmul(
                        PA[:, 0:512], prothalf[:, :], qt, start=True, stop=True),
                      reads=["ut1", "consts"], writes=["pA"])
                    tm2 = H2S[:, 512:1024]
                    E("vector", lambda v, tm2=tm2: v.tensor_mul(
                        tm2, PA[:, 0:512], GRIDS[:, 512:1024]),
                      reads=["pA", "grid"], writes=["h2s1"])
                    E("vector", lambda v, qt=qt: v.tensor_mul(qt, qt, GRIDS[:, 0:512]),
                      reads=["ut1", "grid"], writes=["ut1"])
                    E("vector", lambda v, tg=tg, qt=qt, tm2=tm2, DSTB=DSTB, dkey=dkey: v.tensor_add(
                        DSTB[:, tg * 512: tg * 512 + 512], qt, tm2),
                      reads=["ut1", "h2s1"], writes=[dkey])
                bk, pb = ("pC", PC)
                for c in range(NCHUNK):
                    E("tensor", lambda t, c=c, ws=ws, pb=pb: t.matmul(
                        pb[:, 0:512],
                        WQ3[:, ws * 3072 + 2 * 1024 + c * 128: ws * 3072 + 2 * 1024 + c * 128 + 128],
                        HBS[:, c * 512:(c + 1) * 512],
                        start=(c == 0), stop=(c == NCHUNK - 1)),
                      reads=[f"wq3{ws}2", f"hbs{c}"], writes=[bk])
                E("vector", lambda v, tg=tg, pb=pb: v.tensor_copy(
                    VBc[:, tg * 512: tg * 512 + 512], pb[:, 0:512]),
                  reads=[bk], writes=["vbc"])
            for hh in range(2 if "a" in FSUB else 0):
                hb = 64 * hh
                E("vector", lambda v: v.memset(VT1[:], 1.0), writes=["vt1"])
                for mc in range(16):
                    E("tensor", lambda t, hb=hb, mc=mc: t.transpose(
                        PBB[:, 0:64], VBc[hb:hb + 64, mc * 128: mc * 128 + 128],
                        eye_b[hb:hb + 64, hb:hb + 64]),
                      reads=["vbc", "consts_b"], writes=["pBB"])
                    E("vector", lambda v, mc=mc: v.tensor_copy(
                        VT1[:, mc * 65: mc * 65 + 64], PBB[:, 0:64]),
                      reads=["pBB"], writes=["vt1"])
                for w in range(4):
                    a0 = 0 if w > 0 else 4
                    for ai, a in enumerate(range(a0, 8)):
                        mc = 4 * (w - 1) + a
                        bk, pb = (("pC", PC) if a % 2 == 0 else ("pD", PD))
                        sl = (a % 2) * 512
                        E("tensor", lambda t, hb=hb, mc=mc, w=w, pb=pb: t.matmul(
                            pb[:, 0:512],
                            KBc[hb:hb + 64, mc * 128: mc * 128 + 128],
                            QBc[hb:hb + 64, w * 512: w * 512 + 512],
                            start=True, stop=True),
                          reads=["kbc", "qbc"], writes=[bk])
                        if a >= 5:
                            E("vector", lambda v, a=a, pb=pb: v.tensor_add(
                                pb[:, 0:512], pb[:, 0:512],
                                FMA[:, (a - 5) * 512: (a - 5) * 512 + 512]),
                              reads=[bk, "consts_b"], writes=[bk])
                        E("scalar", lambda s, sl=sl, pb=pb: s.activation(
                            out=PT[:, sl: sl + 512], in_=pb[:, 0:512],
                            func=AF.Exp, scale=8.0, bias=C["NEGC"][:, 0:1]),
                          reads=[bk, "negcc"], writes=[f"pt{sl}"])
                        if a <= 4:
                            E("vector", lambda v, a=a, sl=sl: v.tensor_mul(
                                PT[:, sl: sl + 512], PT[:, sl: sl + 512],
                                FMM[:, a * 512: a * 512 + 512]),
                              reads=[f"pt{sl}", "consts_b"], writes=[f"pt{sl}"])
                        E("tensor", lambda t, sl=sl, mc=mc, st=(ai == 0), sp=(a == 7): t.matmul(
                            PO[0:65, 0:512], VT1[:, mc * 65: mc * 65 + 65],
                            PT[:, sl: sl + 512],
                            start=st, stop=sp),
                          reads=["vt1", f"pt{sl}"], writes=["pO"])
                    E("vector", lambda v: v.reciprocal(U72[64:65, 0:512], PO[64:65, 0:512]),
                      reads=["pO"], writes=["u72"])
                    E("tensor", lambda t: t.matmul(
                        PR[0:64, 0:512], ones[64:65, 0:64], U72[64:65, 0:512],
                        start=True, stop=True),
                      reads=["u72", "consts"], writes=["pR"])
                    E("vector", lambda v: v.tensor_copy(UT[0:64, 0:512], PR[0:64, 0:512]),
                      reads=["pR"], writes=["ut0"])
                    E("vector", lambda v: v.tensor_mul(TMPB[:, 0:512], PO[0:64, 0:512],
                                                       UT[0:64, 0:512]),
                      reads=["pO", "ut0"], writes=["tmpb"])
                    if hh == 0:
                        E("vector", lambda v, w=w: v.tensor_copy(
                            AOC[0:64, w * 512: w * 512 + 512], TMPB[:, 0:512]),
                          reads=["tmpb"], writes=["aoc"])
                    else:
                        E("tensor", lambda t: t.matmul(
                            PBk[64:128, 0:512], eye_b[0:64, 0:64], TMPB[:, 0:512],
                            start=True, stop=True),
                          reads=["tmpb", "consts_b"], writes=["pBk"])
                        E("vector", lambda v, w=w: v.tensor_copy(
                            AOC[64:128, w * 512: w * 512 + 512], PBk[64:128, 0:512]),
                          reads=["pBk"], writes=["aoc"])
            for og in range(NCHUNK if "w" in FSUB else 0):
                for tg in range(TG):
                    bk, pb = (("pC", PC) if tg % 2 == 0 else ("pD", PD))
                    E("tensor", lambda t, og=og, tg=tg, pb=pb: t.matmul(
                        pb[:, 0:512], WOS[:, og * 128: og * 128 + 128],
                        AOC[:, tg * 512: tg * 512 + 512],
                        start=True, stop=True),
                      reads=["wos", "aoc"], writes=[bk])
                    E("vector", lambda v, og=og, tg=tg, pb=pb, bk=bk: v.tensor_add(
                        csl(XI, og, tg * 512, tg * 512 + 512), pb[:, 0:512],
                        csl(XI, og, tg * 512, tg * 512 + 512)),
                      reads=[bk, "xi"], writes=["xi"])
        for tg in range(TG if "v" in FST else 0):
            for c in range(NCHUNK):
                sl = H2S[:, (c % 2) * 512: (c % 2) * 512 + 512]
                shift_tile(sl, c, tg, f"h2s{c % 2}")
                E("tensor", lambda t, c=c, sl=sl: t.matmul(
                    PA[0:1, 0:512], ones[:, 0:1], sl,
                    start=(c == 0), stop=(c == NCHUNK - 1)),
                  reads=[f"h2s{c % 2}", "consts"], writes=["pA"])
                sq = UT[:, 0:512]
                E("scalar", lambda s, sl=sl, sq=sq: s.activation(out=sq, in_=sl, func=AF.Square),
                  reads=[f"h2s{c % 2}"], writes=["ut0"])
                E("tensor", lambda t, c=c, sq=sq: t.matmul(
                    PBk[0:1, 0:512], ones[:, 0:1], sq,
                    start=(c == 0), stop=(c == NCHUNK - 1)),
                  reads=["ut0", "consts"], writes=["pB"])
            mu = UT[0:1, 0:512]
            va = UT[0:1, 512:1024]
            E("scalar", lambda s, mu=mu: s.mul(mu, PA[0:1, 0:512], 1.0 / D),
              reads=["pA"], writes=["ut0"])
            E("scalar", lambda s, va=va: s.mul(va, PBk[0:1, 0:512], 1.0 / D),
              reads=["pB"], writes=["ut1"])
            E("vector", lambda v, mu=mu, va=va: v.scalar_tensor_tensor(
                out=va, in0=mu, scalar=0.0, in1=va, op0=ALU.mult, op1=ALU.bypass)
              if False else
              v.tensor_copy(H2S[0:1, 0:512], mu),
              reads=["ut0"], writes=["h2s0"])
            E("scalar", lambda s: s.activation(out=H2S[0:1, 512:1024], in_=H2S[0:1, 0:512],
                                               func=AF.Square),
              reads=["h2s0"], writes=["h2s1"])
            E("vector", lambda v, va=va: v.tensor_sub(va, va, H2S[0:1, 512:1024]),
              reads=["ut1", "h2s1"], writes=["ut1"])
            E("scalar", lambda s, va=va: s.activation(out=va, in_=va, func=AF.Ln,
                                                      bias=C["scl_s"][0:1, 0:1]),
              reads=["ut1", "consts"], writes=["ut1"])
            E("scalar", lambda s, va=va: s.activation(out=va, in_=va, func=AF.Exp, scale=-0.5),
              reads=["ut1"], writes=["ut1"])
            E("tensor", lambda t, mu=mu: t.matmul(PC[:, 0:512], ones[0:1, :], mu,
                                                  start=True, stop=True),
              reads=["ut0", "consts"], writes=["pC"])
            E("tensor", lambda t, va=va: t.matmul(PD[:, 0:512], ones[0:1, :], va,
                                                  start=True, stop=True),
              reads=["ut1", "consts"], writes=["pD"])
            for c in range(NCHUNK):
                sl = H2S[:, (c % 2) * 512: (c % 2) * 512 + 512]
                shift_tile(sl, c, tg, f"h2s{c % 2}")
                E("vector", lambda v, sl=sl: v.tensor_sub(sl, sl, PC[:, 0:512]),
                  reads=[f"h2s{c % 2}", "pC"], writes=[f"h2s{c % 2}"])
                E("vector", lambda v, sl=sl: v.tensor_mul(sl, sl, PD[:, 0:512]),
                  reads=[f"h2s{c % 2}", "pD"], writes=[f"h2s{c % 2}"])
                E("scalar", lambda s, c=c, sl=sl: s.activation(
                    out=HBS[:, c * 512:(c + 1) * 512], in_=sl,
                    func=AF.Identity, scale=C["LNG"][:, c:c + 1], bias=C["LNB"][:, c:c + 1]),
                  reads=[f"h2s{c % 2}", "lcols"], writes=[f"hbs{c}"])
            for hc in range(22):
                ws = hc % 2
                for sb, nm, kk in ((0, "w1a", "w1a"), (1, "w1g", "w1g")):
                    E("gpsimd", lambda g, hc=hc, ws=ws, sb=sb, nm=nm, lw=lw: g.dma_start(
                        out=W1S[:, (ws * 2 + sb) * 1024: (ws * 2 + sb) * 1024 + 1024]
                            .rearrange("p (a o) -> p a o", a=8),
                        in_=lw[nm].rearrange("(a p) o -> p a o", p=128)
                            [:, :, hc * 128: hc * 128 + 128]),
                      writes=[f"{kk}{ws}"], dma=f"wf{ws}")
                for c in range(NCHUNK):
                    E("tensor", lambda t, ws=ws, c=c: t.matmul(
                        PA[:, 0:512],
                        W1S[:, (ws * 2) * 1024 + c * 128: (ws * 2) * 1024 + c * 128 + 128],
                        HBS[:, c * 512:(c + 1) * 512],
                        start=(c == 0), stop=(c == NCHUNK - 1)),
                      reads=[f"w1a{ws}", f"hbs{c}"], writes=["pA"])
                for c in range(NCHUNK):
                    E("tensor", lambda t, ws=ws, c=c: t.matmul(
                        PBk[:, 0:512],
                        W1S[:, (ws * 2 + 1) * 1024 + c * 128: (ws * 2 + 1) * 1024 + c * 128 + 128],
                        HBS[:, c * 512:(c + 1) * 512],
                        start=(c == 0), stop=(c == NCHUNK - 1)),
                      reads=[f"w1g{ws}", f"hbs{c}"], writes=["pB"])
                gel = UT[:, 0:512]
                E("scalar", lambda s, hc=hc, gel=gel: s.activation(
                    out=gel, in_=PBk[:, 0:512], func=AF.Gelu, bias=C["B1G"][:, hc:hc + 1]),
                  reads=["pB", "lcols"], writes=["ut0"])
                E("vector", lambda v, hc=hc, gel=gel: v.scalar_tensor_tensor(
                    out=UU[:, hc * 512: hc * 512 + 512], in0=PA[:, 0:512],
                    scalar=C["B1A"][:, hc:hc + 1], in1=gel,
                    op0=ALU.add, op1=ALU.mult),
                  reads=["pA", "lcols", "ut0"], writes=[f"uu{hc}"])
            for og in range(NCHUNK):
                bk, pb = (("pC", PC) if og % 2 == 0 else ("pD", PD))
                for hc in range(22):
                    E("tensor", lambda t, og=og, hc=hc, pb=pb: t.matmul(
                        pb[:, 0:512],
                        W2F[:, hc * 1024 + og * 128: hc * 1024 + og * 128 + 128],
                        UU[:, hc * 512: hc * 512 + 512],
                        start=(hc == 0), stop=(hc == 21)),
                      reads=["w2f", f"uu{hc}"], writes=[bk])
                E("vector", lambda v, og=og, tg=tg, pb=pb, bk=bk: v.scalar_tensor_tensor(
                    out=csl(XI, og, tg * 512, tg * 512 + 512),
                    in0=pb[:, 0:512], scalar=C["B2C"][:, og:og + 1],
                    in1=csl(XI, og, tg * 512, tg * 512 + 512),
                    op0=ALU.add, op1=ALU.add),
                  reads=[bk, "lcols", "xi"], writes=["xi"])


def build_launch4(consts):
    nc = bass.Bass("TRN2", target_bir_lowering=False)
    x1 = nc.dram_tensor("x1", [D, T], F32, kind="ExternalInput").ap()
    x2p = nc.dram_tensor("x2p", [D, T], F32, kind="ExternalInput").ap()
    fwin = {}
    for l in range(2):
        for nm, shp in (("wq", [D, 512]), ("wk", [D, 512]), ("wv", [D, 512]),
                        ("wo", [512, D]), ("w1a", [D, 2816]), ("w1g", [D, 2816]),
                        ("w2", [2816, D]), ("b1a", [128, 22]), ("b1g", [128, 22]),
                        ("b2", [128, 8]), ("lng", [128, 8]), ("lnb", [128, 8]),
                        ("qs", [128, 1]), ("ks", [128, 1]),
                        ("qsr", [1, 64]), ("ksr", [1, 64])):
            fwin.setdefault(l, {})[nm] = nc.dram_tensor(
                f"f_{nm}{l}", shp, F32, kind="ExternalInput").ap()
    caps = make_const_inputs(nc, consts)

    out = nc.dram_tensor("out_final", [D, T], F32, kind="ExternalOutput").ap()
    import os as _os
    dbg = {}
    if _os.environ.get("F_DBG"):
        for nm in ("QBc", "KBc", "VBc", "AOC"):
            dbg[nm] = nc.dram_tensor("dbg_" + nm, [128, T], BF16, kind="ExternalOutput").ap()

    import contextlib
    with contextlib.ExitStack() as ec:
        e = ec.enter_context
        X = e(nc.sbuf_tensor("X", [128, NCHUNK * T], F32))
        HBS = e(nc.sbuf_tensor("HBS", [128, 8 * 512], BF16))
        H2S = e(nc.sbuf_tensor("H2S", [128, 2 * 512], F32))
        QBc = e(nc.sbuf_tensor("QBc", [128, T], BF16))
        KBc = e(nc.sbuf_tensor("KBc", [128, T], BF16))
        VBc = e(nc.sbuf_tensor("VBc", [128, T], BF16))
        VT1 = e(nc.sbuf_tensor("VT1", [128, 1040], BF16))
        PT = e(nc.sbuf_tensor("PTS", [128, 2 * 512], BF16))
        AOC = e(nc.sbuf_tensor("AOC", [128, T], BF16))
        UU = e(nc.sbuf_tensor("UU", [128, 22 * 512], BF16))
        WQ3 = e(nc.sbuf_tensor("WQ3", [128, 2 * 3072], BF16))
        WOS = e(nc.sbuf_tensor("WOS", [128, 1024], BF16))
        W1S = e(nc.sbuf_tensor("W1S", [128, 4096], BF16))
        W2F = e(nc.sbuf_tensor("W2F", [128, 22 * 1024], BF16))
        GRIDS = e(nc.sbuf_tensor("GRIDS", [128, 1024], F32))
        TMPB = e(nc.sbuf_tensor("TMPB", [64, 512], BF16))
        FMA = e(nc.sbuf_tensor("FMA", [128, 3 * 512], BF16))
        FMM = e(nc.sbuf_tensor("FMM", [128, 5 * 512], BF16))
        UTIL = e(nc.sbuf_tensor("UTIL", [128, 1024], F32))
        U72 = e(nc.sbuf_tensor("U72", [72, 512], F32))
        QROW = e(nc.sbuf_tensor("QROW", [1, 128], F32))
        NEGC = e(nc.sbuf_tensor("NEGC", [128, 1], F32))
        B1A = e(nc.sbuf_tensor("B1A", [128, 22], F32))
        B1G = e(nc.sbuf_tensor("B1G", [128, 22], F32))
        B2C = e(nc.sbuf_tensor("B2C", [128, 8], F32))
        LNG = e(nc.sbuf_tensor("LNG", [128, 8], F32))
        LNB = e(nc.sbuf_tensor("LNB", [128, 8], F32))
        QSC = e(nc.sbuf_tensor("QSC", [128, 1], F32))
        KSC = e(nc.sbuf_tensor("KSC", [128, 1], F32))
        STG = e(nc.sbuf_tensor("STG", [128, 512], F32))
        ones_f32 = e(nc.sbuf_tensor("ones_f32", [128, 128], F32))
        eye_b_s = e(nc.sbuf_tensor("eye_b_s", [128, 128], BF16))
        scl_s = e(nc.sbuf_tensor("scl_s", [1, 16], F32))
        blockmap_s = e(nc.sbuf_tensor("blockmap_s", [128, 128], F32))
        prothalf_s = e(nc.sbuf_tensor("prothalf_s", [128, 128], F32))
        PA = e(nc.psum_tensor("PA", [128, 512], F32))
        PBk = e(nc.psum_tensor("PBk", [128, 512], F32))
        PC = e(nc.psum_tensor("PC", [128, 512], F32))
        PD = e(nc.psum_tensor("PD", [128, 512], F32))
        PO = e(nc.psum_tensor("PO", [128, 512], F32))
        PR = e(nc.psum_tensor("PR", [128, 512], F32))
        PBB = e(nc.psum_tensor("PBB", [128, 1024], BF16))

        kb = KB(nc)
        E = kb.emit
        C = dict(
            UTIL=UTIL, U72=U72, ones_f32=ones_f32, eye_b_s=eye_b_s, scl_s=scl_s,
            blockmap_s=blockmap_s, prothalf_s=prothalf_s,
            HBS=HBS, H2S=H2S, QBc=QBc, KBc=KBc, VBc=VBc, VT1=VT1, PT=PT,
            AOC=AOC, UU=UU, WQ3=WQ3, WOS=WOS, W1S=W1S, W2F=W2F, GRIDS=GRIDS,
            TMPB=TMPB, FMA=FMA, FMM=FMM, QROW=QROW, NEGC=NEGC,
            B1A=B1A, B1G=B1G, B2C=B2C, LNG=LNG, LNB=LNB, QSC=QSC, KSC=KSC,
            PA=PA, PBk=PBk, PC=PC, PD=PD, PO=PO, PR=PR, PBB=PBB,
            c_xposq=caps["xposq"], c_xposk=caps["xposk"],
            fw0=fwin[0], fw1=fwin[1],
            STAT=None,
        )

        for c in range(NCHUNK):
            E("sync", lambda g, c=c: g.dma_start(
                out=X[:, c * T:(c + 1) * T], in_=x1[c * 128:(c + 1) * 128, :]),
              writes=["xi"], dma="x")
        E("sync", lambda g: g.dma_start(out=ones_f32[:], in_=caps["ones_f"]),
          writes=["consts"], dma="c")
        E("sync", lambda g: g.dma_start(out=scl_s[:], in_=caps["scl"]),
          writes=["consts"], dma="c")
        E("sync", lambda g: g.dma_start(out=blockmap_s[:], in_=caps["blockmap"]),
          writes=["consts"], dma="c")
        E("sync", lambda g: g.dma_start(out=prothalf_s[:], in_=caps["prothalf"]),
          writes=["consts"], dma="c")
        E("sync", lambda g: g.dma_start(out=eye_b_s[:], in_=caps["eye_b"]),
          writes=["consts_b"], dma="c")
        E("sync", lambda g: g.dma_start(out=FMM[:], in_=caps["fmaskmul"][:, 0:5 * 512]),
          writes=["consts_b"], dma="c")
        E("gpsimd", lambda g: g.dma_start(out=FMA[:], in_=caps["fmaskadd"][:, 5 * 512:8 * 512]),
          writes=["consts_b"], dma="cb")

        emit_ln_fp32_inplace(kb, C, X)
        emit_l1f(kb, C, X)

        if dbg:
            for nm, sb, kk2 in (("QBc", QBc, "qbc"), ("KBc", KBc, "kbc"),
                                ("VBc", VBc, "vbc"), ("AOC", AOC, "aoc")):
                E("sync", lambda g, nm=nm, sb=sb: g.dma_start(out=dbg[nm], in_=sb[:, :]),
                  reads=[kk2], dma="st", final=True)
        for c in range(NCHUNK):
            for tg in range(TG):
                E("sync", lambda g, c=c, tg=tg: g.dma_start(
                    out=STG[:, :], in_=x2p[c * 128:(c + 1) * 128, tg * 512:(tg + 1) * 512]),
                  writes=["stg"], dma="x2")
                E("vector", lambda v, c=c, tg=tg: v.tensor_add(
                    STG[:, :], STG[:, :], csl(X, c, tg * 512, tg * 512 + 512)),
                  reads=["stg", "xi"], writes=["stg"])
                E("sync", lambda g, c=c, tg=tg: g.dma_start(
                    out=out[c * 128:(c + 1) * 128, tg * 512:(tg + 1) * 512], in_=STG[:, :]),
                  reads=["stg"], dma="st", final=True)
        kb.build()
    return nc


def emit_ln_fp32_inplace(kb, C, X):
    E = kb.emit
    ones = C["ones_f32"]
    UT = C["UTIL"]
    PA, PBk, PC, PD = C["PA"], C["PBk"], C["PC"], C["PD"]
    for g in range(TG):
        for c in range(NCHUNK):
            E("tensor", lambda t, c=c, g=g: t.matmul(
                PA[0:1, :], ones[:, 0:1], csl(X, c, g * 512, g * 512 + 512),
                start=(c == 0), stop=(c == NCHUNK - 1)),
              reads=["xi", "consts"], writes=["pA"])
        for c in range(NCHUNK):
            sq = UT[:, 0:512]
            E("scalar", lambda s, c=c, g=g, sq=sq: s.activation(
                out=sq, in_=csl(X, c, g * 512, g * 512 + 512), func=AF.Square),
              reads=["xi"], writes=["ut0"])
            E("tensor", lambda t, sq=sq, c=c: t.matmul(
                PBk[0:1, :], ones[:, 0:1], sq,
                start=(c == 0), stop=(c == NCHUNK - 1)),
              reads=["ut0", "consts"], writes=["pB"])
        mu = UT[0:1, 512:1024]
        va = C["U72"][0:1, 0:512]
        E("scalar", lambda s, mu=mu: s.mul(mu, PA[0:1, 0:512], 1.0 / D),
          reads=["pA"], writes=["ut1"])
        E("scalar", lambda s, va=va: s.mul(va, PBk[0:1, 0:512], 1.0 / D),
          reads=["pB"], writes=["u72"])
        musq = C["U72"][1:2, 0:512] if False else None
        E("scalar", lambda s, mu=mu: s.activation(out=UT[0:1, 0:512], in_=mu, func=AF.Square),
          reads=["ut1"], writes=["ut0"])
        E("vector", lambda v, va=va: v.tensor_sub(va, va, UT[0:1, 0:512]),
          reads=["u72", "ut0"], writes=["u72"])
        E("scalar", lambda s, va=va: s.activation(out=va, in_=va, func=AF.Ln,
                                                  bias=C["scl_s"][0:1, 0:1]),
          reads=["u72", "consts"], writes=["u72"])
        E("scalar", lambda s, va=va: s.activation(out=va, in_=va, func=AF.Exp, scale=-0.5),
          reads=["u72"], writes=["u72"])
        E("tensor", lambda t, mu=mu: t.matmul(PC[:, 0:512], ones[0:1, :], mu,
                                              start=True, stop=True),
          reads=["ut1", "consts"], writes=["pC"])
        E("tensor", lambda t, va=va: t.matmul(PD[:, 0:512], ones[0:1, :], va,
                                              start=True, stop=True),
          reads=["u72", "consts"], writes=["pD"])
        for c in range(NCHUNK):
            E("vector", lambda v, c=c, g=g: v.tensor_sub(
                csl(X, c, g * 512, g * 512 + 512), csl(X, c, g * 512, g * 512 + 512),
                PC[:, 0:512]),
              reads=["xi", "pC"], writes=["xi"])
            E("vector", lambda v, c=c, g=g: v.tensor_mul(
                csl(X, c, g * 512, g * 512 + 512), csl(X, c, g * 512, g * 512 + 512),
                PD[:, 0:512]),
              reads=["xi", "pD"], writes=["xi"])


_BUILD_CACHE = {}


def _get_builds():
    if "nc" not in _BUILD_CACHE:
        consts = host_constants()
        _BUILD_CACHE["consts"] = consts
        _BUILD_CACHE["nc1"] = build_launch1(consts)
        _BUILD_CACHE["nc2"] = build_launch2(consts)
        _BUILD_CACHE["nc3"] = build_launch3(consts)
        _BUILD_CACHE["nc4"] = build_launch4(consts)
        _BUILD_CACHE["nc"] = True
    return _BUILD_CACHE


def kernel(**inputs):
    NC = 8
    b = _get_builds()
    consts = b["consts"]
    w = prep_weights(inputs)
    x = np.asarray(inputs["L0_attn_x"])
    cmap = {"c_" + k: v for k, v in consts.items()}

    maps = []
    for core in range(NC):
        m = dict(xT=np.ascontiguousarray(x[core].T),
                 l0_wqkvT=w["l0_wqkvT"], l0_bqkv=w["l0_bqkv"],
                 l0_wprojT=w["l0_wprojT"], l0_bproj=w["l0_bproj"])
        m.update(cmap)
        maps.append(m)
    r1 = run_bass_kernel_spmd(b["nc1"], maps, core_ids=list(range(NC))).results

    maps = []
    for core in range(NC):
        m = dict(x1p=np.asarray(r1[core]["x1p"]),
                 xnb_in=np.asarray(r1[core]["xnb_out"]),
                 moe_wg=np.asarray(inputs["L0_ffn_Wg"]),
                 moe_w1=np.asarray(inputs["L0_ffn_W1"]),
                 moe_w2=np.asarray(inputs["L0_ffn_W2"]),
                 moe_w3=np.asarray(inputs["L0_ffn_W3"]),
                 moe_ws1=np.asarray(inputs["L0_ffn_Ws1"]),
                 moe_ws2=np.asarray(inputs["L0_ffn_Ws2"]),
                 moe_ws3=np.asarray(inputs["L0_ffn_Ws3"]))
        m.update(cmap)
        maps.append(m)
    r2 = run_bass_kernel_spmd(b["nc2"], maps, core_ids=list(range(NC))).results

    maps = []
    for core in range(NC):
        m = dict(x1=np.asarray(r2[core]["x1_out"]),
                 l1_wq=np.asarray(inputs["L1_attn_Wq"]),
                 l1_wk=np.asarray(inputs["L1_attn_Wk"]),
                 l1_wv=np.asarray(inputs["L1_attn_Wv"]),
                 l1_eqT=w["l1_eqT"], l1_ekT=w["l1_ekT"], l1_ekTrev=w["l1_ekTrev"],
                 l1_ev0=w["l1_ev0"], l1_ev128=w["l1_ev128"],
                 l1_evb2a=w["l1_evb2a"], l1_evb2b=w["l1_evb2b"])
        m.update(cmap)
        maps.append(m)
    r3 = run_bass_kernel_spmd(b["nc3"], maps, core_ids=list(range(NC))).results

    maps = []
    for core in range(NC):
        m = dict(x1=np.asarray(r2[core]["x1_out"]),
                 x2p=np.asarray(r3[core]["x2p"]))
        for l in range(2):
            for nm in ("wq", "wk", "wv", "wo", "w1a", "w1g", "w2", "b1a", "b1g",
                       "b2", "lng", "lnb", "qs", "ks", "qsr", "ksr"):
                m[f"f_{nm}{l}"] = w[f"f_{nm}{l}"]
        m.update(cmap)
        maps.append(m)
    r4 = run_bass_kernel_spmd(b["nc4"], maps, core_ids=list(range(NC))).results

    out = np.stack([np.ascontiguousarray(np.asarray(r4[core]["out_final"]).T)
                    for core in range(NC)])
    return out.astype(np.float32)


# version 03
# speedup vs baseline: 1.0313x; 1.0313x over previous
import numpy as np
import ml_dtypes
from collections import defaultdict

import concourse.bass as bass
import concourse.mybir as mybir
from concourse.bass_utils import run_bass_kernel_spmd

F32 = mybir.dt.float32
BF16 = mybir.dt.bfloat16
I32 = mybir.dt.int32
U32 = mybir.dt.uint32
AF = mybir.ActivationFunctionType
ALU = mybir.AluOpType

ENGINES = ("sync", "vector", "scalar", "tensor", "gpsimd")
PSUM_KEYS = frozenset(["pA", "pB", "pC", "pD", "pO", "pR", "pBB", "pBk"])

T = 2048
D = 1024
NCHUNK = 8
TG = 4
NEG = -100000.0


class KB:
    def __init__(self, nc):
        self.nc = nc
        self.ops = {e: [] for e in ENGINES}
        self.count = defaultdict(int)
        self.is_dma_sem = {}
        self.known = {e: defaultdict(int) for e in ENGINES}
        self.key_w = {}
        self.key_r = defaultdict(list)
        self.final_waits = []
        self.barrier_snap = None
        self.barrier_done = set()

    def emit(self, eng, fn, reads=(), writes=(), dma=None, extra_waits=(), final=False):
        if dma is not None:
            assert eng in ("sync", "scalar", "gpsimd"), eng
            pfx = "sw" if eng == "gpsimd" else "hw"
            sem_name = f"D_{pfx}_{dma}"
            self.is_dma_sem[sem_name] = True
            inc = 16
        else:
            assert eng != "sync", "sync engine is DMA-only"
            sem_name = "E_" + eng
            self.is_dma_sem[sem_name] = False
            inc = 1
        deps = list(extra_waits)
        if self.barrier_snap is not None and eng not in self.barrier_done:
            deps.extend(self.barrier_snap.items())
            self.barrier_done.add(eng)
        pr = [k for k in reads if k in PSUM_KEYS]
        if pr:
            reads = [k for k in reads if k not in PSUM_KEYS]
            writes = list(writes) + pr
        for k in reads:
            t = self.key_w.get(k)
            if t is not None:
                deps.append(t)
        for k in writes:
            t = self.key_w.get(k)
            if t is not None:
                deps.append(t)
            deps.extend(self.key_r.get(k, ()))
        waits = {}
        kn = self.known[eng]
        for (s, v) in deps:
            if self.is_dma_sem.get(s):
                v = self.count[s]
            elif s == sem_name and eng == "tensor":
                continue
            if v <= kn[s]:
                continue
            if waits.get(s, 0) < v:
                waits[s] = v
        for s, v in waits.items():
            kn[s] = v
        self.count[sem_name] += inc
        tok = (sem_name, self.count[sem_name])
        self.ops[eng].append((sorted(waits.items()), fn, sem_name, inc))
        for k in writes:
            self.key_w[k] = tok
            self.key_r[k] = []
        for k in reads:
            self.key_r[k].append(tok)
        if final:
            self.final_waits.append(tok)
        return tok

    def barrier(self):
        self.barrier_snap = dict(self.count)
        self.barrier_done = set()

    def build(self):
        nc = self.nc
        import contextlib
        with contextlib.ExitStack() as ctx:
            sems = {s: ctx.enter_context(nc.semaphore(s)) for s in self.count}
            block = ctx.enter_context(nc.Block())

            def mk_section(eng_name):
                def section(engine):
                    for waits, fn, sem_name, inc in self.ops[eng_name]:
                        for s, v in waits:
                            engine.wait_ge(sems[s], v)
                        inst = fn(engine)
                        inst.then_inc(sems[sem_name], inc)
                    if eng_name == "scalar":
                        fw = {}
                        for s, v in self.final_waits:
                            v = self.count[s] if self.is_dma_sem.get(s) else v
                            fw[s] = max(fw.get(s, 0), v)
                        for s, v in sorted(fw.items()):
                            engine.wait_ge(sems[s], v)
                return section

            for e in ENGINES:
                if self.ops[e] or e == "scalar":
                    getattr(block, e)(mk_section(e))
        return self


def host_constants():
    c = {}
    pos = np.arange(T, dtype=np.float64)
    ang = pos[:, None] / (10000.0 ** (np.arange(0, 64, 2, dtype=np.float64) / 64.0))
    rope = np.stack([np.cos(ang), np.sin(ang)], axis=-1).reshape(T, 64).astype(np.float32)
    r1 = np.tile(rope.T, (2, 1))
    r2 = np.tile((1.0 - rope).T, (2, 1)).astype(np.float32)
    c["rope12"] = np.concatenate([r1, r2], axis=1).astype(np.float32)

    P = np.zeros((64, 64), np.float32)
    for i in range(32):
        P[2 * i, 2 * i + 1] = -1.0
        P[2 * i + 1, 2 * i] = 1.0
    PT = P.T
    c["protL0"] = np.block([[PT, np.zeros_like(PT)], [np.zeros_like(PT), PT]]).astype(np.float32)

    c["ones_f"] = np.ones((128, 128), np.float32)
    c["ones_b"] = np.ones((128, 128), np.float32).astype(ml_dtypes.bfloat16)
    c["eye_f"] = np.eye(128, dtype=np.float32)
    c["eye_b"] = np.eye(128, dtype=np.float32).astype(ml_dtypes.bfloat16)

    i_ = np.arange(128)[:, None]
    j_ = np.arange(512)[None, :]
    m = [(i_ <= j_ - d).astype(np.float32) for d in (0, 128, 256, 384)]
    c["cmask"] = np.concatenate(m, axis=1).astype(ml_dtypes.bfloat16)
    scl = np.zeros((1, 16), np.float32)
    scl[0, 0] = 1e-5
    scl[0, 1] = 1e-24
    c["scl"] = scl
    c["iota32"] = np.tile(np.arange(32, dtype=np.float32)[None, :], (128, 1))
    c["tokcol"] = np.arange(128, dtype=np.float32).reshape(128, 1)
    ii = np.arange(128)
    c["lstrict"] = (ii[:, None] < ii[None, :]).astype(np.float32)

    DH = 64
    half = np.arange(0, DH, 2, dtype=np.float64)
    inv_freq = 1.0 / 10000.0 ** (half / DH)
    tt = np.arange(T, dtype=np.float64)
    fr = tt[:, None] * inv_freq[None, :]
    freqs = np.concatenate([fr, fr], axis=1)
    base = (half + 0.4 * DH) / (1.4 * DH)
    power = (tt - T // 2) / 256.0
    scale = base[None, :] ** power[:, None]
    scale = np.concatenate([scale, scale], axis=1)
    cosf = np.cos(freqs); sinf = np.sin(freqs)
    g1q = (cosf * scale).T.astype(np.float32)
    g2q = (sinf * scale).T.astype(np.float32)
    g1k = (cosf / scale).T.astype(np.float32)
    g2k = (sinf / scale).T.astype(np.float32)
    c["xposq"] = np.concatenate([np.tile(g1q, (2, 1)), np.tile(g2q, (2, 1))], axis=1)
    c["xposk"] = np.concatenate([np.tile(g1k, (2, 1)), np.tile(g2k, (2, 1))], axis=1)

    c["blockmap"] = np.kron(np.eye(2, dtype=np.float32),
                            np.ones((64, 64), np.float32))

    Ph = np.zeros((64, 64), np.float32)
    for i in range(32):
        Ph[i, 32 + i] = -1.0
        Ph[32 + i, i] = 1.0
    PhT = Ph.T
    c["prothalf"] = np.block([[PhT, np.zeros_like(PhT)], [np.zeros_like(PhT), PhT]]).astype(np.float32)

    im = np.arange(128)[:, None]
    jl = np.arange(512)[None, :]
    madd = []
    mmul = []
    for a in range(8):
        diff = jl - (im + 128 * a - 512)
        keep = (diff >= 0) & (diff <= 512)
        madd.append(np.where(keep, 0.0, -100000.0).astype(np.float32))
        mmul.append(keep.astype(np.float32))
    c["fmaskadd"] = np.concatenate(madd, axis=1).astype(np.float32)
    c["fmaskmul"] = np.concatenate(mmul, axis=1).astype(ml_dtypes.bfloat16)
    return c


def make_const_inputs(nc, consts):
    aps = {}
    for k, v in consts.items():
        dt = {np.dtype(np.float32): F32, np.dtype(ml_dtypes.bfloat16): BF16,
              np.dtype(np.int32): I32}[v.dtype]
        aps[k] = nc.dram_tensor("c_" + k, list(v.shape), dt, kind="ExternalInput").ap()
    return aps


def prep_weights(inp):
    w = {}
    Wqkv = np.asarray(inp["L0_attn_Wqkv"])
    idx = np.zeros(3072, np.int64)
    for h in range(16):
        for p3 in range(3):
            for d in range(64):
                idx[p3 * 1024 + h * 64 + d] = h * 192 + p3 * 64 + d
    w["l0_wqkvT"] = np.ascontiguousarray(Wqkv[idx].T)
    w["l0_bqkv"] = np.ascontiguousarray(
        np.asarray(inp["L0_attn_bqkv"])[idx].reshape(3, 8, 128).transpose(2, 1, 0).reshape(128, 24))
    w["l0_wprojT"] = np.ascontiguousarray(np.asarray(inp["L0_attn_Wproj"]).T)
    w["l0_bproj"] = np.ascontiguousarray(
        np.asarray(inp["L0_attn_bproj"]).reshape(8, 128).T)
    Eq = np.asarray(inp["L1_attn_Eq"]); Ek = np.asarray(inp["L1_attn_Ek"])
    Ev = np.asarray(inp["L1_attn_Ev"])
    w["l1_eqT"] = np.ascontiguousarray(Eq.T)
    w["l1_ekT"] = np.ascontiguousarray(Ek.T)
    w["l1_ekTrev"] = np.ascontiguousarray(Ek.T[:, ::-1])
    w["l1_ev0"] = np.ascontiguousarray(Ev[0:1])
    w["l1_ev128"] = np.ascontiguousarray(Ev[128:129])
    jj = np.arange(255)
    idx = np.clip(127 - jj, -64, 64) + 64
    evb2 = Ev[idx]
    w["l1_evb2a"] = np.ascontiguousarray(evb2[0:128])
    w["l1_evb2b"] = np.ascontiguousarray(evb2[128:255])
    wqkv = np.asarray(inp["L1_ffn_wqkv"])
    wo = np.asarray(inp["L1_ffn_wo"])
    w1 = np.asarray(inp["L1_ffn_w1"])
    b1 = np.asarray(inp["L1_ffn_b1"])
    w2 = np.asarray(inp["L1_ffn_w2"])
    b2 = np.asarray(inp["L1_ffn_b2"])
    lng = np.asarray(inp["L1_ffn_ln_g"])
    lnb = np.asarray(inp["L1_ffn_ln_b"])
    qs = np.asarray(inp["L1_ffn_q_scale"])
    ks = np.asarray(inp["L1_ffn_k_scale"])
    HID = 2730
    HPAD = 2816
    for l in range(2):
        w[f"f_wq{l}"] = np.ascontiguousarray(wqkv[l][:, 0:512])
        w[f"f_wk{l}"] = np.ascontiguousarray(wqkv[l][:, 512:1024])
        w[f"f_wv{l}"] = np.ascontiguousarray(wqkv[l][:, 1024:1536])
        w[f"f_wo{l}"] = np.ascontiguousarray(wo[l])
        w1a = np.zeros((1024, HPAD), np.float32); w1a[:, :HID] = w1[l][:, :HID]
        w1g = np.zeros((1024, HPAD), np.float32); w1g[:, :HID] = w1[l][:, HID:]
        w[f"f_w1a{l}"] = w1a
        w[f"f_w1g{l}"] = w1g
        w2p = np.zeros((HPAD, 1024), np.float32); w2p[:HID] = w2[l]
        w[f"f_w2{l}"] = w2p
        b1a = np.zeros(HPAD, np.float32); b1a[:HID] = b1[l][:HID]
        b1g = np.zeros(HPAD, np.float32); b1g[:HID] = b1[l][HID:]
        w[f"f_b1a{l}"] = np.ascontiguousarray(b1a.reshape(22, 128).T)
        w[f"f_b1g{l}"] = np.ascontiguousarray(b1g.reshape(22, 128).T)
        w[f"f_b2{l}"] = np.ascontiguousarray(b2[l].reshape(8, 128).T)
        w[f"f_lng{l}"] = np.ascontiguousarray(lng[l].reshape(8, 128).T)
        w[f"f_lnb{l}"] = np.ascontiguousarray(lnb[l].reshape(8, 128).T)
        w[f"f_qs{l}"] = np.ascontiguousarray(np.tile(qs[l], 2).reshape(128, 1))
        w[f"f_ks{l}"] = np.ascontiguousarray(np.tile(ks[l], 2).reshape(128, 1))
        w[f"f_qsr{l}"] = np.ascontiguousarray(qs[l].reshape(1, 64))
        w[f"f_ksr{l}"] = np.ascontiguousarray(ks[l].reshape(1, 64))
    return w


def csl(arena, c, lo, hi):
    return arena[:, c * T + lo: c * T + hi]


def emit_ln(kb, C, src_key, dst_key, X, XNB):
    E = kb.emit
    ones = C["ones_f32"]
    SQ = C["UTIL"]
    PA, PBk, PC, PD = C["PA"], C["PBk"], C["PC"], C["PD"]
    stat = C["STAT"]
    for g in range(TG):
        for c in range(NCHUNK):
            E("tensor", lambda t, c=c, g=g: t.matmul(
                PA[0:1, :], ones[:, 0:1], csl(X, c, g * 512, g * 512 + 512),
                start=(c == 0), stop=(c == NCHUNK - 1)),
              reads=[src_key, "consts"], writes=["pA"])
        for c in range(NCHUNK):
            sq = SQ[:, (c % 2) * 512:(c % 2) * 512 + 512]
            kq = f"ut{c % 2}"
            E("scalar", lambda s, c=c, g=g, sq=sq: s.activation(
                out=sq, in_=csl(X, c, g * 512, g * 512 + 512), func=AF.Square),
              reads=[src_key], writes=[kq])
            E("tensor", lambda t, sq=sq, c=c: t.matmul(
                PBk[0:1, :], ones[:, 0:1], sq,
                start=(c == 0), stop=(c == NCHUNK - 1)),
              reads=[kq, "consts"], writes=["pB"])
        mu = stat[0:1, 0:512]
        ms = stat[0:1, 512:1024]
        va = stat[0:1, 1024:1536]
        rs = stat[0:1, 1536:2048]
        E("scalar", lambda s: s.mul(mu, PA[0:1, :], 1.0 / D), reads=["pA"], writes=["st_mu"])
        E("scalar", lambda s: s.mul(ms, PBk[0:1, :], 1.0 / D), reads=["pB"], writes=["st_ms"])
        E("scalar", lambda s: s.activation(out=va, in_=mu, func=AF.Square),
          reads=["st_mu"], writes=["st_va"])
        E("vector", lambda v: v.tensor_sub(va, ms, va), reads=["st_ms", "st_va"], writes=["st_va"])
        E("scalar", lambda s: s.activation(out=rs, in_=va, func=AF.Ln,
                                           bias=C["scl_s"][0:1, 0:1]),
          reads=["st_va", "consts"], writes=["st_rs"])
        E("scalar", lambda s: s.activation(out=rs, in_=rs, func=AF.Exp, scale=-0.5),
          reads=["st_rs"], writes=["st_rs"])
        E("tensor", lambda t: t.matmul(PC[:, 0:512], ones[0:1, :], mu, start=True, stop=True),
          reads=["st_mu", "consts"], writes=["pC"])
        E("tensor", lambda t: t.matmul(PD[:, 0:512], ones[0:1, :], rs, start=True, stop=True),
          reads=["st_rs", "consts"], writes=["pD"])
        for c in range(NCHUNK):
            tmp = SQ[:, (c % 2) * 512:(c % 2) * 512 + 512]
            kq = f"ut{c % 2}"
            E("vector", lambda v, c=c, g=g, tmp=tmp: v.tensor_sub(
                tmp, csl(X, c, g * 512, g * 512 + 512), PC[:, 0:512]),
              reads=[src_key, "pC"], writes=[kq])
            E("vector", lambda v, c=c, g=g, tmp=tmp: v.tensor_mul(
                csl(XNB, c, g * 512, g * 512 + 512), tmp, PD[:, 0:512]),
              reads=[kq, "pD"], writes=[dst_key])


def emit_l0_attn(kb, C, XNB, X):
    E = kb.emit
    QK, QKB, PT, VT = C["QK"], C["QKB"], C["PT"], C["VT"]
    ONB, WQ = C["ONORM"], C["WQKV"]
    PC, PD, PO, PR, PBB = C["PC"], C["PD"], C["PO"], C["PR"], C["PBB"]
    bcol = C["l0_b_cols"]
    wq_in = C["in_l0_wqkvT"]
    gbank = [("pC", PC), ("pD", PD)]

    for hp in range(8):
        ws = hp % 2
        for i3 in range(3):
            E("gpsimd", lambda g, ws=ws, i3=i3, hp=hp: g.dma_start(
                out=WQ[:, ws * 3072 + i3 * 1024: ws * 3072 + i3 * 1024 + 1024]
                    .rearrange("p (a o) -> p a o", a=8),
                in_=wq_in.rearrange("(a p) o -> p a o", p=128)
                    [:, :, i3 * 1024 + hp * 128: i3 * 1024 + hp * 128 + 128]),
              writes=[f"w{ws}{i3}"], dma=f"wl{ws}")
        for i3 in range(3):
            for g in range(TG):
                bk, pb = gbank[g % 2]
                for c in range(NCHUNK):
                    E("tensor", lambda t, i3=i3, g=g, c=c, ws=ws, pb=pb: t.matmul(
                        pb[:, 0:512],
                        WQ[:, ws * 3072 + i3 * 1024 + c * 128: ws * 3072 + i3 * 1024 + c * 128 + 128],
                        csl(XNB, c, g * 512, g * 512 + 512),
                        start=(c == 0), stop=(c == NCHUNK - 1)),
                      reads=[f"w{ws}{i3}", "xnb"], writes=[bk])
                if i3 == 2:
                    E("scalar", lambda s, g=g, hp=hp, pb=pb: s.activation(
                        out=QKB[:, 2 * T + g * 512: 2 * T + g * 512 + 512],
                        in_=pb[:, 0:512], func=AF.Identity,
                        bias=bcol[:, 3 * hp + 2: 3 * hp + 3]),
                      reads=[bk, "l0bc"], writes=["qkb2"])
                    continue
                qk = QK[:, (g % 2) * 512: (g % 2) * 512 + 512]
                kqf = f"qkf{g % 2}"
                E("scalar", lambda s, i3=i3, g=g, hp=hp, pb=pb, qk=qk: s.activation(
                    out=qk, in_=pb[:, 0:512], func=AF.Identity,
                    bias=bcol[:, 3 * hp + i3: 3 * hp + i3 + 1]),
                  reads=[bk, "l0bc"], writes=[kqf])
                bk2, pb2 = gbank[(g + 1) % 2]
                E("tensor", lambda t, pb2=pb2, qk=qk: t.matmul(
                    pb2[:, 0:512], C["protL0_s"][:, :], qk,
                    start=True, stop=True),
                  reads=[kqf, "consts"], writes=[bk2])
                tmp = C["UTIL"][:, (g % 2) * 512:(g % 2) * 512 + 512]
                kq = f"ut{g % 2}"
                E("vector", lambda v, g=g, pb2=pb2, tmp=tmp: v.tensor_mul(
                    tmp, pb2[:, 0:512],
                    C["rope_s"][:, 2048 + g * 512: 2048 + g * 512 + 512]),
                  reads=[bk2, "consts"], writes=[kq])
                E("vector", lambda v, g=g, qk=qk: v.tensor_mul(
                    qk, qk, C["rope_s"][:, g * 512: g * 512 + 512]),
                  reads=[kqf, "consts"], writes=[kqf])
                E("vector", lambda v, i3=i3, g=g, tmp=tmp, qk=qk: v.tensor_add(
                    QKB[:, i3 * T + g * 512: i3 * T + g * 512 + 512], qk, tmp),
                  reads=[kqf, kq], writes=[f"qkb{i3}"])
        E("vector", lambda v: v.memset(VT[:], 1.0), writes=["vt"])
        for hh in range(2):
            vb = 64 * hh
            for mc in range(16):
                E("tensor", lambda t, vb=vb, mc=mc: t.transpose(
                    PBB[:, 0:64],
                    QKB[vb:vb + 64, 2 * T + mc * 128: 2 * T + mc * 128 + 128],
                    C["eye_b_s"][vb:vb + 64, vb:vb + 64]),
                  reads=["qkb2", "consts_b"], writes=["pBB"])
                E("vector", lambda v, hh=hh, mc=mc: v.tensor_copy(
                    VT[:, (hh * 16 + mc) * 65: (hh * 16 + mc) * 65 + 64],
                    PBB[:, 0:64]),
                  reads=["pBB"], writes=["vt"])
        for hh in range(2):
            qb = 64 * hh
            for g in range(TG):
                nmc = 4 * g + 4
                for mc in range(nmc):
                    bk, pb = gbank[mc % 2]
                    sl = mc % 4
                    pts = PT[:, sl * 512: sl * 512 + 512]
                    E("tensor", lambda t, qb=qb, g=g, mc=mc, pb=pb: t.matmul(
                        pb[:, 0:512],
                        QKB[qb:qb + 64, T + mc * 128: T + mc * 128 + 128],
                        QKB[qb:qb + 64, g * 512: g * 512 + 512],
                        start=True, stop=True),
                      reads=["qkb0", "qkb1"], writes=[bk])
                    E("scalar", lambda s, pb=pb, pts=pts: s.activation(
                        out=pts, in_=pb[:, 0:512],
                        func=AF.Exp, scale=0.125),
                      reads=[bk], writes=[f"pt{sl}"])
                    if mc >= 4 * g:
                        dlt = mc - 4 * g
                        E("vector", lambda v, dlt=dlt, pts=pts: v.tensor_mul(
                            pts, pts,
                            C["cmask_s"][:, dlt * 512: dlt * 512 + 512]),
                          reads=[f"pt{sl}", "consts_b"], writes=[f"pt{sl}"])
                    E("tensor", lambda t, hh=hh, mc=mc, nmc=nmc, pts=pts: t.matmul(
                        PO[0:65, 0:512],
                        VT[:, (hh * 16 + mc) * 65: (hh * 16 + mc) * 65 + 65],
                        pts,
                        start=(mc == 0), stop=(mc == nmc - 1)),
                      reads=["vt", f"pt{sl}"], writes=["pO"])
                E("vector", lambda v: v.reciprocal(C["U72"][64:65, 0:512], PO[64:65, 0:512]),
                  reads=["pO"], writes=["u72"])
                E("tensor", lambda t: t.matmul(
                    PR[0:64, 0:512], C["ones_f32"][64:65, 0:64], C["U72"][64:65, 0:512],
                    start=True, stop=True),
                  reads=["u72", "consts"], writes=["pR"])
                E("vector", lambda v: v.tensor_copy(C["UTIL"][0:64, 0:512], PR[0:64, 0:512]),
                  reads=["pR"], writes=["ut0"])
                E("vector", lambda v, hp=hp, hh=hh, g=g: v.tensor_mul(
                    csl(ONB, hp, g * 512, g * 512 + 512)[64 * hh: 64 * hh + 64, :],
                    PO[0:64, 0:512], C["UTIL"][0:64, 0:512]),
                  reads=["pO", "ut0"], writes=["onorm"])
    wp_in = C["in_l0_wprojT"]
    for og in range(NCHUNK):
        ws = og % 2
        wkey = f"w{ws}0"
        E("gpsimd", lambda g, og=og, ws=ws: g.dma_start(
            out=WQ[:, ws * 3072: ws * 3072 + 1024].rearrange("p (a o) -> p a o", a=8),
            in_=wp_in.rearrange("(a p) o -> p a o", p=128)
                [:, :, og * 128: og * 128 + 128]),
          writes=[wkey], dma=f"wl{ws}")
        for g in range(TG):
            bk, pb = gbank[g % 2]
            for c in range(NCHUNK):
                E("tensor", lambda t, ws=ws, g=g, c=c, pb=pb: t.matmul(
                    pb[:, 0:512],
                    WQ[:, ws * 3072 + c * 128: ws * 3072 + c * 128 + 128],
                    csl(ONB, c, g * 512, g * 512 + 512),
                    start=(c == 0), stop=(c == NCHUNK - 1)),
                  reads=[wkey, "onorm"], writes=[bk])
            E("vector", lambda v, og=og, g=g, pb=pb: v.scalar_tensor_tensor(
                out=csl(X, og, g * 512, g * 512 + 512),
                in0=pb[:, 0:512], scalar=C["l0_bp_col"][:, og:og + 1],
                in1=csl(X, og, g * 512, g * 512 + 512),
                op0=ALU.add, op1=ALU.add),
              reads=[bk, "l0bc", "x"], writes=["x"])


def build_launch1(consts):
    nc = bass.Bass("TRN2", target_bir_lowering=False)
    xT = nc.dram_tensor("xT", [D, T], F32, kind="ExternalInput").ap()
    wqkvT = nc.dram_tensor("l0_wqkvT", [D, 3072], F32, kind="ExternalInput").ap()
    bqkv = nc.dram_tensor("l0_bqkv", [128, 24], F32, kind="ExternalInput").ap()
    wprojT = nc.dram_tensor("l0_wprojT", [D, D], F32, kind="ExternalInput").ap()
    bproj = nc.dram_tensor("l0_bproj", [128, 8], F32, kind="ExternalInput").ap()
    caps = make_const_inputs(nc, consts)

    x1p = nc.dram_tensor("x1p", [D, T], F32, kind="ExternalOutput").ap()
    xnb_out = nc.dram_tensor("xnb_out", [D, T], BF16, kind="ExternalOutput").ap()
    onorm_out = nc.dram_tensor("onorm_out", [D, T], BF16, kind="ExternalOutput").ap()

    import contextlib
    with contextlib.ExitStack() as ec:
        e = ec.enter_context
        X = e(nc.sbuf_tensor("X", [128, NCHUNK * T], F32))
        XNB = e(nc.sbuf_tensor("XNB", [128, NCHUNK * T], BF16))
        ONORM = e(nc.sbuf_tensor("ONORM", [128, NCHUNK * T], BF16))
        QK = e(nc.sbuf_tensor("QK", [128, 1024], F32))
        QKB = e(nc.sbuf_tensor("QKB", [128, 3 * T], BF16))
        PT = e(nc.sbuf_tensor("PTS", [128, 4 * 512], BF16))
        VT = e(nc.sbuf_tensor("VT", [128, 2080], BF16))
        WQKV = e(nc.sbuf_tensor("WQKV", [128, 6144], BF16))
        UTIL = e(nc.sbuf_tensor("UTIL", [128, 1024], F32))
        U72 = e(nc.sbuf_tensor("U72", [72, 512], F32))
        STAT = e(nc.sbuf_tensor("STAT", [1, 2048], F32))
        ones_f32 = e(nc.sbuf_tensor("ones_f32", [128, 128], F32))
        ones_b_s = e(nc.sbuf_tensor("ones_b_s", [128, 128], BF16))
        eye_f_s = e(nc.sbuf_tensor("eye_f_s", [128, 128], F32))
        eye_b_s = e(nc.sbuf_tensor("eye_b_s", [128, 128], BF16))
        protL0_s = e(nc.sbuf_tensor("protL0_s", [128, 128], F32))
        rope_s = e(nc.sbuf_tensor("rope_s", [128, 4096], F32))
        cmask_s = e(nc.sbuf_tensor("cmask_s", [128, 2048], BF16))
        scl_s = e(nc.sbuf_tensor("scl_s", [1, 16], F32))
        bcol = e(nc.sbuf_tensor("l0_b_cols", [128, 24], F32))
        bpcol = e(nc.sbuf_tensor("l0_bp_col", [128, 8], F32))
        PA = e(nc.psum_tensor("PA", [128, 512], F32))
        PBk = e(nc.psum_tensor("PBk", [128, 512], F32))
        PC = e(nc.psum_tensor("PC", [128, 512], F32))
        PD = e(nc.psum_tensor("PD", [128, 512], F32))
        PO = e(nc.psum_tensor("PO", [128, 512], F32))
        PR = e(nc.psum_tensor("PR", [128, 512], F32))
        PBB = e(nc.psum_tensor("PBB", [128, 1024], BF16))

        kb = KB(nc)
        E = kb.emit
        C = dict(
            QK=QK, QKB=QKB, PT=PT, VT=VT, ONORM=ONORM, WQKV=WQKV,
            UTIL=UTIL, U72=U72, STAT=STAT,
            ones_f32=ones_f32, ones_b_s=ones_b_s, eye_f_s=eye_f_s,
            eye_b_s=eye_b_s, protL0_s=protL0_s,
            rope_s=rope_s, cmask_s=cmask_s, scl_s=scl_s,
            l0_b_cols=bcol, l0_bp_col=bpcol,
            in_l0_wqkvT=wqkvT, in_l0_wprojT=wprojT,
            PA=PA, PBk=PBk, PC=PC, PD=PD, PO=PO, PR=PR, PBB=PBB,
        )

        for c in range(NCHUNK):
            E("sync", lambda g, c=c: g.dma_start(
                out=X[:, c * T:(c + 1) * T], in_=xT[c * 128:(c + 1) * 128, :]),
              writes=["x"], dma="x")
        E("sync", lambda g: g.dma_start(out=ones_f32[:], in_=caps["ones_f"]),
          writes=["consts"], dma="c")
        E("gpsimd", lambda g: g.dma_start(out=ones_b_s[:], in_=caps["ones_b"]),
          writes=["consts_b"], dma="cb")
        E("sync", lambda g: g.dma_start(out=eye_f_s[:], in_=caps["eye_f"]),
          writes=["consts"], dma="c")
        E("sync", lambda g: g.dma_start(out=eye_b_s[:], in_=caps["eye_b"]),
          writes=["consts_b"], dma="c")
        E("sync", lambda g: g.dma_start(out=protL0_s[:], in_=caps["protL0"]),
          writes=["consts"], dma="c")
        E("sync", lambda g: g.dma_start(out=rope_s[:], in_=caps["rope12"]),
          writes=["consts"], dma="c")
        E("sync", lambda g: g.dma_start(out=cmask_s[:], in_=caps["cmask"]),
          writes=["consts_b"], dma="c")
        E("sync", lambda g: g.dma_start(out=scl_s[:], in_=caps["scl"]),
          writes=["consts"], dma="c")
        E("sync", lambda g: g.dma_start(out=bcol[:, :], in_=bqkv),
          writes=["l0bc"], dma="c")
        E("sync", lambda g: g.dma_start(out=bpcol[:, :], in_=bproj),
          writes=["l0bc"], dma="c")

        emit_ln(kb, C, "x", "xnb", X, XNB)
        emit_l0_attn(kb, C, XNB, X)

        for c in range(NCHUNK):
            E("sync", lambda g, c=c: g.dma_start(
                out=x1p[c * 128:(c + 1) * 128, :], in_=X[:, c * T:(c + 1) * T]),
              reads=["x"], dma="st", final=True)
            E("sync", lambda g, c=c: g.dma_start(
                out=onorm_out[c * 128:(c + 1) * 128, :], in_=ONORM[:, c * T:(c + 1) * T]),
              reads=["onorm"], dma="st", final=True)
            E("sync", lambda g, c=c: g.dma_start(
                out=xnb_out[c * 128:(c + 1) * 128, :], in_=XNB[:, c * T:(c + 1) * T]),
              reads=["xnb"], dma="st", final=True)
        kb.build()
    return nc




CLOC = 512


def emit_moe(kb, C, X, XNB):
    import os
    MOE_STAGES = frozenset(os.environ.get("MOE_STAGES", "abcde").split(",")[0])
    E = kb.emit
    PA, PBk, PC, PD, PR, PBB = C["PA"], C["PBk"], C["PC"], C["PD"], C["PR"], C["PBB"]
    UT = C["UTIL"]
    SCO, G4, IDXF, RS, AT = C["SCO"], C["G4"], C["IDXF"], C["RS"], C["AT"]
    GIDXI, GATE = C["GIDXI"], C["GATE"]
    XTM, XG, UU = C["XTM"], C["XG"], C["UU"]
    WE1, WE3, WE2, WG = C["WE1"], C["WE3"], C["WE2"], C["WG"]
    stat = C["STAT"]
    eye_b = C["eye_b_s"]
    ones = C["ones_f32"]
    xn_tm, lists, xout = C["d_xn_tm"], C["d_lists"], C["d_xout"]

    for i in range(16 if "a" in MOE_STAGES else 0):
        for c in range(NCHUNK):
            E("tensor", lambda t, i=i, c=c: t.transpose(
                PBB[:, 0:128], csl(XNB, c, i * 128, i * 128 + 128), eye_b[:, :]),
              reads=["xnb", "consts_b"], writes=["pBB"])
            E("vector", lambda v, c=c: v.tensor_copy(XTM[:, c * 128: c * 128 + 128], PBB[:, 0:128]),
              reads=["pBB"], writes=["xtm"])
        E("sync", lambda g, i=i: g.dma_start(
            out=xn_tm[i * 128:(i + 1) * 128, :], in_=XTM[:, :]),
          reads=["xtm"], writes=["d_xntm"], dma="xtm", final=False)

    E("vector", lambda v: v.memset(UT[:, :], 0.0), writes=["ut0", "ut1"])
    E("sync", lambda g: g.dma_start(
        out=lists.rearrange("(p a) c -> p a c", p=128), in_=UT[:, 0:258].rearrange("p (a c) -> p a c", c=2)),
      reads=["ut0"], writes=["d_lists"], dma="lst")
    E("sync", lambda g: g.dma_start(out=xout[16384:16385, :], in_=UT[0:1, 0:1024]),
      reads=["ut0", "ut1"], writes=["d_xout_z"], dma="lst")

    E("gpsimd", lambda g: g.dma_start(
        out=WG[:, :].rearrange("p (a o) -> p a o", a=8),
        in_=C["in_wg"].rearrange("(a p) o -> p a o", p=128)),
      writes=["wg"], dma="wg")
    for tg in range(TG):
        for c in range(NCHUNK):
            E("tensor", lambda t, tg=tg, c=c: t.matmul(
                PC[0:32, 0:512], WG[:, c * 32: c * 32 + 32],
                csl(XNB, c, tg * 512, tg * 512 + 512),
                start=(c == 0), stop=(c == NCHUNK - 1)),
              reads=["wg", "xnb"], writes=["pC"])
        E("vector", lambda v, tg=tg: v.tensor_copy(UT[0:32, 0:512], PC[0:32, 0:512]),
          reads=["pC"], writes=["ut0"])
        for q in range(4):
            E("tensor", lambda t, q=q: t.transpose(
                PD[:, 0:32], UT[0:32, q * 128: q * 128 + 128], C["eye_f_s"][0:32, 0:32]),
              reads=["ut0", "consts"], writes=["pD"])
            E("vector", lambda v, tg=tg, q=q: v.tensor_copy(
                SCO[:, (tg * 4 + q) * 32: (tg * 4 + q) * 32 + 32], PD[:, 0:32]),
              reads=["pD"], writes=["sco"])

    BP = frozenset(os.environ.get("MOE_BP", "123456"))
    E("vector", lambda v: v.memset(stat[0:1, 0:32], 0.0), writes=["prefrow"])
    for i in range(16 if "b" in MOE_STAGES else 0):
        sco = SCO[:, i * 32: i * 32 + 32]
        u0 = UT[:, 0:1]
        u1 = UT[:, 1:2]
        u2 = UT[:, 2:3]
        u3 = UT[:, 3:4]
        E("vector", lambda v, sco=sco: v.reduce_max(u0, sco, axis=mybir.AxisListType.X),
          reads=["sco"], writes=["ut0"])
        E("vector", lambda v: v.tensor_scalar(out=u1, in0=u0, scalar1=-1.0, scalar2=None, op0=ALU.mult),
          reads=["ut0"], writes=["ut0"])
        E("scalar", lambda s, sco=sco: s.activation(out=sco, in_=sco, func=AF.Exp,
                                                    bias=u1, accum_out=u2),
          reads=["sco", "ut0"], writes=["sco", "ut0"])
        E("vector", lambda v: v.reciprocal(u3, u2), reads=["ut0"], writes=["ut0"])
        E("vector", lambda v, sco=sco: v.tensor_scalar(out=sco, in0=sco, scalar1=u3,
                                                       scalar2=None, op0=ALU.mult),
          reads=["sco", "ut0"], writes=["sco"])
        if "2" not in BP:
            continue
        mx8 = UT[:, 8:16]
        E("vector", lambda v, sco=sco, mx8=mx8: v.max(out=mx8, in_=sco), reads=["sco"], writes=["ut0"])
        E("vector", lambda v, i=i, mx8=mx8: v.tensor_copy(G4[:, i * 4: i * 4 + 4], mx8[:, 0:4]),
          reads=["ut0"], writes=["g4"])
        idxu = C["IDXU"]
        E("vector", lambda v, sco=sco, mx8=mx8, idxu=idxu: v.max_index(
            out=idxu[:, 0:8], in_max=mx8[:, 0:8], in_values=sco),
          reads=["sco", "ut0"], writes=["idxu"])
        E("vector", lambda v, i=i, idxu=idxu: v.tensor_copy(IDXF[:, i * 4: i * 4 + 4], idxu[:, 0:4]),
          reads=["idxu"], writes=["idxf"])
        if "3" not in BP:
            continue
        at = AT[:, 0:32]
        E("vector", lambda v, at=at: v.memset(at, 0.0), writes=["at"])
        for k in range(4):
            msk = UT[:, 32:64]
            E("vector", lambda v, i=i, k=k, msk=msk: v.tensor_scalar(
                out=msk, in0=C["iota32_s"][:, :], scalar1=IDXF[:, i * 4 + k: i * 4 + k + 1],
                scalar2=None, op0=ALU.is_equal),
              reads=["idxf", "consts"], writes=["ut0"])
            E("vector", lambda v, at=at, msk=msk: v.tensor_add(at, at, msk),
              reads=["ut0", "at"], writes=["at"])
        if "4" not in BP:
            continue
        E("tensor", lambda t, at=at: t.matmul(PD[:, 0:32], C["lstrict_s"][:, :], at,
                                              start=True, stop=False),
          reads=["at", "consts"], writes=["pD"])
        E("tensor", lambda t: t.matmul(PD[:, 0:32], ones[0:1, :], stat[0:1, 0:32],
                                       start=False, stop=True),
          reads=["prefrow", "consts"], writes=["pD"])
        E("vector", lambda v: v.tensor_copy(RS[:, 0:32], PD[:, 0:32]),
          reads=["pD"], writes=["rs"])
        E("tensor", lambda t, at=at: t.matmul(PR[0:1, 0:32], ones[:, 0:1], at,
                                              start=True, stop=True),
          reads=["at", "consts"], writes=["pR"])
        E("vector", lambda v: v.tensor_add(stat[0:1, 0:32], stat[0:1, 0:32], PR[0:1, 0:32]),
          reads=["pR", "prefrow"], writes=["prefrow"])
        if "5" not in BP:
            continue
        tokf = UT[:, 64:65]
        E("vector", lambda v, i=i, tokf=tokf: v.tensor_scalar(
            out=tokf, in0=C["tokcol_s"][:, :], scalar1=float(i * 128), scalar2=None, op0=ALU.add),
          reads=["consts"], writes=["ut0"])
        for k in range(4):
            msk = UT[:, 32:64]
            rk = UT[:, 65:66]
            vk = UT[:, 66:67]
            gk = UT[:, 67:68]
            dk = UT[:, 68:69]
            sc32 = UT[:, 96:128]
            E("vector", lambda v, i=i, k=k, msk=msk: v.tensor_scalar(
                out=msk, in0=C["iota32_s"][:, :], scalar1=IDXF[:, i * 4 + k: i * 4 + k + 1],
                scalar2=None, op0=ALU.is_equal),
              reads=["idxf", "consts"], writes=["ut0"])
            E("vector", lambda v, msk=msk, sc32=sc32: v.tensor_mul(
                sc32, RS[:, 0:32], msk),
              reads=["rs", "ut0"], writes=["ut0"])
            E("vector", lambda v, rk=rk, sc32=sc32: v.reduce_sum(
                rk, sc32, axis=mybir.AxisListType.X),
              reads=["ut0"], writes=["ut0"])
            E("vector", lambda v, rk=rk, vk=vk: v.tensor_scalar(
                out=vk, in0=rk, scalar1=float(CLOC), scalar2=None, op0=ALU.is_lt),
              reads=["ut0"], writes=["ut0"])
            E("vector", lambda v, i=i, k=k, vk=vk, gk=gk: v.tensor_mul(
                gk, G4[:, i * 4 + k: i * 4 + k + 1], vk),
              reads=["g4", "ut0"], writes=["ut0"])
            E("vector", lambda v, i=i, k=k, gk=gk: v.tensor_copy(
                GATE[:, i * 4 + k: i * 4 + k + 1], gk),
              reads=["ut0"], writes=["gate"])
            E("vector", lambda v, i=i, k=k, dk=dk: v.tensor_scalar(
                out=dk, in0=IDXF[:, i * 4 + k: i * 4 + k + 1], scalar1=float(CLOC),
                scalar2=None, op0=ALU.mult),
              reads=["idxf"], writes=["ut0"])
            E("vector", lambda v, rk=rk, dk=dk: v.tensor_add(dk, dk, rk),
              reads=["ut0", "ut0"], writes=["ut0"])
            E("vector", lambda v, dk=dk: v.tensor_scalar(
                out=dk, in0=dk, scalar1=-16384.0, scalar2=None, op0=ALU.add),
              reads=["ut0"], writes=["ut0"])
            E("vector", lambda v, dk=dk, vk=vk: v.tensor_mul(dk, dk, vk),
              reads=["ut0", "ut0"], writes=["ut0"])
            E("vector", lambda v, dk=dk: v.tensor_scalar(
                out=dk, in0=dk, scalar1=16384.0, scalar2=None, op0=ALU.add),
              reads=["ut0"], writes=["ut0"])
            E("vector", lambda v, i=i, k=k, dk=dk: v.tensor_copy(
                GIDXI[:, i * 4 + k: i * 4 + k + 1], dk),
              reads=["ut0"], writes=["gidxi"])
            pair = UT[:, 70:72]
            E("vector", lambda v, pair=pair, tokf=tokf: v.tensor_copy(pair[:, 0:1], tokf),
              reads=["ut0"], writes=["ut0"])
            E("vector", lambda v, pair=pair, gk=gk: v.tensor_copy(pair[:, 1:2], gk),
              reads=["ut0"], writes=["ut0"])
            di = C["DI1"]
            E("vector", lambda v, dk=dk, di=di: v.tensor_copy(di[:, 0:1], dk),
              reads=["ut0"], writes=["di1"])
            if "6" in BP:
                E("gpsimd", lambda g, pair=pair, di=di: g.indirect_dma_start(
                    out=lists, out_offset=bass.IndirectOffsetOnAxis(ap=di[:, 0:1], axis=0),
                    in_=pair, in_offset=None),
                  reads=["ut0", "ut0", "di1", "d_lists"], writes=["d_lists"], dma="isc")

    for nm, w_in, wt in (("we1", C["in_ws1"], WE1), ("we3", C["in_ws3"], WE3), ("we2", C["in_ws2"], WE2)):
        E("gpsimd", lambda g, w_in=w_in, wt=wt: g.dma_start(
            out=wt[:, :].rearrange("p (a o) -> p a o", a=8),
            in_=w_in.rearrange("(a p) o -> p a o", p=128)),
          writes=[nm + "0", nm + "1"], dma="wl0")
    for tg in range(TG if "c" in MOE_STAGES else 0):
        for hc in range(8):
            for c in range(NCHUNK):
                E("tensor", lambda t, tg=tg, hc=hc, c=c: t.matmul(
                    PA[:, 0:512], WE1[:, c * 1024 + hc * 128: c * 1024 + hc * 128 + 128],
                    csl(XNB, c, tg * 512, tg * 512 + 512),
                    start=(c == 0), stop=(c == NCHUNK - 1)),
                  reads=["we10", "we11", "xnb"], writes=["pA"])
            for c in range(NCHUNK):
                E("tensor", lambda t, tg=tg, hc=hc, c=c: t.matmul(
                    PBk[:, 0:512], WE3[:, c * 1024 + hc * 128: c * 1024 + hc * 128 + 128],
                    csl(XNB, c, tg * 512, tg * 512 + 512),
                    start=(c == 0), stop=(c == NCHUNK - 1)),
                  reads=["we30", "we31", "xnb"], writes=["pB"])
            sl = UT[:, 0:512]
            E("scalar", lambda s, sl=sl: s.activation(out=sl, in_=PA[:, 0:512], func=AF.Silu),
              reads=["pA"], writes=["ut0"])
            E("vector", lambda v, hc=hc, sl=sl: v.tensor_mul(
                UU[:, hc * 512: hc * 512 + 512], PBk[:, 0:512], sl),
              reads=["pB", "ut0"], writes=[f"uu{hc}"])
        for dc in range(NCHUNK):
            bk, pb = (("pC", PC) if dc % 2 == 0 else ("pD", PD))
            for hc in range(8):
                E("tensor", lambda t, dc=dc, hc=hc, pb=pb: t.matmul(
                    pb[:, 0:512],
                    WE2[:, hc * 1024 + dc * 128: hc * 1024 + dc * 128 + 128],
                    UU[:, hc * 512: hc * 512 + 512],
                    start=(hc == 0), stop=(hc == 7)),
                  reads=["we20", "we21", f"uu{hc}"], writes=[bk])
            E("vector", lambda v, tg=tg, dc=dc, pb=pb, bk=bk: v.tensor_add(
                csl(X, dc, tg * 512, tg * 512 + 512), pb[:, 0:512],
                csl(X, dc, tg * 512, tg * 512 + 512)),
              reads=[bk, "x"], writes=["x"])

    LG, LI, GC, DI1 = C["LG"], C["LI"], C["GC"], C["DI1"]
    OFD, OT4, GR, ACC = C["OFD"], C["OT4"], C["GR"], C["ACC"]
    for e in range(32 if "d" in MOE_STAGES else 0):
        s = e % 2
        E("gpsimd", lambda g, e=e, s=s: g.dma_start(
            out=WE1[:, s * 4096: s * 4096 + 4096].rearrange("p (a o) -> p a o", a=8),
            in_=C["in_w1"][e].rearrange("(a p) o -> p a o", p=128)),
          writes=[f"we1{s}"], dma=f"wl{s}")
        E("gpsimd", lambda g, e=e, s=s: g.dma_start(
            out=WE3[:, s * 4096: s * 4096 + 4096].rearrange("p (a o) -> p a o", a=8),
            in_=C["in_w3"][e].rearrange("(a p) o -> p a o", p=128)),
          writes=[f"we3{s}"], dma=f"wl{s}")
        E("gpsimd", lambda g, e=e, s=s: g.dma_start(
            out=WE2[:, s * 4096: s * 4096 + 4096].rearrange("p (a o) -> p a o", a=4),
            in_=C["in_w2"][e].rearrange("(a p) o -> p a o", p=128)),
          writes=[f"we2{s}"], dma=f"wl{s}")
        E("sync", lambda g, e=e: g.dma_start(
            out=LG[:, :].rearrange("p (a c) -> p a c", c=2),
            in_=bass.AP(C["d_lists"].tensor, e * CLOC * 2, [[2, 128], [256, 4], [1, 2]])),
          reads=["d_lists"], writes=["lg"], dma="lst")
        E("vector", lambda v: v.tensor_copy(LI[:, 0:4], LG[:, :].rearrange("p (a c) -> p a c", c=2)[:, :, 0]),
          reads=["lg"], writes=["li"])
        E("vector", lambda v: v.tensor_copy(GC[:, 0:4], LG[:, :].rearrange("p (a c) -> p a c", c=2)[:, :, 1]),
          reads=["lg"], writes=["gc"])
        for j in range(4):
            E("gpsimd", lambda g, j=j: g.indirect_dma_start(
                out=XTM[:, :], out_offset=None, in_=C["d_xn_tm"],
                in_offset=bass.IndirectOffsetOnAxis(ap=LI[:, j:j + 1], axis=0)),
              reads=["li", "d_xntm"], writes=["xtm"], dma="gx")
            for c in range(NCHUNK):
                E("tensor", lambda t, c=c: t.transpose(
                    PBB[:, 0:128], XTM[:, c * 128: c * 128 + 128], C["eye_b_s"][:, :]),
                  reads=["xtm", "consts_b"], writes=["pBB"])
                E("vector", lambda v, c=c, j=j: v.tensor_copy(
                    XG[:, c * 512 + j * 128: c * 512 + j * 128 + 128], PBB[:, 0:128]),
                  reads=["pBB"], writes=["xg"])
        for hc in range(4):
            for c in range(NCHUNK):
                E("tensor", lambda t, hc=hc, c=c, s=s: t.matmul(
                    PA[:, 0:512], WE1[:, s * 4096 + c * 512 + hc * 128: s * 4096 + c * 512 + hc * 128 + 128],
                    XG[:, c * 512: c * 512 + 512],
                    start=(c == 0), stop=(c == NCHUNK - 1)),
                  reads=[f"we1{s}", "xg"], writes=["pA"])
            for c in range(NCHUNK):
                E("tensor", lambda t, hc=hc, c=c, s=s: t.matmul(
                    PBk[:, 0:512], WE3[:, s * 4096 + c * 512 + hc * 128: s * 4096 + c * 512 + hc * 128 + 128],
                    XG[:, c * 512: c * 512 + 512],
                    start=(c == 0), stop=(c == NCHUNK - 1)),
                  reads=[f"we3{s}", "xg"], writes=["pB"])
            sl = UT[:, 0:512]
            E("scalar", lambda s2, sl=sl: s2.activation(out=sl, in_=PA[:, 0:512], func=AF.Silu),
              reads=["pA"], writes=["ut0"])
            E("vector", lambda v, hc=hc, sl=sl: v.tensor_mul(
                UU[:, hc * 512: hc * 512 + 512], PBk[:, 0:512], sl),
              reads=["pB", "ut0"], writes=[f"uu{hc}"])
        for dc in range(NCHUNK):
            bk, pb = (("pC", PC) if dc % 2 == 0 else ("pD", PD))
            for hc in range(4):
                E("tensor", lambda t, dc=dc, hc=hc, s=s, pb=pb: t.matmul(
                    pb[:, 0:512],
                    WE2[:, s * 4096 + hc * 1024 + dc * 128: s * 4096 + hc * 1024 + dc * 128 + 128],
                    UU[:, hc * 512: hc * 512 + 512],
                    start=(hc == 0), stop=(hc == 3)),
                  reads=[f"we2{s}", f"uu{hc}"], writes=[bk])
            E("vector", lambda v, pb=pb, bk=bk: v.tensor_copy(OFD[:, 0:512], pb[:, 0:512]),
              reads=[bk], writes=["ofd"])
            for j in range(4):
                E("tensor", lambda t, j=j: t.transpose(
                    PR[:, 0:128], OFD[:, j * 128: j * 128 + 128], C["eye_f_s"][:, :]),
                  reads=["ofd", "consts"], writes=["pR"])
                E("scalar", lambda s2, j=j, dc=dc: s2.activation(
                    out=OT4[:, j * 1024 + dc * 128: j * 1024 + dc * 128 + 128],
                    in_=PR[:, 0:128], func=AF.Identity, scale=GC[:, j:j + 1]),
                  reads=["pR", "gc"], writes=[f"ot{j}"])
        for j in range(4):
            E("sync", lambda g, e=e, j=j: g.dma_start(
                out=C["d_xout"][e * CLOC + j * 128: e * CLOC + j * 128 + 128, :],
                in_=OT4[:, j * 1024: j * 1024 + 1024]),
              reads=[f"ot{j}"], writes=["d_xout"], dma="xo")

    for i in range(16 if "e" in MOE_STAGES else 0):
        for k in range(4):
            E("vector", lambda v, i=i, k=k: v.tensor_copy(DI1[:, 0:1], GIDXI[:, i * 4 + k: i * 4 + k + 1]),
              reads=["gidxi"], writes=["di1"])
            E("gpsimd", lambda g: g.indirect_dma_start(
                out=GR[:, :], out_offset=None, in_=C["d_xout"],
                in_offset=bass.IndirectOffsetOnAxis(ap=DI1[:, 0:1], axis=0)),
              reads=["di1", "d_xout", "d_xout_z"],
              writes=["gr", "ot0", "ot1", "ot2", "ot3"], dma="gy")
            if k == 0:
                E("vector", lambda v: v.tensor_copy(ACC[:, :], GR[:, :]),
                  reads=["gr"], writes=["acc"])
            else:
                E("vector", lambda v: v.tensor_add(ACC[:, :], ACC[:, :], GR[:, :]),
                  reads=["gr", "acc"], writes=["acc"])
        for c in range(NCHUNK):
            E("tensor", lambda t, c=c: t.transpose(
                PR[:, 0:128], ACC[:, c * 128: c * 128 + 128], C["eye_f_s"][:, :]),
              reads=["acc", "consts"], writes=["pR"])
            E("vector", lambda v, i=i, c=c: v.tensor_add(
                csl(X, c, i * 128, i * 128 + 128), PR[:, 0:128],
                csl(X, c, i * 128, i * 128 + 128)),
              reads=["pR", "x"], writes=["x"])


def build_launch2(consts):
    nc = bass.Bass("TRN2", target_bir_lowering=False)
    x1p = nc.dram_tensor("x1p", [D, T], F32, kind="ExternalInput").ap()
    xnb_in = nc.dram_tensor("xnb_in", [D, T], BF16, kind="ExternalInput").ap()
    wg = nc.dram_tensor("moe_wg", [D, 32], F32, kind="ExternalInput").ap()
    w1 = nc.dram_tensor("moe_w1", [32, D, 512], F32, kind="ExternalInput").ap()
    w2 = nc.dram_tensor("moe_w2", [32, 512, D], F32, kind="ExternalInput").ap()
    w3 = nc.dram_tensor("moe_w3", [32, D, 512], F32, kind="ExternalInput").ap()
    ws1 = nc.dram_tensor("moe_ws1", [D, D], F32, kind="ExternalInput").ap()
    ws2 = nc.dram_tensor("moe_ws2", [D, D], F32, kind="ExternalInput").ap()
    ws3 = nc.dram_tensor("moe_ws3", [D, D], F32, kind="ExternalInput").ap()
    caps = make_const_inputs(nc, consts)

    x1_out = nc.dram_tensor("x1_out", [D, T], F32, kind="ExternalOutput").ap()

    d_xn_tm = nc.dram_tensor("d_xn_tm", [T, D], BF16, kind="Internal").ap()
    d_lists = nc.dram_tensor("d_lists", [16512, 2], F32, kind="Internal").ap()
    d_xout = nc.dram_tensor("d_xout", [16385, D], F32, kind="Internal").ap()

    import contextlib
    with contextlib.ExitStack() as ec:
        e = ec.enter_context
        X = e(nc.sbuf_tensor("X", [128, NCHUNK * T], F32))
        XNB = e(nc.sbuf_tensor("XNB", [128, NCHUNK * T], BF16))
        WE1 = e(nc.sbuf_tensor("WE1", [128, 8192], BF16))
        WE3 = e(nc.sbuf_tensor("WE3", [128, 8192], BF16))
        WE2 = e(nc.sbuf_tensor("WE2", [128, 8192], BF16))
        WG = e(nc.sbuf_tensor("WG", [128, 256], BF16))
        XG = e(nc.sbuf_tensor("XG", [128, 4096], BF16))
        UU = e(nc.sbuf_tensor("UU", [128, 4096], BF16))
        XTM = e(nc.sbuf_tensor("XTM", [128, 1024], BF16))
        OFD = e(nc.sbuf_tensor("OFD", [128, 512], F32))
        OT4 = e(nc.sbuf_tensor("OT4", [128, 4096], F32))
        GR = e(nc.sbuf_tensor("GR", [128, 1024], F32))
        ACC = e(nc.sbuf_tensor("ACC", [128, 1024], F32))
        SCO = e(nc.sbuf_tensor("SCO", [128, 512], F32))
        G4 = e(nc.sbuf_tensor("G4", [128, 64], F32))
        IDXF = e(nc.sbuf_tensor("IDXF", [128, 64], F32))
        IDXU = e(nc.sbuf_tensor("IDXU", [128, 8], U32))
        RS = e(nc.sbuf_tensor("RS", [128, 32], F32))
        AT = e(nc.sbuf_tensor("AT", [128, 32], F32))
        GATE = e(nc.sbuf_tensor("GATE", [128, 64], F32))
        GIDXI = e(nc.sbuf_tensor("GIDXI", [128, 64], F32))
        DI1 = e(nc.sbuf_tensor("DI1", [128, 1], I32))
        LG = e(nc.sbuf_tensor("LG", [128, 8], F32))
        LI = e(nc.sbuf_tensor("LI", [128, 4], I32))
        GC = e(nc.sbuf_tensor("GC", [128, 4], F32))
        UTIL = e(nc.sbuf_tensor("UTIL", [128, 1024], F32))
        STAT = e(nc.sbuf_tensor("STAT", [1, 64], F32))
        ones_f32 = e(nc.sbuf_tensor("ones_f32", [128, 128], F32))
        eye_f_s = e(nc.sbuf_tensor("eye_f_s", [128, 128], F32))
        eye_b_s = e(nc.sbuf_tensor("eye_b_s", [128, 128], BF16))
        iota32_s = e(nc.sbuf_tensor("iota32_s", [128, 32], F32))
        tokcol_s = e(nc.sbuf_tensor("tokcol_s", [128, 1], F32))
        lstrict_s = e(nc.sbuf_tensor("lstrict_s", [128, 128], F32))
        PA = e(nc.psum_tensor("PA", [128, 512], F32))
        PBk = e(nc.psum_tensor("PBk", [128, 512], F32))
        PC = e(nc.psum_tensor("PC", [128, 512], F32))
        PD = e(nc.psum_tensor("PD", [128, 512], F32))
        PR = e(nc.psum_tensor("PR", [128, 512], F32))
        PBB = e(nc.psum_tensor("PBB", [128, 1024], BF16))

        kb = KB(nc)
        E = kb.emit
        C = dict(
            UTIL=UTIL, STAT=STAT, ones_f32=ones_f32, eye_f_s=eye_f_s, eye_b_s=eye_b_s,
            iota32_s=iota32_s, tokcol_s=tokcol_s, lstrict_s=lstrict_s,
            SCO=SCO, G4=G4, IDXF=IDXF, IDXU=IDXU, RS=RS, AT=AT, GATE=GATE, GIDXI=GIDXI,
            XTM=XTM, XG=XG, UU=UU, OFD=OFD, OT4=OT4, GR=GR, ACC=ACC,
            WE1=WE1, WE3=WE3, WE2=WE2, WG=WG, LG=LG, LI=LI, GC=GC, DI1=DI1,
            PA=PA, PBk=PBk, PC=PC, PD=PD, PR=PR, PBB=PBB,
            d_xn_tm=d_xn_tm, d_lists=d_lists, d_xout=d_xout,
            in_wg=wg, in_w1=w1, in_w2=w2, in_w3=w3,
            in_ws1=ws1, in_ws2=ws2, in_ws3=ws3,
        )

        for c in range(NCHUNK):
            E("sync", lambda g, c=c: g.dma_start(
                out=X[:, c * T:(c + 1) * T], in_=x1p[c * 128:(c + 1) * 128, :]),
              writes=["x"], dma="x")
            E("sync", lambda g, c=c: g.dma_start(
                out=XNB[:, c * T:(c + 1) * T], in_=xnb_in[c * 128:(c + 1) * 128, :]),
              writes=["xnb"], dma="x")
        for nm, cap, sb in (("consts", "ones_f", ones_f32), ("consts", "eye_f", eye_f_s),
                            ("consts", "iota32", iota32_s), ("consts", "tokcol", tokcol_s),
                            ("consts", "lstrict", lstrict_s)):
            E("sync", lambda g, cap=cap, sb=sb: g.dma_start(out=sb[:, :], in_=caps[cap]),
              writes=[nm], dma="c")
        E("gpsimd", lambda g: g.dma_start(out=eye_b_s[:], in_=caps["eye_b"].bitcast(BF16) if False else caps["eye_b"]),
          writes=["consts_b"], dma="cb")

        emit_moe(kb, C, X, XNB)

        for c in range(NCHUNK):
            E("sync", lambda g, c=c: g.dma_start(
                out=x1_out[c * 128:(c + 1) * 128, :], in_=X[:, c * T:(c + 1) * T]),
              reads=["x"], dma="st", final=True)
        kb.build()
    return nc


def emit_l1a(kb, C, XNB, X):
    E = kb.emit
    PA, PBk, PC, PD, PO, PR, PBB = (C["PA"], C["PBk"], C["PC"], C["PD"],
                                    C["PO"], C["PR"], C["PBB"])
    UT = C["UTIL"]
    Q1, K1, V1 = C["Q1"], C["K1"], C["V1"]
    VT1, VTLO, VTHI = C["VT1"], C["VTLO"], C["VTHI"]
    PT = C["PT"]
    WH = C["WH"]
    AUG = C["AUG"]
    ONER = C["ONER"]
    PADB = C["PADB"]
    QEF = C["QEF"]
    EVBC = C["EVBC"]
    TMPB = C["TMPB"]
    U72 = C["U72"]
    eye_b = C["eye_b_s"]
    ones = C["ones_f32"]
    ones_b = C["ones_b_s"]
    qepad, kepad, pbd = C["d_qepad"], C["d_kepad"], C["d_pbd"]

    E("vector", lambda v: v.tensor_mul(QEF[0:64, 0:129], C["eqT_f"][:, :], C["ekT_f"][:, :]),
      reads=["consts"], writes=["qef"])
    E("tensor", lambda t: t.matmul(PA[0:1, 0:129], ones[0:64, 0:1], QEF[0:64, 0:129],
                                   start=True, stop=True),
      reads=["qef", "consts"], writes=["pA"])
    E("vector", lambda v: v.tensor_copy(C["PPROW"][0:1, 0:129], PA[0:1, 0:129]),
      reads=["pA"], writes=["pprow"])
    E("tensor", lambda t: t.matmul(PR[:, 0:64], ones_b[0:1, :], C["ev128_b"][0:1, 0:64],
                                   start=True, stop=True),
      reads=["consts_b"], writes=["pR"])
    E("vector", lambda v: v.tensor_copy(EVBC[:, 0:64], PR[:, 0:64]), reads=["pR"], writes=["evbc"])
    E("tensor", lambda t: t.matmul(PR[:, 0:64], ones_b[0:1, :], C["ev0_b"][0:1, 0:64],
                                   start=True, stop=True),
      reads=["consts_b"], writes=["pR"])
    E("vector", lambda v: v.tensor_copy(EVBC[:, 64:128], PR[:, 0:64]), reads=["pR"], writes=["evbc"])
    E("vector", lambda v: v.memset(ONER[:, :], 1.0), writes=["oner"])

    for h in range(16):
        ws = h % 2
        for pi, wsrc in ((0, C["in_wq1"]), (1, C["in_wk1"]), (2, C["in_wv1"])):
            E("gpsimd", lambda g, h=h, ws=ws, pi=pi, wsrc=wsrc: g.dma_start(
                out=WH[:, ws * 1536 + pi * 512: ws * 1536 + pi * 512 + 512]
                    .rearrange("p (a o) -> p a o", a=8),
                in_=wsrc.rearrange("(a p) o -> p a o", p=128)
                    [:, :, h * 64: h * 64 + 64]),
              writes=[f"wh{ws}{pi}"], dma=f"wl{ws}")
        for pi, dst, dk in ((0, Q1, "q1"), (1, K1, "k1"), (2, V1, "v1")):
            for g2 in range(TG):
                bk, pb = (("pC", PC) if g2 % 2 == 0 else ("pD", PD))
                for c in range(NCHUNK):
                    E("tensor", lambda t, ws=ws, pi=pi, g2=g2, c=c, pb=pb: t.matmul(
                        pb[0:64, 0:512],
                        WH[:, ws * 1536 + pi * 512 + c * 64: ws * 1536 + pi * 512 + c * 64 + 64],
                        csl(XNB, c, g2 * 512, g2 * 512 + 512),
                        start=(c == 0), stop=(c == NCHUNK - 1)),
                      reads=[f"wh{ws}{pi}", "xnb"], writes=[bk])
                E("vector", lambda v, dst=dst, g2=g2, pb=pb: v.tensor_copy(
                    dst[:, g2 * 512: g2 * 512 + 512], pb[0:64, 0:512]),
                  reads=[bk], writes=[dk])
        E("vector", lambda v: v.memset(VT1[:], 1.0), writes=["vt1"])
        for mc in range(16):
            E("tensor", lambda t, mc=mc: t.transpose(
                PBB[:, 0:64], V1[:, mc * 128: mc * 128 + 128], eye_b[0:64, 0:64]),
              reads=["v1", "consts_b"], writes=["pBB"])
            E("vector", lambda v, mc=mc: v.tensor_copy(
                VT1[:, mc * 65: mc * 65 + 64], PBB[:, 0:64]),
              reads=["pBB"], writes=["vt1"])
        E("vector", lambda v: v.tensor_copy(VTLO[:], VT1[:]), reads=["vt1"], writes=["vtlo"])
        E("vector", lambda v: v.tensor_copy(VTHI[:], VT1[:]), reads=["vt1"], writes=["vthi"])
        for mc in range(16):
            E("vector", lambda v, mc=mc: v.tensor_add(
                VTLO[:, mc * 65: mc * 65 + 64], VTLO[:, mc * 65: mc * 65 + 64], EVBC[:, 0:64]),
              reads=["evbc", "vtlo"], writes=["vtlo"])
            E("vector", lambda v, mc=mc: v.tensor_add(
                VTHI[:, mc * 65: mc * 65 + 64], VTHI[:, mc * 65: mc * 65 + 64], EVBC[:, 64:128]),
              reads=["evbc", "vthi"], writes=["vthi"])
        for g2 in range(TG):
            E("tensor", lambda t, g2=g2: t.matmul(
                PA[0:1, 0:512], C["ekT_b"][:, 128:129], Q1[:, g2 * 512: g2 * 512 + 512],
                start=True, stop=True),
              reads=["q1", "consts_b"], writes=["pA"])
            E("vector", lambda v, g2=g2: v.tensor_copy(
                AUG[0:1, 0 * 2048 + g2 * 512: 0 * 2048 + g2 * 512 + 512], PA[0:1, 0:512]),
              reads=["pA"], writes=["aug"])
            E("tensor", lambda t, g2=g2: t.matmul(
                PA[0:1, 0:512], C["ekT_b"][:, 0:1], Q1[:, g2 * 512: g2 * 512 + 512],
                start=True, stop=True),
              reads=["q1", "consts_b"], writes=["pA"])
            E("vector", lambda v, g2=g2: v.tensor_copy(
                AUG[0:1, 1 * 2048 + g2 * 512: 1 * 2048 + g2 * 512 + 512], PA[0:1, 0:512]),
              reads=["pA"], writes=["aug"])
            E("tensor", lambda t, g2=g2: t.matmul(
                PA[0:1, 0:512], C["eqT_b"][:, 128:129], K1[:, g2 * 512: g2 * 512 + 512],
                start=True, stop=True),
              reads=["k1", "consts_b"], writes=["pA"])
            E("scalar", lambda s, g2=g2: s.activation(
                out=AUG[0:1, 2 * 2048 + g2 * 512: 2 * 2048 + g2 * 512 + 512],
                in_=PA[0:1, 0:512], func=AF.Identity, bias=C["PPROW"][0:1, 128:129]),
              reads=["pA", "pprow"], writes=["aug"])
            E("tensor", lambda t, g2=g2: t.matmul(
                PA[0:1, 0:512], C["eqT_b"][:, 0:1], K1[:, g2 * 512: g2 * 512 + 512],
                start=True, stop=True),
              reads=["k1", "consts_b"], writes=["pA"])
            E("scalar", lambda s, g2=g2: s.activation(
                out=AUG[0:1, 3 * 2048 + g2 * 512: 3 * 2048 + g2 * 512 + 512],
                in_=PA[0:1, 0:512], func=AF.Identity, bias=C["PPROW"][0:1, 0:1]),
              reads=["pA", "pprow"], writes=["aug"])
        for lt in range(16):
            E("tensor", lambda t, lt=lt: t.matmul(
                PA[:, 0:129], Q1[:, lt * 128: lt * 128 + 128], C["ekTrev_b"][:, :],
                start=True, stop=True),
              reads=["q1", "consts_b"], writes=["pA"])
            E("vector", lambda v: v.tensor_copy(QEF[:, 0:129], PA[:, 0:129]),
              reads=["pA"], writes=["qef"])
            E("vector", lambda v: v.tensor_copy(
                PADB[:, 0:192], QEF[:, 0:1].to_broadcast([128, 192])),
              reads=["qef"], writes=["padb"])
            E("vector", lambda v: v.tensor_copy(PADB[:, 192:321], QEF[:, 0:129]),
              reads=["qef"], writes=["padb"])
            E("vector", lambda v: v.tensor_copy(
                PADB[:, 321:512], QEF[:, 128:129].to_broadcast([128, 191])),
              reads=["qef"], writes=["padb"])
            E("sync", lambda g, lt=lt: g.dma_start(
                out=qepad[lt * 128:(lt + 1) * 128, :], in_=PADB[:, :]),
              reads=["padb"], writes=["d_qepad"], dma="pads")
            E("tensor", lambda t, lt=lt: t.matmul(
                PA[:, 0:129], K1[:, lt * 128: lt * 128 + 128], C["eqT_b"][:, :],
                start=True, stop=False),
              reads=["k1", "consts_b"], writes=["pA"])
            E("tensor", lambda t: t.matmul(
                PA[:, 0:129], ones[0:1, :], C["PPROW"][0:1, 0:129],
                start=False, stop=True),
              reads=["pprow", "consts"], writes=["pA"])
            E("vector", lambda v: v.tensor_copy(QEF[:, 0:129], PA[:, 0:129]),
              reads=["pA"], writes=["qef"])
            E("vector", lambda v: v.tensor_copy(
                UT[:, 0:192], QEF[:, 0:1].to_broadcast([128, 192])),
              reads=["qef"], writes=["ut0"])
            E("vector", lambda v: v.tensor_copy(UT[:, 192:321], QEF[:, 0:129]),
              reads=["qef"], writes=["ut0"])
            E("vector", lambda v: v.tensor_copy(
                UT[:, 321:512], QEF[:, 128:129].to_broadcast([128, 191])),
              reads=["qef"], writes=["ut0"])
            E("sync", lambda g, lt=lt: g.dma_start(
                out=kepad[lt * 128:(lt + 1) * 128, :], in_=UT[:, 0:512]),
              reads=["ut0"], writes=["d_kepad"], dma="pads")
        for g2 in range(TG):
            for mc in range(16):
                cls_all = []
                for lt in range(4 * g2, 4 * g2 + 4):
                    d = mc - lt
                    cls_all.append("B" if abs(d) <= 1 else ("L" if d < 0 else "H"))
                if all(cc == "L" for cc in cls_all):
                    outer = "L"
                elif all(cc == "H" for cc in cls_all):
                    outer = "H"
                else:
                    outer = None
                bk, pb = (("pC", PC) if mc % 2 == 0 else ("pD", PD))
                if outer is not None:
                    ai = (0, 2) if outer == "L" else (1, 3)
                    E("tensor", lambda t, mc=mc, g2=g2, pb=pb: t.matmul(
                        pb[:, 0:512], K1[:, mc * 128: mc * 128 + 128],
                        Q1[:, g2 * 512: g2 * 512 + 512],
                        start=True, stop=False),
                      reads=["q1", "k1"], writes=[bk])
                    E("tensor", lambda t, mc=mc, g2=g2, pb=pb, ai=ai: t.matmul(
                        pb[:, 0:512], ONER[0:1, mc * 128: mc * 128 + 128],
                        AUG[0:1, ai[0] * 2048 + g2 * 512: ai[0] * 2048 + g2 * 512 + 512],
                        start=False, stop=False),
                      reads=["oner", "aug"], writes=[bk])
                    E("tensor", lambda t, mc=mc, g2=g2, pb=pb, ai=ai: t.matmul(
                        pb[:, 0:512],
                        AUG[0:1, ai[1] * 2048 + mc * 128: ai[1] * 2048 + mc * 128 + 128],
                        ONER[0:1, g2 * 512: g2 * 512 + 512],
                        start=False, stop=True),
                      reads=["oner", "aug"], writes=[bk])
                    E("scalar", lambda s, mc=mc, pb=pb: s.activation(
                        out=PT[:, mc * 512: mc * 512 + 512], in_=pb[:, 0:512],
                        func=AF.Exp, scale=0.125),
                      reads=[bk], writes=[f"pt{mc}"])
                else:
                    for li, lt in enumerate(range(4 * g2, 4 * g2 + 4)):
                        cc = cls_all[li]
                        lo = li * 128
                        E("tensor", lambda t, mc=mc, lt=lt, lo=lo, pb=pb, cc=cc: t.matmul(
                            pb[:, lo: lo + 128], K1[:, mc * 128: mc * 128 + 128],
                            Q1[:, lt * 128: lt * 128 + 128],
                            start=True, stop=(cc == "B")),
                          reads=["q1", "k1"], writes=[bk])
                        if cc == "B":
                            pass
                        else:
                            ai = (0, 2) if cc == "L" else (1, 3)
                            E("tensor", lambda t, mc=mc, lt=lt, lo=lo, pb=pb, ai=ai: t.matmul(
                                pb[:, lo: lo + 128], ONER[0:1, mc * 128: mc * 128 + 128],
                                AUG[0:1, ai[0] * 2048 + lt * 128: ai[0] * 2048 + lt * 128 + 128],
                                start=False, stop=False),
                              reads=["oner", "aug"], writes=[bk])
                            E("tensor", lambda t, mc=mc, lt=lt, lo=lo, pb=pb, ai=ai: t.matmul(
                                pb[:, lo: lo + 128],
                                AUG[0:1, ai[1] * 2048 + mc * 128: ai[1] * 2048 + mc * 128 + 128],
                                ONER[0:1, lt * 128: lt * 128 + 128],
                                start=False, stop=True),
                              reads=["oner", "aug"], writes=[bk])
                    for li, lt in enumerate(range(4 * g2, 4 * g2 + 4)):
                        cc = cls_all[li]
                        lo = li * 128
                        if cc == "B":
                            ap2 = bass.AP(kepad.tensor, (mc * 128) * 511 + lt * 128 + 256,
                                          [[511, 128], [1, 128]])
                            kb2 = f"skb{mc % 2}"
                            E("sync", lambda g, ap2=ap2, mc=mc: g.dma_start(
                                out=C["SKF"][:, (mc % 2) * 128: (mc % 2) * 128 + 128], in_=ap2),
                              reads=["d_kepad"], writes=[kb2], dma=f"skf{mc % 2}")
                            E("vector", lambda v, mc=mc, lo=lo, pb=pb: v.tensor_add(
                                pb[:, lo: lo + 128], pb[:, lo: lo + 128],
                                C["SKF"][:, (mc % 2) * 128: (mc % 2) * 128 + 128]),
                              reads=[bk, kb2], writes=[bk])
                            ap1 = bass.AP(qepad.tensor, (lt * 128) * 511 + mc * 128 + 256,
                                          [[511, 128], [1, 128]])
                            kb1 = f"skt{mc % 2}"
                            E("sync", lambda g, ap1=ap1, mc=mc: g.dma_start(
                                out=C["SKB"][:, (mc % 2) * 128: (mc % 2) * 128 + 128], in_=ap1,
                                transpose=True),
                              reads=["d_qepad"], writes=[kb1], dma=f"skq{mc % 2}")
                            E("vector", lambda v, mc=mc, lo=lo, pb=pb: v.tensor_add(
                                pb[:, lo: lo + 128], pb[:, lo: lo + 128],
                                C["SKB"][:, (mc % 2) * 128: (mc % 2) * 128 + 128]),
                              reads=[bk, kb1], writes=[bk])
                    E("scalar", lambda s, mc=mc, pb=pb: s.activation(
                        out=PT[:, mc * 512: mc * 512 + 512], in_=pb[:, 0:512],
                        func=AF.Exp, scale=0.125),
                      reads=[bk], writes=[f"pt{mc}"])
            def emit_pv(mc_list, stop_last):
                for mc in mc_list:
                    for li, lt in enumerate(range(4 * g2, 4 * g2 + 4)):
                        d = mc - lt
                        cc = "B" if abs(d) <= 1 else ("L" if d < 0 else "H")
                        vt, vk = ((VT1, "vt1") if cc == "B" else
                                  (VTLO, "vtlo") if cc == "L" else (VTHI, "vthi"))
                        E("tensor", lambda t, mc=mc, li=li, vt=vt, st=(mc == 0 and li == 0),
                                    sp=(stop_last and mc == 15 and li == 3): t.matmul(
                            PO[0:65, li * 128: li * 128 + 128], vt[:, mc * 65: mc * 65 + 65],
                            PT[:, mc * 512 + li * 128: mc * 512 + li * 128 + 128],
                            start=st, stop=sp),
                          reads=[vk, f"pt{mc}"], writes=["pO"])
            emit_pv(range(15), False)
            for li, lt in enumerate(range(4 * g2, 4 * g2 + 4)):
                E("vector", lambda v, li=li: v.memset(
                    C["PBTM"][:, li * 384: li * 384 + 384], 0.0), writes=[f"pbtm{li}"])
                for bi, mc in enumerate((lt - 1, lt, lt + 1)):
                    if mc < 0 or mc > 15:
                        continue
                    E("tensor", lambda t, mc=mc, li=li: t.transpose(
                        PBB[:, 0:128], PT[:, mc * 512 + li * 128: mc * 512 + li * 128 + 128],
                        eye_b[:, :]),
                      reads=[f"pt{mc}", "consts_b"], writes=["pBB"])
                    E("vector", lambda v, bi=bi, li=li: v.tensor_copy(
                        C["PBTM"][:, li * 384 + bi * 128: li * 384 + bi * 128 + 128],
                        PBB[:, 0:128]),
                      reads=["pBB"], writes=[f"pbtm{li}"])
                E("sync", lambda g, lt=lt, li=li: g.dma_start(
                    out=pbd[lt * 128:(lt + 1) * 128, :],
                    in_=C["PBTM"][:, li * 384: li * 384 + 384]),
                  reads=[f"pbtm{li}"], writes=["d_pbd"], dma="pads")
                for jh in range(2):
                    apj = bass.AP(pbd.tensor, (lt * 128) * 384 + 1 + jh * 128,
                                  [[385, 128], [1, 128]])
                    sn = li * 2 + jh
                    E("sync", lambda g, apj=apj, sn=sn: g.dma_start(
                        out=C["SKB"][:, (2 + sn) * 128: (2 + sn) * 128 + 128], in_=apj,
                        transpose=True),
                      reads=["d_pbd"], writes=[f"skj{sn}"], dma=f"skj{sn % 4}")
            for li, lt in enumerate(range(4 * g2, 4 * g2 + 4)):
                E("tensor", lambda t, li=li: t.matmul(
                    PO[0:64, li * 128: li * 128 + 128], C["evb2a_b"][:, :],
                    C["SKB"][:, (2 + li * 2) * 128: (2 + li * 2) * 128 + 128],
                    start=False, stop=False),
                  reads=[f"skj{li * 2}", "consts_b"], writes=["pO"])
                E("tensor", lambda t, li=li: t.matmul(
                    PO[0:64, li * 128: li * 128 + 128], C["evb2b_b"][:, :],
                    C["SKB"][:, (3 + li * 2) * 128: (3 + li * 2) * 128 + 128][0:127, :],
                    start=False, stop=False),
                  reads=[f"skj{li * 2 + 1}", "consts_b"], writes=["pO"])
            emit_pv([15], True)
            E("vector", lambda v: v.reciprocal(U72[64:65, 0:512], PO[64:65, 0:512]),
              reads=["pO"], writes=["u72"])
            E("tensor", lambda t: t.matmul(
                PR[0:64, 0:512], ones[64:65, 0:64], U72[64:65, 0:512],
                start=True, stop=True),
              reads=["u72", "consts"], writes=["pR"])
            E("vector", lambda v: v.tensor_copy(UT[0:64, 0:512], PR[0:64, 0:512]),
              reads=["pR"], writes=["ut0"])
            E("vector", lambda v: v.tensor_mul(TMPB[:, 0:512], PO[0:64, 0:512], UT[0:64, 0:512]),
              reads=["pO", "ut0"], writes=["tmpb"])
            ch = h // 2
            if h % 2 == 0:
                E("vector", lambda v, ch=ch, g2=g2: v.tensor_add(
                    csl(X, ch, g2 * 512, g2 * 512 + 512)[0:64, :],
                    csl(X, ch, g2 * 512, g2 * 512 + 512)[0:64, :], TMPB[:, 0:512]),
                  reads=["tmpb", "x"], writes=["x"])
            else:
                E("tensor", lambda t: t.matmul(
                    PBk[64:128, 0:512], eye_b[0:64, 0:64], TMPB[:, 0:512],
                    start=True, stop=True),
                  reads=["tmpb", "consts_b"], writes=["pBk"])
                E("vector", lambda v, ch=ch, g2=g2: v.tensor_add(
                    csl(X, ch, g2 * 512, g2 * 512 + 512)[64:128, :],
                    csl(X, ch, g2 * 512, g2 * 512 + 512)[64:128, :], PBk[64:128, 0:512]),
                  reads=["pBk", "x"], writes=["x"])


def build_launch3(consts):
    nc = bass.Bass("TRN2", target_bir_lowering=False)
    x1 = nc.dram_tensor("x1", [D, T], F32, kind="ExternalInput").ap()
    wq1 = nc.dram_tensor("l1_wq", [D, D], F32, kind="ExternalInput").ap()
    wk1 = nc.dram_tensor("l1_wk", [D, D], F32, kind="ExternalInput").ap()
    wv1 = nc.dram_tensor("l1_wv", [D, D], F32, kind="ExternalInput").ap()
    eqT = nc.dram_tensor("l1_eqT", [64, 129], F32, kind="ExternalInput").ap()
    ekT = nc.dram_tensor("l1_ekT", [64, 129], F32, kind="ExternalInput").ap()
    ekTrev = nc.dram_tensor("l1_ekTrev", [64, 129], F32, kind="ExternalInput").ap()
    ev0 = nc.dram_tensor("l1_ev0", [1, 64], F32, kind="ExternalInput").ap()
    ev128 = nc.dram_tensor("l1_ev128", [1, 64], F32, kind="ExternalInput").ap()
    evb2a = nc.dram_tensor("l1_evb2a", [128, 64], F32, kind="ExternalInput").ap()
    evb2b = nc.dram_tensor("l1_evb2b", [127, 64], F32, kind="ExternalInput").ap()
    caps = make_const_inputs(nc, consts)

    x2p = nc.dram_tensor("x2p", [D, T], F32, kind="ExternalOutput").ap()
    xnb_out = nc.dram_tensor("xnb1_out", [D, T], BF16, kind="ExternalOutput").ap()

    d_qepad = nc.dram_tensor("d_qepad", [T, 512], BF16, kind="Internal").ap()
    d_kepad = nc.dram_tensor("d_kepad", [T, 512], F32, kind="Internal").ap()
    d_pbd = nc.dram_tensor("d_pbd", [T + 256, 384], BF16, kind="Internal").ap()

    import contextlib
    with contextlib.ExitStack() as ec:
        e = ec.enter_context
        X = e(nc.sbuf_tensor("X", [128, NCHUNK * T], F32))
        XNB = e(nc.sbuf_tensor("XNB", [128, NCHUNK * T], BF16))
        PT = e(nc.sbuf_tensor("PTS", [128, 16 * 512], BF16))
        Q1 = e(nc.sbuf_tensor("Q1", [64, T], BF16))
        K1 = e(nc.sbuf_tensor("K1", [64, T], BF16))
        V1 = e(nc.sbuf_tensor("V1", [64, T], BF16))
        VT1 = e(nc.sbuf_tensor("VT1", [128, 1040], BF16))
        VTLO = e(nc.sbuf_tensor("VTLO", [128, 1040], BF16))
        VTHI = e(nc.sbuf_tensor("VTHI", [128, 1040], BF16))
        WH = e(nc.sbuf_tensor("WH", [128, 3072], BF16))
        AUG = e(nc.sbuf_tensor("AUG", [1, 8192], BF16))
        ONER = e(nc.sbuf_tensor("ONER", [1, T], BF16))
        PADB = e(nc.sbuf_tensor("PADB", [128, 512], BF16))
        QEF = e(nc.sbuf_tensor("QEF", [128, 512], F32))
        EVBC = e(nc.sbuf_tensor("EVBC", [128, 128], BF16))
        TMPB = e(nc.sbuf_tensor("TMPB", [64, 512], BF16))
        SKB = e(nc.sbuf_tensor("SKB", [128, 1280], BF16))
        SKF = e(nc.sbuf_tensor("SKF", [128, 256], F32))
        PBTM = e(nc.sbuf_tensor("PBTM", [128, 1536], BF16))
        PPROW = e(nc.sbuf_tensor("PPROW", [1, 129], F32))
        UTIL = e(nc.sbuf_tensor("UTIL", [128, 1024], F32))
        U72 = e(nc.sbuf_tensor("U72", [72, 512], F32))
        STAT = e(nc.sbuf_tensor("STAT", [1, 2048], F32))
        ones_f32 = e(nc.sbuf_tensor("ones_f32", [128, 128], F32))
        ones_b_s = e(nc.sbuf_tensor("ones_b_s", [128, 128], BF16))
        eye_b_s = e(nc.sbuf_tensor("eye_b_s", [128, 128], BF16))
        scl_s = e(nc.sbuf_tensor("scl_s", [1, 16], F32))
        eqT_f = e(nc.sbuf_tensor("eqT_f", [64, 129], F32))
        ekT_f = e(nc.sbuf_tensor("ekT_f", [64, 129], F32))
        eqT_b = e(nc.sbuf_tensor("eqT_b", [64, 129], BF16))
        ekT_b = e(nc.sbuf_tensor("ekT_b", [64, 129], BF16))
        ekTrev_b = e(nc.sbuf_tensor("ekTrev_b", [64, 129], BF16))
        ev0_b = e(nc.sbuf_tensor("ev0_b", [1, 64], BF16))
        ev128_b = e(nc.sbuf_tensor("ev128_b", [1, 64], BF16))
        evb2a_b = e(nc.sbuf_tensor("evb2a_b", [128, 64], BF16))
        evb2b_b = e(nc.sbuf_tensor("evb2b_b", [127, 64], BF16))
        PA = e(nc.psum_tensor("PA", [128, 512], F32))
        PBk = e(nc.psum_tensor("PBk", [128, 512], F32))
        PC = e(nc.psum_tensor("PC", [128, 512], F32))
        PD = e(nc.psum_tensor("PD", [128, 512], F32))
        PO = e(nc.psum_tensor("PO", [128, 512], F32))
        PR = e(nc.psum_tensor("PR", [128, 512], F32))
        PBB = e(nc.psum_tensor("PBB", [128, 1024], BF16))

        kb = KB(nc)
        E = kb.emit
        C = dict(
            UTIL=UTIL, U72=U72, STAT=STAT, ones_f32=ones_f32, ones_b_s=ones_b_s,
            eye_b_s=eye_b_s, scl_s=scl_s,
            Q1=Q1, K1=K1, V1=V1, VT1=VT1, VTLO=VTLO, VTHI=VTHI, PT=PT, WH=WH,
            AUG=AUG, ONER=ONER, PADB=PADB, QEF=QEF, EVBC=EVBC, TMPB=TMPB,
            SKB=SKB, SKF=SKF, PBTM=PBTM, PPROW=PPROW,
            eqT_f=eqT_f, ekT_f=ekT_f, eqT_b=eqT_b, ekT_b=ekT_b, ekTrev_b=ekTrev_b,
            ev0_b=ev0_b, ev128_b=ev128_b, evb2a_b=evb2a_b, evb2b_b=evb2b_b,
            PA=PA, PBk=PBk, PC=PC, PD=PD, PO=PO, PR=PR, PBB=PBB,
            d_qepad=d_qepad, d_kepad=d_kepad, d_pbd=d_pbd,
            in_wq1=wq1, in_wk1=wk1, in_wv1=wv1,
        )

        for c in range(NCHUNK):
            E("sync", lambda g, c=c: g.dma_start(
                out=X[:, c * T:(c + 1) * T], in_=x1[c * 128:(c + 1) * 128, :]),
              writes=["x"], dma="x")
        E("sync", lambda g: g.dma_start(out=ones_f32[:], in_=caps["ones_f"]),
          writes=["consts"], dma="c")
        E("sync", lambda g: g.dma_start(out=scl_s[:], in_=caps["scl"]),
          writes=["consts"], dma="c")
        E("sync", lambda g: g.dma_start(out=eqT_f[:], in_=eqT), writes=["consts"], dma="c")
        E("sync", lambda g: g.dma_start(out=ekT_f[:], in_=ekT), writes=["consts"], dma="c")
        for sb, srcd in ((ones_b_s, caps["ones_b"]), (eye_b_s, caps["eye_b"])):
            E("sync", lambda g, sb=sb, srcd=srcd: g.dma_start(out=sb[:], in_=srcd),
              writes=["consts_b"], dma="c")
        for sb, srcd in ((eqT_b, eqT), (ekT_b, ekT), (ekTrev_b, ekTrev),
                         (ev0_b, ev0), (ev128_b, ev128), (evb2a_b, evb2a), (evb2b_b, evb2b)):
            E("gpsimd", lambda g, sb=sb, srcd=srcd: g.dma_start(out=sb[:], in_=srcd),
              writes=["consts_b"], dma="cb")

        emit_ln(kb, C, "x", "xnb", X, XNB)
        emit_l1a(kb, C, XNB, X)

        for c in range(NCHUNK):
            E("sync", lambda g, c=c: g.dma_start(
                out=x2p[c * 128:(c + 1) * 128, :], in_=X[:, c * T:(c + 1) * T]),
              reads=["x"], dma="st", final=True)
            E("sync", lambda g, c=c: g.dma_start(
                out=xnb_out[c * 128:(c + 1) * 128, :], in_=XNB[:, c * T:(c + 1) * T]),
              reads=["xnb"], dma="st", final=True)
        kb.build()
    return nc


def emit_l1f(kb, C, XI):
    import os
    FST = os.environ.get("F_STAGES", "av")
    FSUB = os.environ.get("F_SUB", "qaw")
    NL = int(os.environ.get("F_NL", "2"))
    E = kb.emit
    PA, PBk, PC, PD, PO, PR, PBB = (C["PA"], C["PBk"], C["PC"], C["PD"],
                                    C["PO"], C["PR"], C["PBB"])
    UT = C["UTIL"]
    HBS = C["HBS"]
    H2S = C["H2S"]
    QBc, KBc, VBc = C["QBc"], C["KBc"], C["VBc"]
    VT1 = C["VT1"]
    PT = C["PT"]
    AOC = C["AOC"]
    UU = C["UU"]
    WQ3 = C["WQ3"]
    WOS = C["WOS"]
    W1S = C["W1S"]
    W2F = C["W2F"]
    GRIDS = C["GRIDS"]
    TMPB = C["TMPB"]
    U72 = C["U72"]
    eye_b = C["eye_b_s"]
    ones = C["ones_f32"]
    FMA = C["FMA"]
    FMM = C["FMM"]
    blockmap = C["blockmap_s"]
    prothalf = C["prothalf_s"]

    def shift_tile(dst, c, tg, dkey):
        base = c * T + tg * 512
        if c < 4:
            E("vector", lambda v, dst=dst, base=base: v.tensor_copy(
                dst, XI[:, base: base + 512]), reads=["xi"], writes=[dkey])
        elif tg == 0:
            E("vector", lambda v, dst=dst, base=base: v.tensor_copy(
                dst[:, 1:512], XI[:, base: base + 511]), reads=["xi"], writes=[dkey])
            E("vector", lambda v, dst=dst: v.memset(dst[:, 0:1], 0.0), writes=[dkey])
        else:
            E("vector", lambda v, dst=dst, base=base: v.tensor_copy(
                dst, XI[:, base - 1: base + 511]), reads=["xi"], writes=[dkey])

    for l in range(NL):
        lw = C[f"fw{l}"]
        for nm, sb in (("b1a", C["B1A"]), ("b1g", C["B1G"]), ("b2", C["B2C"]),
                       ("lng", C["LNG"]), ("lnb", C["LNB"]), ("qs", C["QSC"]), ("ks", C["KSC"])):
            E("sync", lambda g, nm=nm, sb=sb, lw=lw: g.dma_start(out=sb[:, :], in_=lw[nm]),
              writes=["lcols"], dma="lc")
        E("sync", lambda g, lw=lw: g.dma_start(out=C["QROW"][0:1, 0:64], in_=lw["qsr"]),
          writes=["qrow"], dma="lc")
        E("sync", lambda g, lw=lw: g.dma_start(out=C["QROW"][0:1, 64:128], in_=lw["ksr"]),
          writes=["qrow"], dma="lc")
        E("vector", lambda v: v.reduce_max(UT[0:1, 0:1], C["QROW"][0:1, 0:64],
                                           axis=mybir.AxisListType.X),
          reads=["qrow"], writes=["ut0"])
        E("vector", lambda v: v.reduce_max(UT[0:1, 1:2], C["QROW"][0:1, 64:128],
                                           axis=mybir.AxisListType.X),
          reads=["qrow"], writes=["ut0"])
        E("vector", lambda v: v.tensor_mul(UT[0:1, 2:3], UT[0:1, 0:1], UT[0:1, 1:2]),
          reads=["ut0"], writes=["ut0"])
        E("vector", lambda v: v.tensor_scalar(out=UT[0:1, 3:4], in0=UT[0:1, 2:3],
                                              scalar1=-8.0, scalar2=None, op0=ALU.mult),
          reads=["ut0"], writes=["ut0"])
        E("tensor", lambda t: t.matmul(PR[:, 0:1], ones[0:1, :], UT[0:1, 3:4],
                                       start=True, stop=True),
          reads=["ut0", "consts"], writes=["pR"])
        E("vector", lambda v: v.tensor_copy(C["NEGC"][:, 0:1], PR[:, 0:1]),
          reads=["pR"], writes=["negcc"])
        E("gpsimd", lambda g, lw=lw: g.dma_start(
            out=W2F[:, :].rearrange("p (a o) -> p a o", a=22),
            in_=lw["w2"].rearrange("(a p) o -> p a o", p=128)),
          writes=["w2f"], dma="wlw")

        for fc in range(4 if "a" in FST else 0):
            ws = (fc % 2) if C["WQ3"].shape[1] >= 6144 else 0
            for pi, nm in ((0, "wq"), (1, "wk"), (2, "wv")):
                E("gpsimd", lambda g, pi=pi, nm=nm, fc=fc, ws=ws, lw=lw: g.dma_start(
                    out=WQ3[:, ws * 3072 + pi * 1024: ws * 3072 + pi * 1024 + 1024]
                        .rearrange("p (a o) -> p a o", a=8),
                    in_=lw[nm].rearrange("(a p) o -> p a o", p=128)
                        [:, :, fc * 128: fc * 128 + 128]),
                  writes=[f"wq3{ws}{pi}"], dma=f"wlq{ws}")
            E("gpsimd", lambda g, fc=fc, lw=lw: g.dma_start(
                out=WOS[:, :], in_=lw["wo"][fc * 128: fc * 128 + 128, :]),
              writes=["wos"], dma="wo")
            for tg in range(TG):
                for c in range(NCHUNK):
                    shift_tile(HBS[:, c * 512:(c + 1) * 512], c, tg, f"hbs{c}")
                for qk in range(2):
                    E("sync", lambda g, qk=qk, tg=tg: g.dma_start(
                        out=GRIDS[:, 0:512],
                        in_=(C["c_xposq"] if qk == 0 else C["c_xposk"])[:, tg * 512: tg * 512 + 512]),
                      writes=["grid"], dma="gr")
                    E("sync", lambda g, qk=qk, tg=tg: g.dma_start(
                        out=GRIDS[:, 512:1024],
                        in_=(C["c_xposq"] if qk == 0 else C["c_xposk"])[:, 2048 + tg * 512: 2048 + tg * 512 + 512]),
                      writes=["grid"], dma="gr")
                    SCL = C["QSC"] if qk == 0 else C["KSC"]
                    DSTB = QBc if qk == 0 else KBc
                    dkey = "qbc" if qk == 0 else "kbc"
                    bk, pb = (("pC", PC) if qk == 0 else ("pD", PD))
                    for c in range(NCHUNK):
                        E("tensor", lambda t, qk=qk, c=c, ws=ws, pb=pb: t.matmul(
                            pb[:, 0:512],
                            WQ3[:, ws * 3072 + qk * 1024 + c * 128: ws * 3072 + qk * 1024 + c * 128 + 128],
                            HBS[:, c * 512:(c + 1) * 512],
                            start=(c == 0), stop=(c == NCHUNK - 1)),
                          reads=[f"wq3{ws}{qk}", f"hbs{c}"], writes=[bk])
                    if "m" == os.environ.get("F_CHAIN", ""):
                        E("vector", lambda v, tg=tg, pb=pb, DSTB=DSTB, dkey=dkey: v.tensor_copy(
                            DSTB[:, tg * 512: tg * 512 + 512], pb[:, 0:512]),
                          reads=[bk], writes=[dkey])
                        continue
                    FCH = os.environ.get("F_CHAIN", "")
                    sq = UT[:, 0:512]
                    E("scalar", lambda s, pb=pb, sq=sq: s.activation(
                        out=sq, in_=pb[:, 0:512], func=AF.Square),
                      reads=[bk], writes=["ut0"])
                    qt = UT[:, 512:1024]
                    E("vector", lambda v, pb=pb, qt=qt: v.tensor_copy(qt, pb[:, 0:512]),
                      reads=[bk], writes=["ut1"])
                    if FCH == "1":
                        E("vector", lambda v, tg=tg, qt=qt, DSTB=DSTB, dkey=dkey: v.tensor_copy(
                            DSTB[:, tg * 512: tg * 512 + 512], qt),
                          reads=["ut1", "ut0"], writes=[dkey])
                        continue
                    E("tensor", lambda t, sq=sq: t.matmul(
                        PA[:, 0:512], blockmap[:, :], sq, start=True, stop=True),
                      reads=["ut0", "consts"], writes=["pA"])
                    rn = H2S[:, 0:512]
                    E("vector", lambda v, rn=rn: v.tensor_scalar(
                        out=rn, in0=PA[:, 0:512], scalar1=1e-24, scalar2=None, op0=ALU.max),
                      reads=["pA"], writes=["h2s0"])
                    if FCH == "2":
                        E("vector", lambda v, tg=tg, rn=rn, DSTB=DSTB, dkey=dkey: v.tensor_copy(
                            DSTB[:, tg * 512: tg * 512 + 512], rn),
                          reads=["h2s0", "ut1"], writes=[dkey])
                        continue
                    E("scalar", lambda s, rn=rn: s.activation(out=rn, in_=rn, func=AF.Ln),
                      reads=["h2s0"], writes=["h2s0"])
                    E("scalar", lambda s, rn=rn: s.activation(out=rn, in_=rn, func=AF.Exp,
                                                              scale=-0.5),
                      reads=["h2s0"], writes=["h2s0"])
                    E("vector", lambda v, rn=rn, SCL=SCL: v.tensor_scalar(
                        out=rn, in0=rn, scalar1=SCL[:, 0:1], scalar2=None, op0=ALU.mult),
                      reads=["h2s0", "lcols"], writes=["h2s0"])
                    E("vector", lambda v, qt=qt, rn=rn: v.tensor_mul(qt, qt, rn),
                      reads=["ut1", "h2s0"], writes=["ut1"])
                    if "n" == os.environ.get("F_CHAIN", ""):
                        E("vector", lambda v, tg=tg, qt=qt, DSTB=DSTB, dkey=dkey: v.tensor_copy(
                            DSTB[:, tg * 512: tg * 512 + 512], qt),
                          reads=["ut1"], writes=[dkey])
                        continue
                    E("tensor", lambda t, qt=qt: t.matmul(
                        PA[:, 0:512], prothalf[:, :], qt, start=True, stop=True),
                      reads=["ut1", "consts"], writes=["pA"])
                    tm2 = H2S[:, 512:1024]
                    E("vector", lambda v, tm2=tm2: v.tensor_mul(
                        tm2, PA[:, 0:512], GRIDS[:, 512:1024]),
                      reads=["pA", "grid"], writes=["h2s1"])
                    E("vector", lambda v, qt=qt: v.tensor_mul(qt, qt, GRIDS[:, 0:512]),
                      reads=["ut1", "grid"], writes=["ut1"])
                    E("vector", lambda v, tg=tg, qt=qt, tm2=tm2, DSTB=DSTB, dkey=dkey: v.tensor_add(
                        DSTB[:, tg * 512: tg * 512 + 512], qt, tm2),
                      reads=["ut1", "h2s1"], writes=[dkey])
                bk, pb = ("pC", PC)
                for c in range(NCHUNK):
                    E("tensor", lambda t, c=c, ws=ws, pb=pb: t.matmul(
                        pb[:, 0:512],
                        WQ3[:, ws * 3072 + 2 * 1024 + c * 128: ws * 3072 + 2 * 1024 + c * 128 + 128],
                        HBS[:, c * 512:(c + 1) * 512],
                        start=(c == 0), stop=(c == NCHUNK - 1)),
                      reads=[f"wq3{ws}2", f"hbs{c}"], writes=[bk])
                E("vector", lambda v, tg=tg, pb=pb: v.tensor_copy(
                    VBc[:, tg * 512: tg * 512 + 512], pb[:, 0:512]),
                  reads=[bk], writes=["vbc"])
            for hh in range(2 if "a" in FSUB else 0):
                hb = 64 * hh
                E("vector", lambda v: v.memset(VT1[:], 1.0), writes=["vt1"])
                for mc in range(16):
                    E("tensor", lambda t, hb=hb, mc=mc: t.transpose(
                        PBB[:, 0:64], VBc[hb:hb + 64, mc * 128: mc * 128 + 128],
                        eye_b[hb:hb + 64, hb:hb + 64]),
                      reads=["vbc", "consts_b"], writes=["pBB"])
                    E("vector", lambda v, mc=mc: v.tensor_copy(
                        VT1[:, mc * 65: mc * 65 + 64], PBB[:, 0:64]),
                      reads=["pBB"], writes=["vt1"])
                for w in range(4):
                    a0 = 0 if w > 0 else 4
                    for ai, a in enumerate(range(a0, 8)):
                        mc = 4 * (w - 1) + a
                        bk, pb = (("pC", PC) if a % 2 == 0 else ("pD", PD))
                        sl = (a % 2) * 512
                        E("tensor", lambda t, hb=hb, mc=mc, w=w, pb=pb: t.matmul(
                            pb[:, 0:512],
                            KBc[hb:hb + 64, mc * 128: mc * 128 + 128],
                            QBc[hb:hb + 64, w * 512: w * 512 + 512],
                            start=True, stop=True),
                          reads=["kbc", "qbc"], writes=[bk])
                        if a >= 5:
                            E("vector", lambda v, a=a, pb=pb: v.tensor_add(
                                pb[:, 0:512], pb[:, 0:512],
                                FMA[:, (a - 5) * 512: (a - 5) * 512 + 512]),
                              reads=[bk, "consts_b"], writes=[bk])
                        E("scalar", lambda s, sl=sl, pb=pb: s.activation(
                            out=PT[:, sl: sl + 512], in_=pb[:, 0:512],
                            func=AF.Exp, scale=8.0, bias=C["NEGC"][:, 0:1]),
                          reads=[bk, "negcc"], writes=[f"pt{sl}"])
                        if a <= 4:
                            E("vector", lambda v, a=a, sl=sl: v.tensor_mul(
                                PT[:, sl: sl + 512], PT[:, sl: sl + 512],
                                FMM[:, a * 512: a * 512 + 512]),
                              reads=[f"pt{sl}", "consts_b"], writes=[f"pt{sl}"])
                        E("tensor", lambda t, sl=sl, mc=mc, st=(ai == 0), sp=(a == 7): t.matmul(
                            PO[0:65, 0:512], VT1[:, mc * 65: mc * 65 + 65],
                            PT[:, sl: sl + 512],
                            start=st, stop=sp),
                          reads=["vt1", f"pt{sl}"], writes=["pO"])
                    E("vector", lambda v: v.reciprocal(U72[64:65, 0:512], PO[64:65, 0:512]),
                      reads=["pO"], writes=["u72"])
                    E("tensor", lambda t: t.matmul(
                        PR[0:64, 0:512], ones[64:65, 0:64], U72[64:65, 0:512],
                        start=True, stop=True),
                      reads=["u72", "consts"], writes=["pR"])
                    E("vector", lambda v: v.tensor_copy(UT[0:64, 0:512], PR[0:64, 0:512]),
                      reads=["pR"], writes=["ut0"])
                    E("vector", lambda v: v.tensor_mul(TMPB[:, 0:512], PO[0:64, 0:512],
                                                       UT[0:64, 0:512]),
                      reads=["pO", "ut0"], writes=["tmpb"])
                    if hh == 0:
                        E("vector", lambda v, w=w: v.tensor_copy(
                            AOC[0:64, w * 512: w * 512 + 512], TMPB[:, 0:512]),
                          reads=["tmpb"], writes=["aoc"])
                    else:
                        E("tensor", lambda t: t.matmul(
                            PBk[64:128, 0:512], eye_b[0:64, 0:64], TMPB[:, 0:512],
                            start=True, stop=True),
                          reads=["tmpb", "consts_b"], writes=["pBk"])
                        E("vector", lambda v, w=w: v.tensor_copy(
                            AOC[64:128, w * 512: w * 512 + 512], PBk[64:128, 0:512]),
                          reads=["pBk"], writes=["aoc"])
            for og in range(NCHUNK if "w" in FSUB else 0):
                for tg in range(TG):
                    bk, pb = (("pC", PC) if tg % 2 == 0 else ("pD", PD))
                    E("tensor", lambda t, og=og, tg=tg, pb=pb: t.matmul(
                        pb[:, 0:512], WOS[:, og * 128: og * 128 + 128],
                        AOC[:, tg * 512: tg * 512 + 512],
                        start=True, stop=True),
                      reads=["wos", "aoc"], writes=[bk])
                    E("vector", lambda v, og=og, tg=tg, pb=pb, bk=bk: v.tensor_add(
                        csl(XI, og, tg * 512, tg * 512 + 512), pb[:, 0:512],
                        csl(XI, og, tg * 512, tg * 512 + 512)),
                      reads=[bk, "xi"], writes=["xi"])
        for tg in range(TG if "v" in FST else 0):
            for c in range(NCHUNK):
                sl = H2S[:, (c % 2) * 512: (c % 2) * 512 + 512]
                shift_tile(sl, c, tg, f"h2s{c % 2}")
                E("tensor", lambda t, c=c, sl=sl: t.matmul(
                    PA[0:1, 0:512], ones[:, 0:1], sl,
                    start=(c == 0), stop=(c == NCHUNK - 1)),
                  reads=[f"h2s{c % 2}", "consts"], writes=["pA"])
                sq = UT[:, 0:512]
                E("scalar", lambda s, sl=sl, sq=sq: s.activation(out=sq, in_=sl, func=AF.Square),
                  reads=[f"h2s{c % 2}"], writes=["ut0"])
                E("tensor", lambda t, c=c, sq=sq: t.matmul(
                    PBk[0:1, 0:512], ones[:, 0:1], sq,
                    start=(c == 0), stop=(c == NCHUNK - 1)),
                  reads=["ut0", "consts"], writes=["pB"])
            mu = UT[0:1, 0:512]
            va = UT[0:1, 512:1024]
            E("scalar", lambda s, mu=mu: s.mul(mu, PA[0:1, 0:512], 1.0 / D),
              reads=["pA"], writes=["ut0"])
            E("scalar", lambda s, va=va: s.mul(va, PBk[0:1, 0:512], 1.0 / D),
              reads=["pB"], writes=["ut1"])
            E("vector", lambda v, mu=mu, va=va: v.scalar_tensor_tensor(
                out=va, in0=mu, scalar=0.0, in1=va, op0=ALU.mult, op1=ALU.bypass)
              if False else
              v.tensor_copy(H2S[0:1, 0:512], mu),
              reads=["ut0"], writes=["h2s0"])
            E("scalar", lambda s: s.activation(out=H2S[0:1, 512:1024], in_=H2S[0:1, 0:512],
                                               func=AF.Square),
              reads=["h2s0"], writes=["h2s1"])
            E("vector", lambda v, va=va: v.tensor_sub(va, va, H2S[0:1, 512:1024]),
              reads=["ut1", "h2s1"], writes=["ut1"])
            E("scalar", lambda s, va=va: s.activation(out=va, in_=va, func=AF.Ln,
                                                      bias=C["scl_s"][0:1, 0:1]),
              reads=["ut1", "consts"], writes=["ut1"])
            E("scalar", lambda s, va=va: s.activation(out=va, in_=va, func=AF.Exp, scale=-0.5),
              reads=["ut1"], writes=["ut1"])
            E("tensor", lambda t, mu=mu: t.matmul(PC[:, 0:512], ones[0:1, :], mu,
                                                  start=True, stop=True),
              reads=["ut0", "consts"], writes=["pC"])
            E("tensor", lambda t, va=va: t.matmul(PD[:, 0:512], ones[0:1, :], va,
                                                  start=True, stop=True),
              reads=["ut1", "consts"], writes=["pD"])
            for c in range(NCHUNK):
                sl = H2S[:, (c % 2) * 512: (c % 2) * 512 + 512]
                shift_tile(sl, c, tg, f"h2s{c % 2}")
                E("vector", lambda v, sl=sl: v.tensor_sub(sl, sl, PC[:, 0:512]),
                  reads=[f"h2s{c % 2}", "pC"], writes=[f"h2s{c % 2}"])
                E("vector", lambda v, sl=sl: v.tensor_mul(sl, sl, PD[:, 0:512]),
                  reads=[f"h2s{c % 2}", "pD"], writes=[f"h2s{c % 2}"])
                E("scalar", lambda s, c=c, sl=sl: s.activation(
                    out=HBS[:, c * 512:(c + 1) * 512], in_=sl,
                    func=AF.Identity, scale=C["LNG"][:, c:c + 1], bias=C["LNB"][:, c:c + 1]),
                  reads=[f"h2s{c % 2}", "lcols"], writes=[f"hbs{c}"])
            for hc in range(22):
                ws = hc % 2
                for sb, nm, kk in ((0, "w1a", "w1a"), (1, "w1g", "w1g")):
                    E("gpsimd", lambda g, hc=hc, ws=ws, sb=sb, nm=nm, lw=lw: g.dma_start(
                        out=W1S[:, (ws * 2 + sb) * 1024: (ws * 2 + sb) * 1024 + 1024]
                            .rearrange("p (a o) -> p a o", a=8),
                        in_=lw[nm].rearrange("(a p) o -> p a o", p=128)
                            [:, :, hc * 128: hc * 128 + 128]),
                      writes=[f"{kk}{ws}"], dma=f"wf{ws}")
                for c in range(NCHUNK):
                    E("tensor", lambda t, ws=ws, c=c: t.matmul(
                        PA[:, 0:512],
                        W1S[:, (ws * 2) * 1024 + c * 128: (ws * 2) * 1024 + c * 128 + 128],
                        HBS[:, c * 512:(c + 1) * 512],
                        start=(c == 0), stop=(c == NCHUNK - 1)),
                      reads=[f"w1a{ws}", f"hbs{c}"], writes=["pA"])
                for c in range(NCHUNK):
                    E("tensor", lambda t, ws=ws, c=c: t.matmul(
                        PBk[:, 0:512],
                        W1S[:, (ws * 2 + 1) * 1024 + c * 128: (ws * 2 + 1) * 1024 + c * 128 + 128],
                        HBS[:, c * 512:(c + 1) * 512],
                        start=(c == 0), stop=(c == NCHUNK - 1)),
                      reads=[f"w1g{ws}", f"hbs{c}"], writes=["pB"])
                gel = UT[:, 0:512]
                E("scalar", lambda s, hc=hc, gel=gel: s.activation(
                    out=gel, in_=PBk[:, 0:512], func=AF.Gelu, bias=C["B1G"][:, hc:hc + 1]),
                  reads=["pB", "lcols"], writes=["ut0"])
                E("vector", lambda v, hc=hc, gel=gel: v.scalar_tensor_tensor(
                    out=UU[:, hc * 512: hc * 512 + 512], in0=PA[:, 0:512],
                    scalar=C["B1A"][:, hc:hc + 1], in1=gel,
                    op0=ALU.add, op1=ALU.mult),
                  reads=["pA", "lcols", "ut0"], writes=[f"uu{hc}"])
            for og in range(NCHUNK):
                bk, pb = (("pC", PC) if og % 2 == 0 else ("pD", PD))
                for hc in range(22):
                    E("tensor", lambda t, og=og, hc=hc, pb=pb: t.matmul(
                        pb[:, 0:512],
                        W2F[:, hc * 1024 + og * 128: hc * 1024 + og * 128 + 128],
                        UU[:, hc * 512: hc * 512 + 512],
                        start=(hc == 0), stop=(hc == 21)),
                      reads=["w2f", f"uu{hc}"], writes=[bk])
                E("vector", lambda v, og=og, tg=tg, pb=pb, bk=bk: v.scalar_tensor_tensor(
                    out=csl(XI, og, tg * 512, tg * 512 + 512),
                    in0=pb[:, 0:512], scalar=C["B2C"][:, og:og + 1],
                    in1=csl(XI, og, tg * 512, tg * 512 + 512),
                    op0=ALU.add, op1=ALU.add),
                  reads=[bk, "lcols", "xi"], writes=["xi"])


def build_launch4(consts):
    nc = bass.Bass("TRN2", target_bir_lowering=False)
    x1 = nc.dram_tensor("x1", [D, T], F32, kind="ExternalInput").ap()
    x2p = nc.dram_tensor("x2p", [D, T], F32, kind="ExternalInput").ap()
    fwin = {}
    for l in range(2):
        for nm, shp in (("wq", [D, 512]), ("wk", [D, 512]), ("wv", [D, 512]),
                        ("wo", [512, D]), ("w1a", [D, 2816]), ("w1g", [D, 2816]),
                        ("w2", [2816, D]), ("b1a", [128, 22]), ("b1g", [128, 22]),
                        ("b2", [128, 8]), ("lng", [128, 8]), ("lnb", [128, 8]),
                        ("qs", [128, 1]), ("ks", [128, 1]),
                        ("qsr", [1, 64]), ("ksr", [1, 64])):
            fwin.setdefault(l, {})[nm] = nc.dram_tensor(
                f"f_{nm}{l}", shp, F32, kind="ExternalInput").ap()
    caps = make_const_inputs(nc, consts)

    out = nc.dram_tensor("out_final", [D, T], F32, kind="ExternalOutput").ap()
    import os as _os
    dbg = {}
    if _os.environ.get("F_DBG"):
        for nm in ("QBc", "KBc", "VBc", "AOC"):
            dbg[nm] = nc.dram_tensor("dbg_" + nm, [128, T], BF16, kind="ExternalOutput").ap()

    import contextlib
    with contextlib.ExitStack() as ec:
        e = ec.enter_context
        X = e(nc.sbuf_tensor("X", [128, NCHUNK * T], F32))
        HBS = e(nc.sbuf_tensor("HBS", [128, 8 * 512], BF16))
        H2S = e(nc.sbuf_tensor("H2S", [128, 2 * 512], F32))
        QBc = e(nc.sbuf_tensor("QBc", [128, T], BF16))
        KBc = e(nc.sbuf_tensor("KBc", [128, T], BF16))
        VBc = e(nc.sbuf_tensor("VBc", [128, T], BF16))
        VT1 = e(nc.sbuf_tensor("VT1", [128, 1040], BF16))
        PT = e(nc.sbuf_tensor("PTS", [128, 2 * 512], BF16))
        AOC = e(nc.sbuf_tensor("AOC", [128, T], BF16))
        UU = e(nc.sbuf_tensor("UU", [128, 22 * 512], BF16))
        WQ3 = e(nc.sbuf_tensor("WQ3", [128, 2 * 3072], BF16))
        WOS = e(nc.sbuf_tensor("WOS", [128, 1024], BF16))
        W1S = e(nc.sbuf_tensor("W1S", [128, 4096], BF16))
        W2F = e(nc.sbuf_tensor("W2F", [128, 22 * 1024], BF16))
        GRIDS = e(nc.sbuf_tensor("GRIDS", [128, 1024], F32))
        TMPB = e(nc.sbuf_tensor("TMPB", [64, 512], BF16))
        FMA = e(nc.sbuf_tensor("FMA", [128, 3 * 512], BF16))
        FMM = e(nc.sbuf_tensor("FMM", [128, 5 * 512], BF16))
        UTIL = e(nc.sbuf_tensor("UTIL", [128, 1024], F32))
        U72 = e(nc.sbuf_tensor("U72", [72, 512], F32))
        QROW = e(nc.sbuf_tensor("QROW", [1, 128], F32))
        NEGC = e(nc.sbuf_tensor("NEGC", [128, 1], F32))
        B1A = e(nc.sbuf_tensor("B1A", [128, 22], F32))
        B1G = e(nc.sbuf_tensor("B1G", [128, 22], F32))
        B2C = e(nc.sbuf_tensor("B2C", [128, 8], F32))
        LNG = e(nc.sbuf_tensor("LNG", [128, 8], F32))
        LNB = e(nc.sbuf_tensor("LNB", [128, 8], F32))
        QSC = e(nc.sbuf_tensor("QSC", [128, 1], F32))
        KSC = e(nc.sbuf_tensor("KSC", [128, 1], F32))
        STG = e(nc.sbuf_tensor("STG", [128, 512], F32))
        ones_f32 = e(nc.sbuf_tensor("ones_f32", [128, 128], F32))
        eye_b_s = e(nc.sbuf_tensor("eye_b_s", [128, 128], BF16))
        scl_s = e(nc.sbuf_tensor("scl_s", [1, 16], F32))
        blockmap_s = e(nc.sbuf_tensor("blockmap_s", [128, 128], F32))
        prothalf_s = e(nc.sbuf_tensor("prothalf_s", [128, 128], F32))
        PA = e(nc.psum_tensor("PA", [128, 512], F32))
        PBk = e(nc.psum_tensor("PBk", [128, 512], F32))
        PC = e(nc.psum_tensor("PC", [128, 512], F32))
        PD = e(nc.psum_tensor("PD", [128, 512], F32))
        PO = e(nc.psum_tensor("PO", [128, 512], F32))
        PR = e(nc.psum_tensor("PR", [128, 512], F32))
        PBB = e(nc.psum_tensor("PBB", [128, 1024], BF16))

        kb = KB(nc)
        E = kb.emit
        C = dict(
            UTIL=UTIL, U72=U72, ones_f32=ones_f32, eye_b_s=eye_b_s, scl_s=scl_s,
            blockmap_s=blockmap_s, prothalf_s=prothalf_s,
            HBS=HBS, H2S=H2S, QBc=QBc, KBc=KBc, VBc=VBc, VT1=VT1, PT=PT,
            AOC=AOC, UU=UU, WQ3=WQ3, WOS=WOS, W1S=W1S, W2F=W2F, GRIDS=GRIDS,
            TMPB=TMPB, FMA=FMA, FMM=FMM, QROW=QROW, NEGC=NEGC,
            B1A=B1A, B1G=B1G, B2C=B2C, LNG=LNG, LNB=LNB, QSC=QSC, KSC=KSC,
            PA=PA, PBk=PBk, PC=PC, PD=PD, PO=PO, PR=PR, PBB=PBB,
            c_xposq=caps["xposq"], c_xposk=caps["xposk"],
            fw0=fwin[0], fw1=fwin[1],
            STAT=None,
        )

        for c in range(NCHUNK):
            E("sync", lambda g, c=c: g.dma_start(
                out=X[:, c * T:(c + 1) * T], in_=x1[c * 128:(c + 1) * 128, :]),
              writes=["xi"], dma="x")
        E("sync", lambda g: g.dma_start(out=ones_f32[:], in_=caps["ones_f"]),
          writes=["consts"], dma="c")
        E("sync", lambda g: g.dma_start(out=scl_s[:], in_=caps["scl"]),
          writes=["consts"], dma="c")
        E("sync", lambda g: g.dma_start(out=blockmap_s[:], in_=caps["blockmap"]),
          writes=["consts"], dma="c")
        E("sync", lambda g: g.dma_start(out=prothalf_s[:], in_=caps["prothalf"]),
          writes=["consts"], dma="c")
        E("sync", lambda g: g.dma_start(out=eye_b_s[:], in_=caps["eye_b"]),
          writes=["consts_b"], dma="c")
        E("sync", lambda g: g.dma_start(out=FMM[:], in_=caps["fmaskmul"][:, 0:5 * 512]),
          writes=["consts_b"], dma="c")
        E("gpsimd", lambda g: g.dma_start(out=FMA[:], in_=caps["fmaskadd"][:, 5 * 512:8 * 512]),
          writes=["consts_b"], dma="cb")

        emit_ln_fp32_inplace(kb, C, X)
        emit_l1f(kb, C, X)

        if dbg:
            for nm, sb, kk2 in (("QBc", QBc, "qbc"), ("KBc", KBc, "kbc"),
                                ("VBc", VBc, "vbc"), ("AOC", AOC, "aoc")):
                E("sync", lambda g, nm=nm, sb=sb: g.dma_start(out=dbg[nm], in_=sb[:, :]),
                  reads=[kk2], dma="st", final=True)
        for c in range(NCHUNK):
            for tg in range(TG):
                E("sync", lambda g, c=c, tg=tg: g.dma_start(
                    out=STG[:, :], in_=x2p[c * 128:(c + 1) * 128, tg * 512:(tg + 1) * 512]),
                  writes=["stg"], dma="x2")
                E("vector", lambda v, c=c, tg=tg: v.tensor_add(
                    STG[:, :], STG[:, :], csl(X, c, tg * 512, tg * 512 + 512)),
                  reads=["stg", "xi"], writes=["stg"])
                E("sync", lambda g, c=c, tg=tg: g.dma_start(
                    out=out[c * 128:(c + 1) * 128, tg * 512:(tg + 1) * 512], in_=STG[:, :]),
                  reads=["stg"], dma="st", final=True)
        kb.build()
    return nc


def emit_ln_fp32_inplace(kb, C, X):
    E = kb.emit
    ones = C["ones_f32"]
    UT = C["UTIL"]
    PA, PBk, PC, PD = C["PA"], C["PBk"], C["PC"], C["PD"]
    for g in range(TG):
        for c in range(NCHUNK):
            E("tensor", lambda t, c=c, g=g: t.matmul(
                PA[0:1, :], ones[:, 0:1], csl(X, c, g * 512, g * 512 + 512),
                start=(c == 0), stop=(c == NCHUNK - 1)),
              reads=["xi", "consts"], writes=["pA"])
        for c in range(NCHUNK):
            sq = UT[:, 0:512]
            E("scalar", lambda s, c=c, g=g, sq=sq: s.activation(
                out=sq, in_=csl(X, c, g * 512, g * 512 + 512), func=AF.Square),
              reads=["xi"], writes=["ut0"])
            E("tensor", lambda t, sq=sq, c=c: t.matmul(
                PBk[0:1, :], ones[:, 0:1], sq,
                start=(c == 0), stop=(c == NCHUNK - 1)),
              reads=["ut0", "consts"], writes=["pB"])
        mu = UT[0:1, 512:1024]
        va = C["U72"][0:1, 0:512]
        E("scalar", lambda s, mu=mu: s.mul(mu, PA[0:1, 0:512], 1.0 / D),
          reads=["pA"], writes=["ut1"])
        E("scalar", lambda s, va=va: s.mul(va, PBk[0:1, 0:512], 1.0 / D),
          reads=["pB"], writes=["u72"])
        musq = C["U72"][1:2, 0:512] if False else None
        E("scalar", lambda s, mu=mu: s.activation(out=UT[0:1, 0:512], in_=mu, func=AF.Square),
          reads=["ut1"], writes=["ut0"])
        E("vector", lambda v, va=va: v.tensor_sub(va, va, UT[0:1, 0:512]),
          reads=["u72", "ut0"], writes=["u72"])
        E("scalar", lambda s, va=va: s.activation(out=va, in_=va, func=AF.Ln,
                                                  bias=C["scl_s"][0:1, 0:1]),
          reads=["u72", "consts"], writes=["u72"])
        E("scalar", lambda s, va=va: s.activation(out=va, in_=va, func=AF.Exp, scale=-0.5),
          reads=["u72"], writes=["u72"])
        E("tensor", lambda t, mu=mu: t.matmul(PC[:, 0:512], ones[0:1, :], mu,
                                              start=True, stop=True),
          reads=["ut1", "consts"], writes=["pC"])
        E("tensor", lambda t, va=va: t.matmul(PD[:, 0:512], ones[0:1, :], va,
                                              start=True, stop=True),
          reads=["u72", "consts"], writes=["pD"])
        for c in range(NCHUNK):
            E("vector", lambda v, c=c, g=g: v.tensor_sub(
                csl(X, c, g * 512, g * 512 + 512), csl(X, c, g * 512, g * 512 + 512),
                PC[:, 0:512]),
              reads=["xi", "pC"], writes=["xi"])
            E("vector", lambda v, c=c, g=g: v.tensor_mul(
                csl(X, c, g * 512, g * 512 + 512), csl(X, c, g * 512, g * 512 + 512),
                PD[:, 0:512]),
              reads=["xi", "pD"], writes=["xi"])




def build_fused(consts):
    nc = bass.Bass("TRN2", target_bir_lowering=False)
    xT = nc.dram_tensor("xT", [D, T], F32, kind="ExternalInput").ap()
    wqkvT = nc.dram_tensor("l0_wqkvT", [D, 3072], F32, kind="ExternalInput").ap()
    bqkv = nc.dram_tensor("l0_bqkv", [128, 24], F32, kind="ExternalInput").ap()
    wprojT = nc.dram_tensor("l0_wprojT", [D, D], F32, kind="ExternalInput").ap()
    bproj = nc.dram_tensor("l0_bproj", [128, 8], F32, kind="ExternalInput").ap()
    wg = nc.dram_tensor("moe_wg", [D, 32], F32, kind="ExternalInput").ap()
    w1m = nc.dram_tensor("moe_w1", [32, D, 512], F32, kind="ExternalInput").ap()
    w2m = nc.dram_tensor("moe_w2", [32, 512, D], F32, kind="ExternalInput").ap()
    w3m = nc.dram_tensor("moe_w3", [32, D, 512], F32, kind="ExternalInput").ap()
    ws1 = nc.dram_tensor("moe_ws1", [D, D], F32, kind="ExternalInput").ap()
    ws2 = nc.dram_tensor("moe_ws2", [D, D], F32, kind="ExternalInput").ap()
    ws3 = nc.dram_tensor("moe_ws3", [D, D], F32, kind="ExternalInput").ap()
    wq1 = nc.dram_tensor("l1_wq", [D, D], F32, kind="ExternalInput").ap()
    wk1 = nc.dram_tensor("l1_wk", [D, D], F32, kind="ExternalInput").ap()
    wv1 = nc.dram_tensor("l1_wv", [D, D], F32, kind="ExternalInput").ap()
    eqT = nc.dram_tensor("l1_eqT", [64, 129], F32, kind="ExternalInput").ap()
    ekT = nc.dram_tensor("l1_ekT", [64, 129], F32, kind="ExternalInput").ap()
    ekTrev = nc.dram_tensor("l1_ekTrev", [64, 129], F32, kind="ExternalInput").ap()
    ev0 = nc.dram_tensor("l1_ev0", [1, 64], F32, kind="ExternalInput").ap()
    ev128 = nc.dram_tensor("l1_ev128", [1, 64], F32, kind="ExternalInput").ap()
    evb2a = nc.dram_tensor("l1_evb2a", [128, 64], F32, kind="ExternalInput").ap()
    evb2b = nc.dram_tensor("l1_evb2b", [127, 64], F32, kind="ExternalInput").ap()
    fwin = {}
    for l in range(2):
        for nm, shp in (("wq", [D, 512]), ("wk", [D, 512]), ("wv", [D, 512]),
                        ("wo", [512, D]), ("w1a", [D, 2816]), ("w1g", [D, 2816]),
                        ("w2", [2816, D]), ("b1a", [128, 22]), ("b1g", [128, 22]),
                        ("b2", [128, 8]), ("lng", [128, 8]), ("lnb", [128, 8]),
                        ("qs", [128, 1]), ("ks", [128, 1]),
                        ("qsr", [1, 64]), ("ksr", [1, 64])):
            fwin.setdefault(l, {})[nm] = nc.dram_tensor(
                f"f_{nm}{l}", shp, F32, kind="ExternalInput").ap()
    caps = make_const_inputs(nc, consts)
    out = nc.dram_tensor("out_final", [D, T], F32, kind="ExternalOutput").ap()

    d_xn_tm = nc.dram_tensor("d_xn_tm", [T, D], BF16, kind="Internal").ap()
    d_lists = nc.dram_tensor("d_lists", [16512, 2], F32, kind="Internal").ap()
    d_xout = nc.dram_tensor("d_xout", [16385, D], F32, kind="Internal").ap()
    d_qepad = nc.dram_tensor("d_qepad", [T, 512], BF16, kind="Internal").ap()
    d_kepad = nc.dram_tensor("d_kepad", [T, 512], F32, kind="Internal").ap()
    d_pbd = nc.dram_tensor("d_pbd", [T + 256, 384], BF16, kind="Internal").ap()
    d_x1 = nc.dram_tensor("d_x1", [128, NCHUNK * T], F32, kind="Internal").ap()
    d_x2p = nc.dram_tensor("d_x2p", [128, NCHUNK * T], F32, kind="Internal").ap()

    import contextlib
    with contextlib.ExitStack() as ec:
        e = ec.enter_context
        X = e(nc.sbuf_tensor("X", [128, NCHUNK * T], F32))
        UTIL = e(nc.sbuf_tensor("UTIL", [128, 1024], F32))
        U72 = e(nc.sbuf_tensor("U72", [72, 512], F32))
        ones_f32 = e(nc.sbuf_tensor("ones_f32", [128, 128], F32))
        ones_b_s = e(nc.sbuf_tensor("ones_b_s", [128, 128], BF16))
        eye_f_s = e(nc.sbuf_tensor("eye_f_s", [128, 128], F32))
        eye_b_s = e(nc.sbuf_tensor("eye_b_s", [128, 128], BF16))
        scl_s = e(nc.sbuf_tensor("scl_s", [1, 16], F32))
        PA = e(nc.psum_tensor("PA", [128, 512], F32))
        PBk = e(nc.psum_tensor("PBk", [128, 512], F32))
        PC = e(nc.psum_tensor("PC", [128, 512], F32))
        PD = e(nc.psum_tensor("PD", [128, 512], F32))
        PO = e(nc.psum_tensor("PO", [128, 512], F32))
        PR = e(nc.psum_tensor("PR", [128, 512], F32))
        PBB = e(nc.psum_tensor("PBB", [128, 1024], BF16))

        kb = KB(nc)
        E = kb.emit
        COM = dict(UTIL=UTIL, U72=U72, ones_f32=ones_f32,
                   ones_b_s=ones_b_s, eye_f_s=eye_f_s, eye_b_s=eye_b_s, scl_s=scl_s,
                   PA=PA, PBk=PBk, PC=PC, PD=PD, PO=PO, PR=PR, PBB=PBB)

        for c in range(NCHUNK):
            E("sync", lambda g, c=c: g.dma_start(
                out=X[:, c * T:(c + 1) * T], in_=xT[c * 128:(c + 1) * 128, :]),
              writes=["x"], dma="x")
        for sb, cap in ((ones_f32, "ones_f"), (eye_f_s, "eye_f"), (scl_s, "scl")):
            E("sync", lambda g, sb=sb, cap=cap: g.dma_start(out=sb[:, :], in_=caps[cap]),
              writes=["consts"], dma="c")
        for sb, cap in ((ones_b_s, "ones_b"), (eye_b_s, "eye_b")):
            E("sync", lambda g, sb=sb, cap=cap: g.dma_start(out=sb[:, :], in_=caps[cap]),
              writes=["consts_b"], dma="c")

        with contextlib.ExitStack() as s0:
            e0 = s0.enter_context
            XNB = e0(nc.sbuf_tensor("XNB0", [128, NCHUNK * T], BF16))
            with contextlib.ExitStack() as sA:
                eA = sA.enter_context
                ONORM = eA(nc.sbuf_tensor("ONORM", [128, NCHUNK * T], BF16))
                QK = eA(nc.sbuf_tensor("QK", [128, 1024], F32))
                QKB = eA(nc.sbuf_tensor("QKB", [128, 3 * T], BF16))
                PT0 = eA(nc.sbuf_tensor("PTS0", [128, 4 * 512], BF16))
                VT = eA(nc.sbuf_tensor("VT", [128, 2080], BF16))
                WQKV = eA(nc.sbuf_tensor("WQKV", [128, 6144], BF16))
                protL0_s = eA(nc.sbuf_tensor("protL0_s", [128, 128], F32))
                rope_s = eA(nc.sbuf_tensor("rope_s", [128, 4096], F32))
                cmask_s = eA(nc.sbuf_tensor("cmask_s", [128, 2048], BF16))
                bcol = eA(nc.sbuf_tensor("l0_b_cols", [128, 24], F32))
                bpcol = eA(nc.sbuf_tensor("l0_bp_col", [128, 8], F32))
                STAT0 = eA(nc.sbuf_tensor("STAT0", [1, 2048], F32))
                C = dict(COM, STAT=STAT0, QK=QK, QKB=QKB, PT=PT0, VT=VT, ONORM=ONORM,
                         WQKV=WQKV, protL0_s=protL0_s, rope_s=rope_s,
                         cmask_s=cmask_s, l0_b_cols=bcol, l0_bp_col=bpcol,
                         in_l0_wqkvT=wqkvT, in_l0_wprojT=wprojT)
                E("sync", lambda g: g.dma_start(out=protL0_s[:], in_=caps["protL0"]),
                  writes=["consts"], dma="c")
                E("sync", lambda g: g.dma_start(out=rope_s[:], in_=caps["rope12"]),
                  writes=["consts"], dma="c")
                E("sync", lambda g: g.dma_start(out=cmask_s[:], in_=caps["cmask"]),
                  writes=["consts_b"], dma="c")
                E("sync", lambda g: g.dma_start(out=bcol[:, :], in_=bqkv),
                  writes=["l0bc"], dma="c")
                E("sync", lambda g: g.dma_start(out=bpcol[:, :], in_=bproj),
                  writes=["l0bc"], dma="c")
                emit_ln(kb, C, "x", "xnb", X, XNB)
                emit_l0_attn(kb, C, XNB, X)
            kb.barrier()
            with contextlib.ExitStack() as sM:
                eM = sM.enter_context
                WE1 = eM(nc.sbuf_tensor("WE1", [128, 8192], BF16))
                WE3 = eM(nc.sbuf_tensor("WE3", [128, 8192], BF16))
                WE2 = eM(nc.sbuf_tensor("WE2", [128, 8192], BF16))
                WG = eM(nc.sbuf_tensor("WG", [128, 256], BF16))
                XG = eM(nc.sbuf_tensor("XG", [128, 4096], BF16))
                UU = eM(nc.sbuf_tensor("UUm", [128, 4096], BF16))
                XTM = eM(nc.sbuf_tensor("XTM", [128, 1024], BF16))
                OFD = eM(nc.sbuf_tensor("OFD", [128, 512], F32))
                OT4 = eM(nc.sbuf_tensor("OT4", [128, 4096], F32))
                GR = None
                ACC = eM(nc.sbuf_tensor("ACC", [128, 1024], F32))
                SCO = eM(nc.sbuf_tensor("SCO", [128, 512], F32))
                G4 = eM(nc.sbuf_tensor("G4", [128, 64], F32))
                IDXF = eM(nc.sbuf_tensor("IDXF", [128, 64], F32))
                IDXU = eM(nc.sbuf_tensor("IDXU", [128, 8], U32))
                RS = eM(nc.sbuf_tensor("RS", [128, 32], F32))
                AT = eM(nc.sbuf_tensor("AT", [128, 32], F32))
                GATE = eM(nc.sbuf_tensor("GATE", [128, 64], F32))
                GIDXI = eM(nc.sbuf_tensor("GIDXI", [128, 64], F32))
                DI1 = eM(nc.sbuf_tensor("DI1", [128, 1], I32))
                LG = eM(nc.sbuf_tensor("LG", [128, 8], F32))
                LI = eM(nc.sbuf_tensor("LI", [128, 4], I32))
                GC = eM(nc.sbuf_tensor("GC", [128, 4], F32))
                iota32_s = eM(nc.sbuf_tensor("iota32_s", [128, 32], F32))
                tokcol_s = eM(nc.sbuf_tensor("tokcol_s", [128, 1], F32))
                lstrict_s = eM(nc.sbuf_tensor("lstrict_s", [128, 128], F32))
                STATM = eM(nc.sbuf_tensor("STATM", [1, 64], F32))
                C = dict(COM, STAT=STATM, SCO=SCO, G4=G4, IDXF=IDXF, IDXU=IDXU, RS=RS, AT=AT,
                         GATE=GATE, GIDXI=GIDXI, XTM=XTM, XG=XG, UU=UU, OFD=OFD,
                         OT4=OT4, GR=OT4[:, 0:1024], ACC=ACC, WE1=WE1, WE3=WE3, WE2=WE2, WG=WG,
                         LG=LG, LI=LI, GC=GC, DI1=DI1,
                         iota32_s=iota32_s, tokcol_s=tokcol_s, lstrict_s=lstrict_s,
                         d_xn_tm=d_xn_tm, d_lists=d_lists, d_xout=d_xout,
                         in_wg=wg, in_w1=w1m, in_w2=w2m, in_w3=w3m,
                         in_ws1=ws1, in_ws2=ws2, in_ws3=ws3)
                for sb, cap in ((iota32_s, "iota32"), (tokcol_s, "tokcol"),
                                (lstrict_s, "lstrict")):
                    E("sync", lambda g, sb=sb, cap=cap: g.dma_start(out=sb[:, :], in_=caps[cap]),
                      writes=["consts"], dma="c")
                emit_moe(kb, C, X, XNB)
            kb.barrier()
        for c in range(NCHUNK):
            E("sync", lambda g, c=c: g.dma_start(
                out=d_x1[:, c * T:(c + 1) * T], in_=X[:, c * T:(c + 1) * T]),
              reads=["x"], writes=["d_x1"], dma="sp")

        with contextlib.ExitStack() as s1:
            e1 = s1.enter_context
            XNB = e1(nc.sbuf_tensor("XNB1", [128, NCHUNK * T], BF16))
            PT = e1(nc.sbuf_tensor("PTS1", [128, 16 * 512], BF16))
            Q1 = e1(nc.sbuf_tensor("Q1", [64, T], BF16))
            K1 = e1(nc.sbuf_tensor("K1", [64, T], BF16))
            V1 = e1(nc.sbuf_tensor("V1", [64, T], BF16))
            VT1 = e1(nc.sbuf_tensor("VT1a", [128, 1040], BF16))
            VTLO = e1(nc.sbuf_tensor("VTLO", [128, 1040], BF16))
            VTHI = e1(nc.sbuf_tensor("VTHI", [128, 1040], BF16))
            WH = e1(nc.sbuf_tensor("WH", [128, 3072], BF16))
            AUG = e1(nc.sbuf_tensor("AUG", [1, 8192], BF16))
            ONER = e1(nc.sbuf_tensor("ONER", [1, T], BF16))
            PADB = e1(nc.sbuf_tensor("PADB", [128, 512], BF16))
            QEF = e1(nc.sbuf_tensor("QEF", [128, 512], F32))
            EVBC = e1(nc.sbuf_tensor("EVBC", [128, 128], BF16))
            TMPB = e1(nc.sbuf_tensor("TMPB1", [64, 512], BF16))
            SKB = e1(nc.sbuf_tensor("SKB", [128, 1280], BF16))
            SKF = e1(nc.sbuf_tensor("SKF", [128, 256], F32))
            PBTM = e1(nc.sbuf_tensor("PBTM", [128, 1536], BF16))
            PPROW = e1(nc.sbuf_tensor("PPROW", [1, 129], F32))
            STAT1 = e1(nc.sbuf_tensor("STAT1", [1, 2048], F32))
            eqT_f = e1(nc.sbuf_tensor("eqT_f", [64, 129], F32))
            ekT_f = e1(nc.sbuf_tensor("ekT_f", [64, 129], F32))
            eqT_b = e1(nc.sbuf_tensor("eqT_b", [64, 129], BF16))
            ekT_b = e1(nc.sbuf_tensor("ekT_b", [64, 129], BF16))
            ekTrev_b = e1(nc.sbuf_tensor("ekTrev_b", [64, 129], BF16))
            ev0_b = e1(nc.sbuf_tensor("ev0_b", [1, 64], BF16))
            ev128_b = e1(nc.sbuf_tensor("ev128_b", [1, 64], BF16))
            evb2a_b = e1(nc.sbuf_tensor("evb2a_b", [128, 64], BF16))
            evb2b_b = e1(nc.sbuf_tensor("evb2b_b", [127, 64], BF16))
            C = dict(COM, STAT=STAT1, Q1=Q1, K1=K1, V1=V1, VT1=VT1, VTLO=VTLO, VTHI=VTHI,
                     PT=PT, WH=WH, AUG=AUG, ONER=ONER, PADB=PADB, QEF=QEF,
                     EVBC=EVBC, TMPB=TMPB, SKB=SKB, SKF=SKF, PBTM=PBTM, PPROW=PPROW,
                     eqT_f=eqT_f, ekT_f=ekT_f, eqT_b=eqT_b, ekT_b=ekT_b,
                     ekTrev_b=ekTrev_b, ev0_b=ev0_b, ev128_b=ev128_b,
                     evb2a_b=evb2a_b, evb2b_b=evb2b_b,
                     d_qepad=d_qepad, d_kepad=d_kepad, d_pbd=d_pbd,
                     in_wq1=wq1, in_wk1=wk1, in_wv1=wv1)
            E("sync", lambda g: g.dma_start(out=eqT_f[:], in_=eqT), writes=["consts"], dma="c")
            E("sync", lambda g: g.dma_start(out=ekT_f[:], in_=ekT), writes=["consts"], dma="c")
            for sb, srcd in ((eqT_b, eqT), (ekT_b, ekT), (ekTrev_b, ekTrev),
                             (ev0_b, ev0), (ev128_b, ev128), (evb2a_b, evb2a),
                             (evb2b_b, evb2b)):
                E("gpsimd", lambda g, sb=sb, srcd=srcd: g.dma_start(out=sb[:], in_=srcd),
                  writes=["consts_b"], dma="cb")
            emit_ln(kb, C, "x", "xnb", X, XNB)
            emit_l1a(kb, C, XNB, X)
        kb.barrier()
        for c in range(NCHUNK):
            E("sync", lambda g, c=c: g.dma_start(
                out=d_x2p[:, c * T:(c + 1) * T], in_=X[:, c * T:(c + 1) * T]),
              reads=["x"], writes=["d_x2p"], dma="sp")
        for c in range(NCHUNK):
            E("sync", lambda g, c=c: g.dma_start(
                out=X[:, c * T:(c + 1) * T], in_=d_x1[:, c * T:(c + 1) * T]),
              reads=["d_x1", "x"], writes=["xi"], dma="x")
        emit_ln_fp32_inplace(kb, dict(COM), X)

        with contextlib.ExitStack() as s2:
            e2 = s2.enter_context
            HBS = e2(nc.sbuf_tensor("HBS", [128, 8 * 512], BF16))
            H2S = e2(nc.sbuf_tensor("H2S", [128, 2 * 512], F32))
            QBc = e2(nc.sbuf_tensor("QBc", [128, T], BF16))
            KBc = e2(nc.sbuf_tensor("KBc", [128, T], BF16))
            VBc = e2(nc.sbuf_tensor("VBc", [128, T], BF16))
            VT1f = e2(nc.sbuf_tensor("VT1f", [128, 1040], BF16))
            PTf = e2(nc.sbuf_tensor("PTSf", [128, 2 * 512], BF16))
            AOC = e2(nc.sbuf_tensor("AOC", [128, T], BF16))
            UU = e2(nc.sbuf_tensor("UUf", [128, 22 * 512], BF16))
            WQ3 = e2(nc.sbuf_tensor("WQ3", [128, 1 * 3072], BF16))
            WOS = e2(nc.sbuf_tensor("WOS", [128, 1024], BF16))
            W1S = e2(nc.sbuf_tensor("W1S", [128, 4096], BF16))
            W2F = e2(nc.sbuf_tensor("W2F", [128, 22 * 1024], BF16))
            GRIDS = e2(nc.sbuf_tensor("GRIDS", [128, 1024], F32))
            TMPBf = e2(nc.sbuf_tensor("TMPBf", [64, 512], BF16))
            FMA = e2(nc.sbuf_tensor("FMA", [128, 3 * 512], BF16))
            FMM = e2(nc.sbuf_tensor("FMM", [128, 5 * 512], BF16))
            QROW = e2(nc.sbuf_tensor("QROW", [1, 128], F32))
            NEGC = e2(nc.sbuf_tensor("NEGC", [128, 1], F32))
            B1A = e2(nc.sbuf_tensor("B1A", [128, 22], F32))
            B1G = e2(nc.sbuf_tensor("B1G", [128, 22], F32))
            B2C = e2(nc.sbuf_tensor("B2C", [128, 8], F32))
            LNG = e2(nc.sbuf_tensor("LNG", [128, 8], F32))
            LNB = e2(nc.sbuf_tensor("LNB", [128, 8], F32))
            QSC = e2(nc.sbuf_tensor("QSC", [128, 1], F32))
            KSC = e2(nc.sbuf_tensor("KSC", [128, 1], F32))
            STG = e2(nc.sbuf_tensor("STG", [128, 512], F32))
            blockmap_s = e2(nc.sbuf_tensor("blockmap_s", [128, 128], F32))
            prothalf_s = e2(nc.sbuf_tensor("prothalf_s", [128, 128], F32))
            C = dict(COM, HBS=HBS, H2S=H2S, QBc=QBc, KBc=KBc, VBc=VBc, VT1=VT1f,
                     PT=PTf, AOC=AOC, UU=UU, WQ3=WQ3, WOS=WOS, W1S=W1S, W2F=W2F,
                     GRIDS=GRIDS, TMPB=TMPBf, FMA=FMA, FMM=FMM, QROW=QROW,
                     NEGC=NEGC, B1A=B1A, B1G=B1G, B2C=B2C, LNG=LNG, LNB=LNB,
                     QSC=QSC, KSC=KSC, blockmap_s=blockmap_s, prothalf_s=prothalf_s,
                     c_xposq=caps["xposq"], c_xposk=caps["xposk"],
                     fw0=fwin[0], fw1=fwin[1])
            E("sync", lambda g: g.dma_start(out=blockmap_s[:], in_=caps["blockmap"]),
              writes=["consts"], dma="c")
            E("sync", lambda g: g.dma_start(out=prothalf_s[:], in_=caps["prothalf"]),
              writes=["consts"], dma="c")
            E("sync", lambda g: g.dma_start(out=FMM[:], in_=caps["fmaskmul"][:, 0:5 * 512]),
              writes=["consts_b"], dma="c")
            E("gpsimd", lambda g: g.dma_start(out=FMA[:], in_=caps["fmaskadd"][:, 5 * 512:8 * 512]),
              writes=["consts_b"], dma="cb")
            emit_l1f(kb, C, X)
            for c in range(NCHUNK):
                for tg in range(TG):
                    E("sync", lambda g, c=c, tg=tg: g.dma_start(
                        out=STG[:, :], in_=d_x2p[:, c * T + tg * 512: c * T + tg * 512 + 512]),
                      reads=["d_x2p"], writes=["stg"], dma="x2")
                    E("vector", lambda v, c=c, tg=tg: v.tensor_add(
                        STG[:, :], STG[:, :], csl(X, c, tg * 512, tg * 512 + 512)),
                      reads=["stg", "xi"], writes=["stg"])
                    E("sync", lambda g, c=c, tg=tg: g.dma_start(
                        out=out[c * 128:(c + 1) * 128, tg * 512:(tg + 1) * 512], in_=STG[:, :]),
                      reads=["stg"], dma="st", final=True)
        kb.build()
    return nc


def kernel_fused(**inputs):
    NC = 8
    if "ncF" not in _BUILD_CACHE:
        _BUILD_CACHE.setdefault("consts", host_constants())
        _BUILD_CACHE["ncF"] = build_fused(_BUILD_CACHE["consts"])
    consts = _BUILD_CACHE["consts"]
    w = prep_weights(inputs)
    x = np.asarray(inputs["L0_attn_x"])
    cmap = {"c_" + k: v for k, v in consts.items()}
    maps = []
    for core in range(NC):
        m = dict(xT=np.ascontiguousarray(x[core].T),
                 l0_wqkvT=w["l0_wqkvT"], l0_bqkv=w["l0_bqkv"],
                 l0_wprojT=w["l0_wprojT"], l0_bproj=w["l0_bproj"],
                 moe_wg=np.asarray(inputs["L0_ffn_Wg"]),
                 moe_w1=np.asarray(inputs["L0_ffn_W1"]),
                 moe_w2=np.asarray(inputs["L0_ffn_W2"]),
                 moe_w3=np.asarray(inputs["L0_ffn_W3"]),
                 moe_ws1=np.asarray(inputs["L0_ffn_Ws1"]),
                 moe_ws2=np.asarray(inputs["L0_ffn_Ws2"]),
                 moe_ws3=np.asarray(inputs["L0_ffn_Ws3"]),
                 l1_wq=np.asarray(inputs["L1_attn_Wq"]),
                 l1_wk=np.asarray(inputs["L1_attn_Wk"]),
                 l1_wv=np.asarray(inputs["L1_attn_Wv"]),
                 l1_eqT=w["l1_eqT"], l1_ekT=w["l1_ekT"], l1_ekTrev=w["l1_ekTrev"],
                 l1_ev0=w["l1_ev0"], l1_ev128=w["l1_ev128"],
                 l1_evb2a=w["l1_evb2a"], l1_evb2b=w["l1_evb2b"])
        for l in range(2):
            for nm in ("wq", "wk", "wv", "wo", "w1a", "w1g", "w2", "b1a", "b1g",
                       "b2", "lng", "lnb", "qs", "ks", "qsr", "ksr"):
                m[f"f_{nm}{l}"] = w[f"f_{nm}{l}"]
        m.update(cmap)
        maps.append(m)
    r = run_bass_kernel_spmd(_BUILD_CACHE["ncF"], maps, core_ids=list(range(NC))).results
    outa = np.stack([np.ascontiguousarray(np.asarray(r[core]["out_final"]).T)
                     for core in range(NC)])
    return outa.astype(np.float32)


_BUILD_CACHE = {}


def _get_builds():
    if "nc" not in _BUILD_CACHE:
        consts = host_constants()
        _BUILD_CACHE["consts"] = consts
        _BUILD_CACHE["nc1"] = build_launch1(consts)
        _BUILD_CACHE["nc2"] = build_launch2(consts)
        _BUILD_CACHE["nc3"] = build_launch3(consts)
        _BUILD_CACHE["nc4"] = build_launch4(consts)
        _BUILD_CACHE["nc"] = True
    return _BUILD_CACHE


def kernel(**inputs):
    import os
    if not os.environ.get("KERNEL_UNFUSED"):
        return kernel_fused(**inputs)
    NC = 8
    b = _get_builds()
    consts = b["consts"]
    w = prep_weights(inputs)
    x = np.asarray(inputs["L0_attn_x"])
    cmap = {"c_" + k: v for k, v in consts.items()}

    maps = []
    for core in range(NC):
        m = dict(xT=np.ascontiguousarray(x[core].T),
                 l0_wqkvT=w["l0_wqkvT"], l0_bqkv=w["l0_bqkv"],
                 l0_wprojT=w["l0_wprojT"], l0_bproj=w["l0_bproj"])
        m.update(cmap)
        maps.append(m)
    r1 = run_bass_kernel_spmd(b["nc1"], maps, core_ids=list(range(NC))).results

    maps = []
    for core in range(NC):
        m = dict(x1p=np.asarray(r1[core]["x1p"]),
                 xnb_in=np.asarray(r1[core]["xnb_out"]),
                 moe_wg=np.asarray(inputs["L0_ffn_Wg"]),
                 moe_w1=np.asarray(inputs["L0_ffn_W1"]),
                 moe_w2=np.asarray(inputs["L0_ffn_W2"]),
                 moe_w3=np.asarray(inputs["L0_ffn_W3"]),
                 moe_ws1=np.asarray(inputs["L0_ffn_Ws1"]),
                 moe_ws2=np.asarray(inputs["L0_ffn_Ws2"]),
                 moe_ws3=np.asarray(inputs["L0_ffn_Ws3"]))
        m.update(cmap)
        maps.append(m)
    r2 = run_bass_kernel_spmd(b["nc2"], maps, core_ids=list(range(NC))).results

    maps = []
    for core in range(NC):
        m = dict(x1=np.asarray(r2[core]["x1_out"]),
                 l1_wq=np.asarray(inputs["L1_attn_Wq"]),
                 l1_wk=np.asarray(inputs["L1_attn_Wk"]),
                 l1_wv=np.asarray(inputs["L1_attn_Wv"]),
                 l1_eqT=w["l1_eqT"], l1_ekT=w["l1_ekT"], l1_ekTrev=w["l1_ekTrev"],
                 l1_ev0=w["l1_ev0"], l1_ev128=w["l1_ev128"],
                 l1_evb2a=w["l1_evb2a"], l1_evb2b=w["l1_evb2b"])
        m.update(cmap)
        maps.append(m)
    r3 = run_bass_kernel_spmd(b["nc3"], maps, core_ids=list(range(NC))).results

    maps = []
    for core in range(NC):
        m = dict(x1=np.asarray(r2[core]["x1_out"]),
                 x2p=np.asarray(r3[core]["x2p"]))
        for l in range(2):
            for nm in ("wq", "wk", "wv", "wo", "w1a", "w1g", "w2", "b1a", "b1g",
                       "b2", "lng", "lnb", "qs", "ks", "qsr", "ksr"):
                m[f"f_{nm}{l}"] = w[f"f_{nm}{l}"]
        m.update(cmap)
        maps.append(m)
    r4 = run_bass_kernel_spmd(b["nc4"], maps, core_ids=list(range(NC))).results

    out = np.stack([np.ascontiguousarray(np.asarray(r4[core]["out_final"]).T)
                    for core in range(NC)])
    return out.astype(np.float32)
